# version 09
# baseline (speedup 1.0000x reference)
import numpy as np
import ml_dtypes
from contextlib import ExitStack

import concourse.bass as bass
import concourse.mybir as mybir
from concourse.alu_op_type import AluOpType as OP
from concourse.bass_utils import run_bass_kernel_spmd

F32 = mybir.dt.float32
BF16 = mybir.dt.bfloat16
U8 = mybir.dt.uint8
I32 = mybir.dt.int32
AF = mybir.ActivationFunctionType
AX = mybir.AxisListType

B, S, H = 2, 4096, 2048
NCORES = 8
TPC = 1024
TT = 8
KF = 16
NH, HD = 16, 128
NIH = 8
NEXP, IFFN = 8, 1408
SI = 2816
SCALE = float(np.float32(1.0) / np.sqrt(np.float32(HD)))
ESHIFT = -32.0
BIG = 1.0e9
NROUND = 8
EPS = 1e-5
NSLOT = 23
IBW = [512, 512, 384]
IBOFF = [0, 512, 1024]
NKT = 32

DEBUG = False


class Ctx:
    def __init__(self, nc, stack):
        self.nc = nc
        self.stack = stack
        self.sem = {}
        self.cnt = {}

    def mksem(self, name):
        self.sem[name] = self.stack.enter_context(self.nc.semaphore(name))
        self.cnt[name] = 0

    def inc(self, inst, name, n=1):
        inst.then_inc(self.sem[name], n)
        self.cnt[name] += n
        return self.cnt[name]

    def val(self, name):
        return self.cnt[name]

    def wait(self, e, name, v=None):
        if v is None:
            v = self.cnt[name]
        if v > 0:
            e.wait_ge(self.sem[name], v)


def build_program(debug=DEBUG):
    nc = bass.Bass("TRN2", target_bir_lowering=False)

    d_x = nc.dram_tensor("x_in", [TPC, H], F32, kind="ExternalInput")
    d_meta = nc.dram_tensor("meta", [1, 8], F32, kind="ExternalInput")
    d_idbf = nc.dram_tensor("id_bf", [128, 128], BF16, kind="ExternalInput")
    d_idf = nc.dram_tensor("id_f32", [128, 128], F32, kind="ExternalInput")
    d_onesbf = nc.dram_tensor("ones_bf", [128, 128], BF16, kind="ExternalInput")
    d_onesf = nc.dram_tensor("ones_f32", [1, 128], F32, kind="ExternalInput")
    d_halves = nc.dram_tensor("halves_bf", [1, 128], BF16, kind="ExternalInput")
    d_Wq = nc.dram_tensor("Wq", [H, H], F32, kind="ExternalInput")
    d_Wk = nc.dram_tensor("Wk", [H, H], F32, kind="ExternalInput")
    d_Wv = nc.dram_tensor("Wv", [H, H], F32, kind="ExternalInput")
    d_Wo = nc.dram_tensor("Wo", [H, H], F32, kind="ExternalInput")
    d_Wqi = nc.dram_tensor("Wq_ind", [H, H], F32, kind="ExternalInput")
    d_Wki = nc.dram_tensor("Wk_ind", [H, H], F32, kind="ExternalInput")
    d_hw = nc.dram_tensor("head_w", [1, 8], F32, kind="ExternalInput")
    d_temp = nc.dram_tensor("temp", [1, 1], F32, kind="ExternalInput")
    d_rw = nc.dram_tensor("router_w", [8, H], F32, kind="ExternalInput")
    d_gw = nc.dram_tensor("gate_w", [NEXP, H, IFFN], F32, kind="ExternalInput")
    d_uw = nc.dram_tensor("up_w", [NEXP, H, IFFN], F32, kind="ExternalInput")
    d_dw = nc.dram_tensor("down_w", [NEXP, IFFN, H], F32, kind="ExternalInput")
    d_sgw = nc.dram_tensor("s_gate_w", [H, SI], F32, kind="ExternalInput")
    d_suw = nc.dram_tensor("s_up_w", [H, SI], F32, kind="ExternalInput")
    d_sdw = nc.dram_tensor("s_down_w", [SI, H], F32, kind="ExternalInput")
    d_sgb = nc.dram_tensor("s_gate_b", [1, SI], F32, kind="ExternalInput")
    d_sub = nc.dram_tensor("s_up_b", [1, SI], F32, kind="ExternalInput")
    d_sdb = nc.dram_tensor("s_down_b", [1, H], F32, kind="ExternalInput")
    d_out = nc.dram_tensor("x_out", [TPC, H], F32, kind="ExternalOutput")

    d_gwt = nc.dram_tensor("gwt", [NEXP, 12, 128, 2048], BF16)
    d_uwt = nc.dram_tensor("uwt", [NEXP, 12, 128, 2048], BF16)
    d_dwt = nc.dram_tensor("dwt", [NEXP, 4, 128, 11 * 512], BF16)
    d_sgwt = nc.dram_tensor("sgwt", [2, 12, 128, 2048], BF16)
    d_suwt = nc.dram_tensor("suwt", [2, 12, 128, 2048], BF16)
    d_sdwt = nc.dram_tensor("sdwt", [2, 4, 128, 11 * 512], BF16)
    d_Wqt = nc.dram_tensor("Wqt", [16, 128, 2048], BF16)
    d_Wkt = nc.dram_tensor("Wkt", [16, 128, 2048], BF16)
    d_Wvt = nc.dram_tensor("Wvt", [16, 128, 2048], BF16)
    d_Wot = nc.dram_tensor("Wot", [16, 128, 2048], BF16)
    d_Wqit = nc.dram_tensor("Wqit", [16, 128, 2048], BF16)
    d_Wkit = nc.dram_tensor("Wkit", [16, 128, 2048], BF16)

    CC1R = 6144
    d_cc1i = nc.dram_tensor("cc1i", [CC1R, TPC], BF16)
    d_cc1o = nc.dram_tensor("cc1o", [4 * CC1R, TPC], BF16)
    d_cc2i = nc.dram_tensor("cc2i", [1, TPC], F32)
    d_cc2o = nc.dram_tensor("cc2o", [4, TPC], F32)
    d_qsp = nc.dram_tensor("qspill", [H, TPC], BF16)
    d_utri = nc.dram_tensor("utri", [128, 128], BF16, kind="ExternalInput")
    d_tok = nc.dram_tensor("tokidx", [NSLOT * 128, 1], I32)
    d_cev = nc.dram_tensor("cevals", [NSLOT * 128, 1], F32)
    d_xng = nc.dram_tensor("xng", [TPC, H], BF16)
    d_yasg = nc.dram_tensor("yasg", [NSLOT * 128, H], F32)

    dbg = {}
    if debug:
        dbg["x1"] = nc.dram_tensor("dbg_x1", [TPC, H], F32, kind="ExternalOutput")
        dbg["rel"] = nc.dram_tensor("dbg_rel", [4, TPC], F32, kind="ExternalOutput")
        dbg["sel"] = nc.dram_tensor("dbg_sel", [128, NKT], F32, kind="ExternalOutput")
        dbg["x2"] = nc.dram_tensor("dbg_x2", [TPC, H], F32, kind="ExternalOutput")

    main = ExitStack()
    with main:
        c = Ctx(nc, main)
        sb = lambda st, *a: st.enter_context(nc.sbuf_tensor(*a))
        psn = lambda *a: main.enter_context(nc.psum_tensor(*a))
        semnames = (["tS", "vS", "aS", "gS", "ioS", "ccS", "ldS", "bbS", "cbS",
                     "qsS", "rlS", "r4S", "outS", "wpA"]
                    + [f"wpJ{j}" for j in range(9)]
                    + [f"wgS{i}" for i in range(5)] + [f"wuS{i}" for i in range(5)]
                    + [f"wdS{i}" for i in range(2)] + [f"stS{i}" for i in range(2)]
                    + [f"vqS{i}" for i in range(2)]
                    + ["bzS", "skS", "tkS"]
                    + [f"xsS{i}" for i in range(2)] + [f"gaS{i}" for i in range(2)]
                    + [f"igS{i}" for i in range(5)] + [f"iuS{i}" for i in range(4)]
                    + [f"idS{i}" for i in range(2)] + [f"ysS{i}" for i in range(2)]
                    + [f"ygS{i}" for i in range(2)])
        for sname in semnames:
            c.mksem(sname)

        s_x = sb(main, "s_x", [128, TT, H], F32)
        s_idbf = sb(main, "s_idbf", [128, 128], BF16)
        s_idf = sb(main, "s_idf", [128, 128], F32)
        s_onesbf = sb(main, "s_onesbf", [128, 128], BF16)
        s_onesf = sb(main, "s_onesf", [1, 128], F32)
        s_halves = sb(main, "s_halves", [1, 128], BF16)
        s_meta = sb(main, "s_meta", [1, 8], F32)
        s_hw = sb(main, "s_hw", [1, 8], F32)
        s_temp = sb(main, "s_temp", [1, 1], F32)
        s_stats = sb(main, "s_stats", [128, 6, TT], F32)
        s_xn_tm = sb(main, "s_xn_tm", [128, 1, H], F32)
        s_rwTf = sb(main, "s_rwTf", [128, KF, 8], F32)
        s_lgtm = sb(main, "s_lgtm", [128, TT, 8], F32)
        s_r2 = sb(main, "s_r2", [128, 8, TT], F32)
        s_ce = sb(main, "s_ce", [128, TT, 8], F32)
        s_wg = sb(main, "s_wg", [128, 5, 4, 512], BF16)
        s_sg = sb(main, "s_sg", [128, 2, 512], BF16)
        s_relpc = sb(main, "s_relpc", [128, NKT], F32)
        s_sel = sb(main, "s_sel", [128, NKT], F32)
        s_bis = sb(main, "s_bis", [1, 16], F32)
        s_smallc = sb(main, "s_smallc", [128, 8], F32)

        pb = [psn(f"pb{i}", [128, 512], F32) for i in range(7)]
        ptb = psn("ptb", [128, 1024], BF16)

        block = main.enter_context(nc.Block())
        sec = lambda name: getattr(block, name)

        def snap():
            return {k: c.val(k) for k in c.cnt}

        def soft_barrier():
            bar = snap()
            for en in ("tensor", "vector", "scalar", "gpsimd", "sync"):
                @sec(en)
                def _(e, bar=bar):
                    for k, v in bar.items():
                        c.wait(e, k, v)

        RGfree = {i: 0 for i in range(5)}
        RGload = {}

        @sec("sync")
        def _(e):
            c.inc(e.dma_start(s_x[:], d_x.rearrange("(tt p) f -> p tt f", p=128)), "ldS", 16)
            c.inc(e.dma_start(s_idbf[:], d_idbf[:]), "ldS", 16)
            c.inc(e.dma_start(s_idf[:], d_idf[:]), "ldS", 16)
            c.inc(e.dma_start(s_onesbf[:], d_onesbf[:]), "ldS", 16)
            c.inc(e.dma_start(s_onesf[:], d_onesf[:]), "ldS", 16)
            c.inc(e.dma_start(s_meta[:], d_meta[:]), "ldS", 16)
            c.inc(e.dma_start(s_hw[:], d_hw[:]), "ldS", 16)
            c.inc(e.dma_start(s_temp[:], d_temp[:]), "ldS", 16)
            c.inc(e.dma_start(s_xn_tm[0:8, 0, :], d_rw[:]), "ldS", 16)
            c.inc(e.dma_start(s_halves[:], d_halves[:]), "ldS", 16)
        lds_done = c.val("ldS")

        wp_done = {}

        def prep_gu(e, dstt, srcw, semn):
            n = 0
            for ibp in range(3):
                w = 512 if ibp < 2 else 384
                for q in range(4):
                    c.inc(e.dma_start(
                        dstt[ibp * 4 + q, :, 0:4 * w].rearrange("p (kf i) -> p kf i", kf=4),
                        srcw[512 * q:512 * (q + 1), 512 * ibp:512 * ibp + w]
                        .rearrange("(kf p) i -> p kf i", p=128)), semn, 16)
                    n += 1
            return n

        def prep_dw(e, dstt, srcw, semn):
            for fb in range(4):
                c.inc(e.dma_start(
                    dstt[fb].rearrange("p (ic i) -> p ic i", ic=11),
                    srcw[:, 512 * fb:512 * (fb + 1)]
                    .rearrange("(ic p) i -> p ic i", p=128)), semn, 16)
            return 4

        def prep_att(e, dstt, srcw, semn):
            dv = dstt.rearrange("b p (kf i) -> b p kf i", kf=4)
            for coln in range(4):
                for q in range(4):
                    c.inc(e.dma_start(
                        dv[coln * 4 + q],
                        srcw[512 * q:512 * (q + 1), 512 * coln:512 * (coln + 1)]
                        .rearrange("(kf p) i -> p kf i", p=128)), semn, 16)
            return 16

        @sec("gpsimd")
        def _(e):
            for j in range(NEXP):
                prep_gu(e, d_gwt[j], d_gw[j], f"wpJ{j}")
                prep_gu(e, d_uwt[j], d_uw[j], f"wpJ{j}")
                prep_dw(e, d_dwt[j], d_dw[j], f"wpJ{j}")
            for hh in range(2):
                prep_gu(e, d_sgwt[hh], d_sgw[:, hh * IFFN:(hh + 1) * IFFN], "wpJ8")
                prep_gu(e, d_suwt[hh], d_suw[:, hh * IFFN:(hh + 1) * IFFN], "wpJ8")
                prep_dw(e, d_sdwt[hh], d_sdw[hh * IFFN:(hh + 1) * IFFN, :], "wpJ8")
            for dst, srcw in ((d_Wvt, d_Wv), (d_Wkt, d_Wk), (d_Wkit, d_Wki),
                              (d_Wqt, d_Wq), (d_Wqit, d_Wqi), (d_Wot, d_Wo)):
                prep_att(e, dst, srcw, "wpA")
        for j in range(NEXP):
            wp_done[("exp", j)] = (f"wpJ{j}", 16 * 28)
        wp_done[("sh", 0)] = wp_done[("sh", 1)] = ("wpJ8", 16 * 56)
        wpA_done = c.val("wpA")

        for kf in range(KF):
            bank = pb[4 + (kf % 2)]
            prev_v = c.val("vS")

            @sec("tensor")
            def _(e, kf=kf, bank=bank, prev_v=prev_v):
                if kf == 0:
                    c.wait(e, "ldS", lds_done)
                c.wait(e, "vS", prev_v)
                c.inc(e.transpose(bank[:, 0:8], s_xn_tm[0:8, 0, kf * 128:(kf + 1) * 128],
                                  s_idf[0:8, 0:8]), "tS")
            tdone = c.val("tS")

            @sec("vector")
            def _(e, kf=kf, bank=bank, tdone=tdone):
                c.wait(e, "tS", tdone)
                c.inc(e.tensor_copy(s_rwTf[:, kf, :], bank[:, 0:8]), "vS")
        rwT_done = c.val("vS")

        def layernorm_to_xnT(xnT, sq_junk, xwait=None, with_logits=False, s_stage=None,
                             spill=None):
            @sec("scalar")
            def _(e):
                c.wait(e, "ldS", lds_done)
                if xwait:
                    c.wait(e, xwait[0], xwait[1])
                c.wait(e, "vS", c.val("vS"))
                for tt in range(TT):
                    c.wait(e, "aS", c.val("aS"))
                    c.inc(e.activation(sq_junk, s_x[:, tt, :], AF.Square,
                                       accum_out=s_stats[:, 1, tt:tt + 1]), "aS")
            sq_done = c.val("aS")

            @sec("vector")
            def _(e):
                c.wait(e, "ldS", lds_done)
                if xwait:
                    c.wait(e, xwait[0], xwait[1])
                for tt in range(TT):
                    c.inc(e.reduce_sum(s_stats[:, 0, tt:tt + 1], s_x[:, tt, :], axis=AX.X), "vS")
                c.wait(e, "vS")
                c.wait(e, "aS", sq_done)
                c.inc(e.tensor_scalar(s_stats[:, 2, :], s_stats[:, 0, :], 1.0 / H, None, op0=OP.mult), "vS")
                c.wait(e, "vS")
                c.inc(e.tensor_tensor(s_stats[:, 3, :], s_stats[:, 2, :], s_stats[:, 2, :], op=OP.mult), "vS")
                c.wait(e, "vS")
                c.inc(e.scalar_tensor_tensor(s_stats[:, 3, :], s_stats[:, 1, :], 1.0 / H,
                                             s_stats[:, 3, :], op0=OP.mult, op1=OP.subtract), "vS")
                c.wait(e, "vS")
                c.inc(e.tensor_scalar(s_stats[:, 3, :], s_stats[:, 3, :], EPS, None, op0=OP.add), "vS")
            varpe_done = c.val("vS")

            @sec("scalar")
            def _(e):
                c.wait(e, "vS", varpe_done)
                c.inc(e.activation(s_stats[:, 4, :], s_stats[:, 3, :], AF.Ln), "aS")
                c.wait(e, "aS")
                c.inc(e.activation(s_stats[:, 5, :], s_stats[:, 4, :], AF.Exp, scale=-0.5), "aS")
            rstd_done = c.val("aS")

            for tt in range(TT):
                prev_t = c.val("tS")

                @sec("vector")
                def _(e, tt=tt, prev_t=prev_t):
                    c.wait(e, "aS", rstd_done)
                    c.wait(e, "tS", prev_t)
                    c.inc(e.tensor_scalar(s_xn_tm[:, 0, :], s_x[:, tt, :],
                                          s_stats[:, 2, tt:tt + 1], s_stats[:, 5, tt:tt + 1],
                                          op0=OP.subtract, op1=OP.mult), "vS")
                xn_ready = c.val("vS")

                if spill is not None:
                    s_xgs = spill
                    sn = f"xsS{tt % 2}"
                    prevs = c.val(sn)

                    @sec("vector")
                    def _(e, tt=tt, sn=sn, prevs=prevs, s_xgs=s_xgs):
                        c.wait(e, sn, prevs)
                        c.inc(e.tensor_copy(s_xgs[:, tt % 2, :], s_xn_tm[:, 0, :]), "vS")
                    castv = c.val("vS")

                    @sec("sync")
                    def _(e, tt=tt, sn=sn, castv=castv, s_xgs=s_xgs):
                        c.wait(e, "vS", castv)
                        c.inc(e.dma_start(d_xng[tt * 128:(tt + 1) * 128, :],
                                          s_xgs[:, tt % 2, :]), sn, 16)

                for half in range(2):
                    bank = pb[4 + half]
                    for grp in range(2):
                        prev_v = c.val("vS")

                        @sec("tensor")
                        def _(e, half=half, grp=grp, bank=bank, w=max(xn_ready, prev_v)):
                            c.wait(e, "vS", w)
                            for j in range(4):
                                kf = half * 8 + grp * 4 + j
                                c.inc(e.transpose(bank[:, j * 128:(j + 1) * 128],
                                                  s_xn_tm[:, 0, kf * 128:(kf + 1) * 128],
                                                  s_idf[:]), "tS")
                        tdone = c.val("tS")

                        @sec("vector")
                        def _(e, tt=tt, half=half, grp=grp, bank=bank, tdone=tdone,
                              with_logits=with_logits):
                            c.wait(e, "tS", tdone)
                            for j in range(4):
                                kf = half * 8 + grp * 4 + j
                                c.inc(e.tensor_copy(xnT[:, kf, tt * 128:(tt + 1) * 128],
                                                    bank[:, j * 128:(j + 1) * 128]), "vS")
                            if with_logits:
                                for j in range(4):
                                    c.inc(e.tensor_copy(s_stage[:, j, :],
                                                        bank[:, j * 128:(j + 1) * 128]), "vS")
                        vdone = c.val("vS")

                        if with_logits:
                            @sec("tensor")
                            def _(e, half=half, grp=grp, vdone=vdone):
                                c.wait(e, "vS", vdone)
                                for j in range(4):
                                    kf = half * 8 + grp * 4 + j
                                    c.inc(e.matmul(pb[6][:, 0:8], s_stage[:, j, :],
                                                   s_rwTf[:, kf, :],
                                                   start=(kf == 0), stop=(kf == KF - 1)), "tS")
                if with_logits:
                    lg_mm_done = c.val("tS")

                    @sec("vector")
                    def _(e, tt=tt, lg_mm_done=lg_mm_done):
                        c.wait(e, "tS", lg_mm_done)
                        c.inc(e.tensor_copy(s_lgtm[:, tt, :], pb[6][:, 0:8]), "vS")
            return c.val("vS")

        def top2():
            @sec("vector")
            def _(e):
                c.wait(e, "vS", c.val("vS"))
                for tt in range(TT):
                    c.inc(e.reduce_max(s_r2[:, 0, tt:tt + 1], s_lgtm[:, tt, :], axis=AX.X), "vS")
                c.wait(e, "vS")
                for tt in range(TT):
                    c.inc(e.tensor_scalar(s_ce[:, tt, :], s_lgtm[:, tt, :],
                                          s_r2[:, 0, tt:tt + 1], -BIG, op0=OP.is_ge, op1=OP.mult), "vS")
                c.wait(e, "vS")
                for tt in range(TT):
                    c.inc(e.tensor_tensor(s_ce[:, tt, :], s_lgtm[:, tt, :], s_ce[:, tt, :], op=OP.add), "vS")
                c.wait(e, "vS")
                for tt in range(TT):
                    c.inc(e.reduce_max(s_r2[:, 1, tt:tt + 1], s_ce[:, tt, :], axis=AX.X), "vS")
                c.wait(e, "vS")
                c.inc(e.tensor_tensor(s_r2[:, 2, :], s_r2[:, 1, :], s_r2[:, 0, :], op=OP.subtract), "vS")
            d_done = c.val("vS")

            @sec("scalar")
            def _(e):
                c.wait(e, "vS", d_done)
                c.inc(e.activation(s_r2[:, 3, :], s_r2[:, 2, :], AF.Exp), "aS")
            ed_done = c.val("aS")

            @sec("vector")
            def _(e):
                c.wait(e, "aS", ed_done)
                c.inc(e.tensor_scalar(s_r2[:, 4, :], s_r2[:, 3, :], 1.0, None, op0=OP.add), "vS")
                c.wait(e, "vS")
                c.inc(e.reciprocal(s_r2[:, 4, :], s_r2[:, 4, :]), "vS")
                c.wait(e, "vS")
                c.inc(e.tensor_scalar(s_r2[:, 4, :], s_r2[:, 4, :], 0.5, None, op0=OP.mult), "vS")
                c.wait(e, "vS")
                c.inc(e.tensor_tensor(s_r2[:, 5, :], s_r2[:, 3, :], s_r2[:, 4, :], op=OP.mult), "vS")
                c.wait(e, "vS")
                for tt in range(TT):
                    c.inc(e.tensor_scalar(s_ce[:, tt, :], s_lgtm[:, tt, :],
                                          s_r2[:, 0, tt:tt + 1], s_r2[:, 4, tt:tt + 1],
                                          op0=OP.is_equal, op1=OP.mult), "vS")
                c.wait(e, "vS")
                for tt in range(TT):
                    c.inc(e.tensor_scalar(s_lgtm[:, tt, :], s_lgtm[:, tt, :],
                                          s_r2[:, 1, tt:tt + 1], s_r2[:, 5, tt:tt + 1],
                                          op0=OP.is_equal, op1=OP.mult), "vS")
                c.wait(e, "vS")
                for tt in range(TT):
                    c.inc(e.tensor_tensor(s_ce[:, tt, :], s_ce[:, tt, :], s_lgtm[:, tt, :], op=OP.add), "vS")
            return c.val("vS")

        def moe(fb_, ce_done, xnT_done):
            xnT = fb_["xnT"]
            s_wu, s_wd, s_act = fb_["wu"], fb_["wd"], fb_["act"]
            s_sgbb, s_subb, s_sdbb = fb_["sgb"], fb_["sub"], fb_["sdb"]
            bias_done = fb_["bias_done"]
            jobs = [("exp", i) for i in range(NEXP)] + [("sh", 0), ("sh", 1)]
            ru_free = {i: 0 for i in range(5)}
            ru_load = {}
            wd_free = [0, 0]
            wd_load = {}
            psets = [(pb[0], pb[1]), (pb[2], pb[3]), (pb[4], pb[5])]
            bank_free = [c.val("vS")] * 6
            sweep_i = 0
            hidx = 0

            def gu_src(kind, idx, mat, ibp, q):
                w = IBW[ibp]
                if kind == "exp":
                    t = d_gwt if mat == "g" else d_uwt
                    return t[idx, ibp * 4 + q, :, 0:4 * w]
                t = d_sgwt if mat == "g" else d_suwt
                return t[idx, ibp * 4 + q, :, 0:4 * w]

            def dw_src(kind, idx, fbk):
                t = d_dwt if kind == "exp" else d_sdwt
                return t[idx, fbk].rearrange("p (ic i) -> p ic i", ic=11)

            for kind, idx in jobs:
                if kind == "sh" and idx == 1:
                    prev_bias_mm = c.val("tS")

                    @sec("gpsimd")
                    def _(e, prev_bias_mm=prev_bias_mm, fb_=fb_):
                        c.wait(e, "tS", prev_bias_mm)
                        c.inc(e.dma_start(fb_["sgb"][:], d_sgb[0:1, IFFN:SI]), "bbS", 16)
                        c.inc(e.dma_start(fb_["sub"][:], d_sub[0:1, IFFN:SI]), "bbS", 16)
                    bias_done = c.val("bbS")
                for ibp in range(3):
                    w = IBW[ibp]
                    c0 = IBOFF[ibp]
                    my_h = [hidx + i for i in range(4)]
                    hidx += 4

                    @sec("sync")
                    def _(e, kind=kind, idx=idx, ibp=ibp, my_h=my_h,
                          gfree=[RGfree[hh % 5] for hh in my_h]):
                        c.wait(e, *wp_done[(kind, idx)])
                        for i, hh in enumerate(my_h):
                            buf = hh % 5
                            c.wait(e, "tS", gfree[i])
                            w4 = 4 * IBW[ibp]
                            c.inc(e.dma_start(
                                s_wg.rearrange("p b kf i -> p b (kf i)")[:, buf, 0:w4],
                                gu_src(kind, idx, "g", ibp, i)), f"wgS{buf}", 16)
                            RGload[("m", hh)] = (f"wgS{buf}", c.val(f"wgS{buf}"))

                    @sec("scalar")
                    def _(e, kind=kind, idx=idx, ibp=ibp, my_h=my_h,
                          ufree=[ru_free[hh % 5] for hh in my_h]):
                        c.wait(e, *wp_done[(kind, idx)])
                        for i, hh in enumerate(my_h):
                            buf = hh % 5
                            c.wait(e, "tS", ufree[i])
                            w4 = 4 * IBW[ibp]
                            c.inc(e.dma_start(
                                s_wu.rearrange("p b kf i -> p b (kf i)")[:, buf, 0:w4],
                                gu_src(kind, idx, "u", ibp, i)), f"wuS{buf}", 16)
                            ru_load[hh] = (f"wuS{buf}", c.val(f"wuS{buf}"))

                    for tt in range(TT):
                        si = sweep_i % 3
                        sweep_i += 1
                        pg, pu = psets[si]
                        pfree = max(bank_free[2 * si], bank_free[2 * si + 1])

                        @sec("tensor")
                        def _(e, kind=kind, idx=idx, tt=tt, w=w, c0=c0, pg=pg, pu=pu,
                              pfree=pfree, my_h=my_h):
                            c.wait(e, "vS", max(pfree, ce_done, xnT_done))
                            for kf in range(KF):
                                hh = my_h[kf // 4]
                                buf = hh % 5
                                if kf % 4 == 0:
                                    c.wait(e, *RGload[("m", hh)])
                                    c.wait(e, *ru_load[hh])
                                last = (kf == KF - 1) and (kind == "exp")
                                wgv = s_wg.rearrange("p b kf i -> p b (kf i)")
                                wuv = s_wu.rearrange("p b kf i -> p b (kf i)")
                                o0 = (kf % 4) * w
                                c.inc(e.matmul(pg[:, 0:w], xnT[:, kf, tt * 128:(tt + 1) * 128],
                                               wgv[:, buf, o0:o0 + w],
                                               start=(kf == 0), stop=last), "tS")
                                c.inc(e.matmul(pu[:, 0:w], xnT[:, kf, tt * 128:(tt + 1) * 128],
                                               wuv[:, buf, o0:o0 + w],
                                               start=(kf == 0), stop=last), "tS")
                                if kf % 4 == 3:
                                    RGfree[buf] = c.val("tS") - 1
                                    ru_free[buf] = c.val("tS")
                            if kind == "sh":
                                c.wait(e, "bbS", bias_done)
                                c.inc(e.matmul(pg[:, 0:w], s_onesbf[0:1, :],
                                               s_sgbb[0:1, c0:c0 + w],
                                               start=False, stop=True), "tS")
                                c.inc(e.matmul(pu[:, 0:w], s_onesbf[0:1, :],
                                               s_subb[0:1, c0:c0 + w],
                                               start=False, stop=True), "tS")
                        mm_done = c.val("tS")

                        prev_stt = c.val("vS")

                        @sec("scalar")
                        def _(e, tt=tt, w=w, pg=pg, mm_done=mm_done, prev_stt=prev_stt):
                            c.wait(e, "tS", mm_done)
                            c.wait(e, "vS", prev_stt)
                            c.inc(e.activation(s_sg[:, tt % 2, 0:w], pg[:, 0:w], AF.Silu), "aS")
                        silu_done = c.val("aS")

                        @sec("vector")
                        def _(e, kind=kind, idx=idx, tt=tt, w=w, c0=c0, pu=pu,
                              mm_done=mm_done, silu_done=silu_done):
                            c.wait(e, "tS", mm_done)
                            c.wait(e, "aS", silu_done)
                            scalar = s_ce[:, tt, idx:idx + 1] if kind == "exp" else 0.5
                            c.inc(e.scalar_tensor_tensor(s_act[:, tt, c0:c0 + w], pu[:, 0:w],
                                                         scalar, s_sg[:, tt % 2, 0:w],
                                                         op0=OP.mult, op1=OP.mult), "vS")
                        bank_free[2 * si] = bank_free[2 * si + 1] = c.val("vS")
                act_done = c.val("vS")

                for tt in range(TT):
                    for grp, gn in ((0, 4), (1, 4), (2, 3)):
                        prev_v = c.val("vS")

                        @sec("tensor")
                        def _(e, tt=tt, grp=grp, gn=gn, w=max(prev_v, act_done)):
                            c.wait(e, "vS", w)
                            for j in range(gn):
                                ic = grp * 4 + j
                                sl = ic % 8
                                c.inc(e.transpose(ptb[:, sl * 128:(sl + 1) * 128],
                                                  s_act[:, tt, ic * 128:(ic + 1) * 128],
                                                  s_idbf[:]), "tS")
                        tdone = c.val("tS")

                        @sec("vector")
                        def _(e, tt=tt, grp=grp, gn=gn, tdone=tdone):
                            c.wait(e, "tS", tdone)
                            for j in range(gn):
                                ic = grp * 4 + j
                                sl = ic % 8
                                c.inc(e.tensor_copy(s_act[:, tt, ic * 128:(ic + 1) * 128],
                                                    ptb[:, sl * 128:(sl + 1) * 128]), "vS")
                actT_done = c.val("vS")

                for fbk in range(4):
                    buf = fbk % 2

                    @sec("scalar")
                    def _(e, kind=kind, idx=idx, fbk=fbk, buf=buf, wfree=wd_free[buf]):
                        c.wait(e, *wp_done[(kind, idx)])
                        c.wait(e, "tS", wfree)
                        c.inc(e.dma_start(s_wd[:, buf, :, :], dw_src(kind, idx, fbk)), f"wdS{buf}", 16)
                        wd_load[fbk] = (f"wdS{buf}", c.val(f"wdS{buf}"))

                    for tt in range(TT):
                        bi = (fbk * TT + tt) % 6
                        bank = pb[bi]
                        bfree = bank_free[bi]

                        @sec("tensor")
                        def _(e, kind=kind, idx=idx, tt=tt, fbk=fbk, buf=buf, bank=bank,
                              bfree=bfree, actT_done=actT_done):
                            c.wait(e, "vS", max(bfree, actT_done))
                            c.wait(e, *wd_load[fbk])
                            add_bias = (kind == "sh" and idx == 1)
                            for ic in range(11):
                                last = (ic == 10) and not add_bias
                                c.inc(e.matmul(bank[:], s_act[:, tt, ic * 128:(ic + 1) * 128],
                                               s_wd[:, buf, ic, :],
                                               start=(ic == 0), stop=last), "tS")
                            if add_bias:
                                c.wait(e, "bbS", bias_done)
                                c.inc(e.matmul(bank[:], s_halves[0:1, :],
                                               s_sdbb[0:1, fbk * 512:(fbk + 1) * 512],
                                               start=False, stop=True), "tS")
                            if tt == TT - 1:
                                wd_free[buf] = c.val("tS")
                        dmm_done = c.val("tS")

                        @sec("vector")
                        def _(e, tt=tt, fbk=fbk, bank=bank, dmm_done=dmm_done):
                            c.wait(e, "tS", dmm_done)
                            c.inc(e.tensor_tensor(s_x[:, tt, fbk * 512:(fbk + 1) * 512],
                                                  bank[:], s_x[:, tt, fbk * 512:(fbk + 1) * 512],
                                                  op=OP.add), "vS")
                        bank_free[bi] = c.val("vS")
            return c.val("vS")

        def moe_sparse(fb_, ce_done, xnT_done):
            xnT = fb_["xnT"]
            s_wu, s_wd, s_act = fb_["wu"], fb_["wd"], fb_["act"]
            s_sgbb, s_subb, s_sdbb = fb_["sgb"], fb_["sub"], fb_["sdb"]
            s_ut, s_bind = fb_["ut"], fb_["bind"]
            s_xg, s_yg = fb_["xg"], fb_["yg"]
            s_zeroi, s_posi = fb_["zeroi"], fb_["posi"]
            s_sidx, s_sval = fb_["sidx"], fb_["sval"]
            s_svalf, s_iotap = fb_["svalf"], fb_["iotap"]
            s_tokc, s_cec = fb_["tokc"], fb_["cec"]
            s_widxg, s_widxd = fb_["widxg"], fb_["widxd"]
            stF = fb_["stage"].rearrange("p a b -> p (a b)")
            r2F = s_r2.rearrange("p a b -> p (a b)")
            lgF = s_lgtm.rearrange("p a b -> p (a b)")
            PREF = lambda a, b: stF[:, a:b]
            TB = lambda a, b: stF[:, 64 + a:64 + b]
            ZERO = stF[:, 128:128 + NSLOT]
            PAB = lambda j: stF[:, 152 + j:153 + j]
            PABf = stF[:, 152:168]
            CAB = lambda j: stF[:, 168 + j:169 + j]
            SCR = stF[:, 184:192]
            TOT = lambda a, b: stF[0:1, 192 + a:192 + b]
            TR2 = lambda a, b: stF[0:1, 272 + a:272 + b]
            SEROW = stF[0:1, 352:384]
            SIOTA = stF[0:1, 384:416]
            SCR32 = stF[0:1, 416:448]
            wgv = s_wg.rearrange("p b kf i -> p b (kf i)")
            wuv = s_wu.rearrange("p b kf i -> p b (kf i)")
            wdv = s_wd.rearrange("p b ic i -> p b (ic i)")
            d_gwf = d_gwt.rearrange("e b p r -> (e b p) r")
            d_uwf = d_uwt.rearrange("e b p r -> (e b p) r")
            d_dwf = d_dwt.rearrange("e f p r -> (e f p) r")
            bank_free = [c.val("vS")] * 6

            @sec("gpsimd")
            def _(e):
                c.wait(e, "vS", ce_done)
                c.inc(e.iota(s_iotap[:], [[1, 1]], base=0, channel_multiplier=1,
                             allow_small_or_imprecise_dtypes=True), "ioS")
                c.inc(e.iota(SIOTA, [[1, 32]], base=0, channel_multiplier=0,
                             allow_small_or_imprecise_dtypes=True), "ioS")
            iot_done = c.val("ioS")

            @sec("vector")
            def _(e):
                c.wait(e, "vS", ce_done)
                c.inc(e.memset(ZERO, 0.0), "vS")
                c.inc(e.memset(s_zeroi[:], 0), "vS")
                c64 = s_ce.rearrange("p a b -> p (a b)")
                i64 = s_bind.rearrange("p a b -> p (a b)")
                c.inc(e.tensor_scalar(i64, c64, 0.0, None, op0=OP.is_gt), "vS")
            ind_done = c.val("vS")

            @sec("gpsimd")
            def _(e):
                c.wait(e, "vS", ind_done)
                c.inc(e.dma_start(d_cev.rearrange("(p s) one -> p (s one)", p=128), ZERO), "bzS", 16)
                c.inc(e.dma_start(d_tok.rearrange("(p s) one -> p (s one)", p=128), s_zeroi[:]), "bzS", 16)
            zf_done = c.val("bzS")

            for tt in range(TT):
                bf = max(bank_free[tt % 2], bank_free[2 + tt % 2])

                @sec("tensor")
                def _(e, tt=tt, bf=bf, ut_done=fb_["ut_done"]):
                    c.wait(e, "ldS", ut_done)
                    c.wait(e, "vS", max(bf, ind_done))
                    c.inc(e.matmul(pb[tt % 2][:, 0:NEXP], s_ut[:], s_bind[:, tt, :],
                                   start=True, stop=True), "tS")
                    c.inc(e.matmul(pb[2 + tt % 2][0:1, 0:NEXP], s_ut[:, 127:128],
                                   s_bind[:, tt, :], start=True, stop=True), "tS")
                td = c.val("tS")

                @sec("vector")
                def _(e, tt=tt, td=td):
                    c.wait(e, "tS", td)
                    c.inc(e.tensor_copy(PREF(tt * 8, tt * 8 + 8), pb[tt % 2][:, 0:NEXP]), "vS")
                    c.inc(e.tensor_copy(TOT(tt * 8, tt * 8 + 8), pb[2 + tt % 2][0:1, 0:NEXP]), "vS")
                bank_free[tt % 2] = bank_free[2 + tt % 2] = c.val("vS")
            pref_done = c.val("vS")

            @sec("vector")
            def _(e):
                c.wait(e, "vS", pref_done)
                c.inc(e.tensor_copy(TR2(0, 64), TOT(0, 64)), "vS")
                c.wait(e, "vS")
                c.inc(e.tensor_tensor(TR2(8, 64), TR2(8, 64), TOT(0, 56), op=OP.add), "vS")
                c.wait(e, "vS")
                c.inc(e.tensor_copy(TOT(0, 64), TR2(0, 64)), "vS")
                c.wait(e, "vS")
                c.inc(e.tensor_tensor(TR2(16, 64), TR2(16, 64), TOT(0, 48), op=OP.add), "vS")
                c.wait(e, "vS")
                c.inc(e.tensor_copy(TOT(0, 64), TR2(0, 64)), "vS")
                c.wait(e, "vS")
                c.inc(e.tensor_tensor(TR2(32, 64), TR2(32, 64), TOT(0, 32), op=OP.add), "vS")
                c.wait(e, "vS")
                c.inc(e.memset(TOT(0, 8), 0.0), "vS")
                c.wait(e, "vS")
                c.inc(e.tensor_copy(TOT(8, 64), TR2(0, 56)), "vS")
                c.wait(e, "vS")
                c.inc(e.tensor_copy(TOT(64, 72), TR2(56, 64)), "vS")
                c.wait(e, "vS")
                c.inc(e.memset(TR2(72, 80), 0.0), "vS")
                for k in range(1, 9):
                    c.wait(e, "vS")
                    c.inc(e.tensor_scalar(TOT(72, 80), TOT(64, 72), k * 128 - 0.5, None,
                                          op0=OP.is_ge), "vS")
                    c.wait(e, "vS")
                    c.inc(e.tensor_tensor(TR2(72, 80), TR2(72, 80), TOT(72, 80), op=OP.add), "vS")
                c.wait(e, "vS")
                c.inc(e.tensor_scalar(TOT(72, 80), TR2(72, 80), 128.0, None, op0=OP.mult), "vS")
                c.wait(e, "vS")
                c.inc(e.tensor_tensor(TOT(72, 80), TOT(64, 72), TOT(72, 80), op=OP.is_gt), "vS")
                c.wait(e, "vS")
                c.inc(e.tensor_tensor(TR2(72, 80), TR2(72, 80), TOT(72, 80), op=OP.add), "vS")
                c.wait(e, "vS")
                c.inc(e.tensor_copy(TOT(72, 80), TR2(72, 80)), "vS")
                c.wait(e, "vS")
                c.inc(e.tensor_tensor(TOT(73, 80), TOT(73, 80), TR2(72, 79), op=OP.add), "vS")
                c.wait(e, "vS")
                c.inc(e.tensor_copy(TR2(72, 80), TOT(72, 80)), "vS")
                c.wait(e, "vS")
                c.inc(e.tensor_tensor(TOT(74, 80), TOT(74, 80), TR2(72, 78), op=OP.add), "vS")
                c.wait(e, "vS")
                c.inc(e.tensor_copy(TR2(72, 80), TOT(72, 80)), "vS")
                c.wait(e, "vS")
                c.inc(e.tensor_tensor(TOT(76, 80), TOT(76, 80), TR2(72, 76), op=OP.add), "vS")
                c.wait(e, "vS")
                c.inc(e.memset(TR2(72, 73), 0.0), "vS")
                c.wait(e, "vS")
                c.inc(e.tensor_copy(TR2(73, 80), TOT(72, 79)), "vS")
                c.wait(e, "vS")
                c.inc(e.tensor_scalar(TR2(72, 80), TR2(72, 80), 128.0, None, op0=OP.mult), "vS")
            offd_done = c.val("vS")

            @sec("vector")
            def _(e):
                c.wait(e, "vS", offd_done)
                c.wait(e, "ioS", iot_done)
                c.inc(e.memset(SEROW, 0.0), "vS")
                c.wait(e, "vS")
                c.inc(e.tensor_scalar(TR2(0, 8), TR2(72, 80), 1.0 / 128, None, op0=OP.mult), "vS")
                c.wait(e, "vS")
                for k in range(1, 8):
                    c.inc(e.tensor_scalar(SCR32, SIOTA, TR2(k, k + 1), None, op0=OP.is_ge), "vS")
                    c.wait(e, "vS")
                    c.inc(e.tensor_tensor(SEROW, SEROW, SCR32, op=OP.add), "vS")
                    c.wait(e, "vS")
            se_done = c.val("vS")

            bf01 = max(bank_free[0], bank_free[1])

            @sec("tensor")
            def _(e, bf01=bf01):
                c.wait(e, "vS", max(se_done, bf01))
                c.inc(e.matmul(pb[0][:, 0:8], s_onesf[:], TR2(72, 80), start=True, stop=True), "tS")
                c.inc(e.matmul(pb[1][:, 0:64], s_onesf[:], TOT(0, 64), start=True, stop=True), "tS")
            bc_done = c.val("tS")

            MBIG = 1.0e6

            @sec("vector")
            def _(e, bc_done=bc_done):
                c.wait(e, "tS", bc_done)
                c.inc(e.tensor_copy(SCR, pb[0][:, 0:8]), "vS")
                c.inc(e.tensor_copy(TB(0, 64), pb[1][:, 0:64]), "vS")
                c.wait(e, "vS")
                c.inc(e.tensor_tensor(r2F[:, 0:64], PREF(0, 64), TB(0, 64), op=OP.add), "vS")
                c.wait(e, "vS")
                for tt in range(TT):
                    c.inc(e.tensor_tensor(r2F[:, tt * 8:tt * 8 + 8],
                                          r2F[:, tt * 8:tt * 8 + 8], SCR, op=OP.add), "vS")
                c.wait(e, "vS")
                c.inc(e.tensor_scalar(r2F[:, 0:64], r2F[:, 0:64], 1.0, None, op0=OP.subtract), "vS")
                c.wait(e, "vS")
                for tt in range(TT):
                    iv = s_bind[:, tt, :]
                    pv = r2F[:, tt * 8:tt * 8 + 8]
                    c.inc(e.scalar_tensor_tensor(SCR, iv, MBIG, pv, op0=OP.mult, op1=OP.subtract), "vS")
                    c.wait(e, "vS")
                    c.inc(e.tensor_scalar(SCR, SCR, -1.0, MBIG, op0=OP.mult, op1=OP.add), "vS")
                    c.wait(e, "vS")
                    c.inc(e.tensor_reduce(PAB(tt * 2), SCR, axis=AX.X, op=OP.min), "vS")
                    c.wait(e, "vS")
                    c.inc(e.scalar_tensor_tensor(SCR, iv, MBIG, pv, op0=OP.mult, op1=OP.add), "vS")
                    c.wait(e, "vS")
                    c.inc(e.tensor_scalar(SCR, SCR, 1.0, -MBIG, op0=OP.mult, op1=OP.add), "vS")
                    c.wait(e, "vS")
                    c.inc(e.tensor_reduce(PAB(tt * 2 + 1), SCR, axis=AX.X, op=OP.max), "vS")
                    c.wait(e, "vS")
                    for k in range(2):
                        c.inc(e.tensor_scalar(SCR, pv, PAB(tt * 2 + k), None, op0=OP.is_equal), "vS")
                        c.wait(e, "vS")
                        c.inc(e.tensor_tensor(SCR, SCR, s_ce[:, tt, :], op=OP.mult), "vS")
                        c.wait(e, "vS")
                        c.inc(e.tensor_reduce(CAB(tt * 2 + k), SCR, axis=AX.X, op=OP.add), "vS")
                        c.wait(e, "vS")
            pos_done = c.val("vS")
            bank_free[0] = bank_free[1] = pos_done

            bf2 = bank_free[2]

            @sec("tensor")
            def _(e, bf2=bf2):
                c.wait(e, "vS", max(bf2, se_done))
                c.inc(e.matmul(pb[2][:, 0:32], s_onesf[:], SEROW, start=True, stop=True), "tS")
            seb_mm = c.val("tS")

            @sec("vector")
            def _(e, seb_mm=seb_mm):
                c.wait(e, "tS", seb_mm)
                c.wait(e, "ioS", iot_done)
                c.wait(e, "vS", pos_done)
                c.inc(e.tensor_copy(s_posi[:], PABf), "vS")
                c.wait(e, "vS")
                c.inc(e.tensor_copy(lgF[:, 0:32], pb[2][:, 0:32]), "vS")
                c.wait(e, "vS")
                c.inc(e.tensor_scalar(lgF[:, 32:32 + NSLOT], lgF[:, 0:NSLOT], 1536.0,
                                      s_iotap[:], op0=OP.mult, op1=OP.add), "vS")
                c.wait(e, "vS")
                c.inc(e.tensor_copy(s_widxg[:], lgF[:, 32:32 + NSLOT]), "vS")
                c.wait(e, "vS")
                c.inc(e.tensor_scalar(lgF[:, 32:32 + NSLOT], lgF[:, 0:NSLOT], 512.0,
                                      s_iotap[:], op0=OP.mult, op1=OP.add), "vS")
                c.wait(e, "vS")
                c.inc(e.tensor_copy(s_widxd[:], lgF[:, 32:32 + NSLOT]), "vS")
            widx_done = c.val("vS")
            bank_free[2] = widx_done

            for tt in range(TT):
                for k in range(2):
                    pvv = c.val("vS")
                    kprev = c.val("skS")

                    @sec("vector")
                    def _(e, tt=tt, k=k, pvv=pvv, kprev=kprev):
                        c.wait(e, "vS", max(pvv, pos_done))
                        c.wait(e, "skS", kprev)
                        c.wait(e, "ioS", iot_done)
                        c.inc(e.tensor_scalar(s_svalf[:], s_iotap[:], 1.0, float(tt * 128),
                                              op0=OP.mult, op1=OP.add), "vS")
                        c.wait(e, "vS")
                        c.inc(e.tensor_copy(s_sval[:], s_svalf[:]), "vS")
                        c.wait(e, "vS")
                        c.inc(e.tensor_copy(s_sidx[:], PAB(tt * 2 + k)), "vS")
                    spv = c.val("vS")

                    @sec("gpsimd")
                    def _(e, spv=spv):
                        c.wait(e, "vS", spv)
                        c.wait(e, "bzS", zf_done)
                        c.inc(e.indirect_dma_start(
                            d_tok[:], bass.IndirectOffsetOnAxis(ap=s_sidx[:], axis=0),
                            s_sval[:], None), "skS", 16)
                    sc1 = c.val("skS")

                    @sec("vector")
                    def _(e, sc1=sc1, tt=tt, k=k):
                        c.wait(e, "skS", sc1)
                        c.inc(e.tensor_copy(s_svalf[:], CAB(tt * 2 + k)), "vS")
                    sv2 = c.val("vS")

                    @sec("gpsimd")
                    def _(e, sv2=sv2):
                        c.wait(e, "vS", sv2)
                        c.inc(e.indirect_dma_start(
                            d_cev[:], bass.IndirectOffsetOnAxis(ap=s_sidx[:], axis=0),
                            s_svalf[:], None), "skS", 16)
            scat_done = c.val("skS")

            @sec("sync")
            def _(e):
                c.wait(e, "skS", scat_done)
                c.inc(e.dma_start(s_tokc[:], d_tok.rearrange("(s p) one -> p (s one)", p=128),
                                  allow_slow_non_contiguous=True), "tkS", 16)
                c.inc(e.dma_start(s_cec[:], d_cev.rearrange("(s p) one -> p (s one)", p=128),
                                  allow_slow_non_contiguous=True), "tkS", 16)
            tok_done = c.val("tkS")

            psets = [(pb[0], pb[1]), (pb[2], pb[3]), (pb[4], pb[5])]
            sweep_i = 0
            dn_i = 0
            ru_free = {i: 0 for i in range(4)}
            ru_load = {}
            wd_free = [0, 0]
            wd_load = {}
            xg_free = [0, 0]
            hh0 = [1000]
            uu0 = [0]
            wp_all = [(f"wpJ{j}", 16 * 28) for j in range(NEXP)]
            xs_final = (c.val("xsS0"), c.val("xsS1"))

            for s in range(NSLOT):
                xb = s % 2

                @sec("gpsimd")
                def _(e, s=s, xb=xb, xf=xg_free[xb]):
                    if s == 0:
                        for sn, v in wp_all:
                            c.wait(e, sn, v)
                        c.wait(e, "xsS0", xs_final[0])
                        c.wait(e, "xsS1", xs_final[1])
                    c.wait(e, "tkS", tok_done)
                    c.wait(e, "tS", xf)
                    c.inc(e.indirect_dma_start(
                        s_xg[:, xb, :], None, d_xng[:],
                        bass.IndirectOffsetOnAxis(ap=s_tokc[:, s:s + 1], axis=0)), f"gaS{xb}", 16)
                ga_ld = (f"gaS{xb}", c.val(f"gaS{xb}"))

                for grp in range(4):
                    prev_v = c.val("vS")

                    @sec("tensor")
                    def _(e, s=s, xb=xb, grp=grp, w=prev_v, ga_ld=ga_ld):
                        if grp == 0:
                            c.wait(e, *ga_ld)
                        c.wait(e, "vS", w)
                        for j in range(4):
                            kf = grp * 4 + j
                            sl = (grp % 2) * 4 + j
                            c.inc(e.transpose(ptb[:, sl * 128:(sl + 1) * 128],
                                              s_xg[:, xb, kf * 128:(kf + 1) * 128],
                                              s_idbf[:]), "tS")
                    tdone = c.val("tS")

                    @sec("vector")
                    def _(e, s=s, xb=xb, grp=grp, tdone=tdone):
                        c.wait(e, "tS", tdone)
                        for j in range(4):
                            kf = grp * 4 + j
                            sl = (grp % 2) * 4 + j
                            c.inc(e.tensor_copy(s_xg[:, xb, kf * 128:(kf + 1) * 128],
                                                ptb[:, sl * 128:(sl + 1) * 128]), "vS")
                xgT_done = c.val("vS")

                for ibp in range(3):
                    w = IBW[ibp]
                    c0 = IBOFF[ibp]
                    my_h = [hh0[0] + i for i in range(4)]
                    hh0[0] += 4
                    my_u = [uu0[0] + i for i in range(4)]
                    uu0[0] += 4

                    @sec("gpsimd")
                    def _(e, s=s, ibp=ibp, my_h=my_h, my_u=my_u,
                          gfree=[RGfree[hh % 5] for hh in my_h],
                          ufree=[ru_free[uu % 4] for uu in my_u]):
                        c.wait(e, "vS", widx_done)
                        w4 = 4 * IBW[ibp]
                        for i, hh in enumerate(my_h):
                            buf = hh % 5
                            c.wait(e, "tS", gfree[i])
                            c.inc(e.indirect_dma_start(
                                wgv[:, buf, 0:w4], None, d_gwf[:],
                                bass.IndirectOffsetOnAxis(ap=s_widxg[:, s:s + 1], axis=0),
                                element_offset=(ibp * 4 + i) * 128 * 2048), f"igS{buf}", 16)
                            RGload[("m", hh)] = (f"igS{buf}", c.val(f"igS{buf}"))
                        for i, uu in enumerate(my_u):
                            buf = uu % 4
                            c.wait(e, "tS", ufree[i])
                            c.inc(e.indirect_dma_start(
                                wuv[:, buf, 0:w4], None, d_uwf[:],
                                bass.IndirectOffsetOnAxis(ap=s_widxg[:, s:s + 1], axis=0),
                                element_offset=(ibp * 4 + i) * 128 * 2048), f"iuS{buf}", 16)
                            ru_load[uu] = (f"iuS{buf}", c.val(f"iuS{buf}"))

                    si = sweep_i % 3
                    sweep_i += 1
                    pg, pu = psets[si]
                    pfree = max(bank_free[2 * si], bank_free[2 * si + 1])

                    @sec("tensor")
                    def _(e, s=s, w=w, pg=pg, pu=pu, pfree=pfree, my_h=my_h, my_u=my_u,
                          xgT_done=xgT_done):
                        c.wait(e, "vS", max(pfree, xgT_done))
                        for kf in range(KF):
                            hh = my_h[kf // 4]
                            uu = my_u[kf // 4]
                            gbuf = hh % 5
                            ubuf = uu % 4
                            if kf % 4 == 0:
                                c.wait(e, *RGload[("m", hh)])
                                c.wait(e, *ru_load[uu])
                            o0 = (kf % 4) * w
                            c.inc(e.matmul(pg[:, 0:w], s_xg[:, s % 2, kf * 128:(kf + 1) * 128],
                                           wgv[:, gbuf, o0:o0 + w],
                                           start=(kf == 0), stop=(kf == KF - 1)), "tS")
                            c.inc(e.matmul(pu[:, 0:w], s_xg[:, s % 2, kf * 128:(kf + 1) * 128],
                                           wuv[:, ubuf, o0:o0 + w],
                                           start=(kf == 0), stop=(kf == KF - 1)), "tS")
                            if kf % 4 == 3:
                                RGfree[gbuf] = c.val("tS") - 1
                                ru_free[ubuf] = c.val("tS")
                    mm_done = c.val("tS")
                    if ibp == 2:
                        xg_free[s % 2] = mm_done

                    prev_stt = c.val("vS")

                    @sec("scalar")
                    def _(e, s=s, w=w, pg=pg, mm_done=mm_done, prev_stt=prev_stt):
                        c.wait(e, "tS", mm_done)
                        c.wait(e, "vS", prev_stt)
                        c.inc(e.activation(s_sg[:, s % 2, 0:w], pg[:, 0:w], AF.Silu), "aS")
                    silu_done = c.val("aS")

                    @sec("vector")
                    def _(e, s=s, w=w, c0=c0, pu=pu, mm_done=mm_done, silu_done=silu_done):
                        c.wait(e, "tS", mm_done)
                        c.wait(e, "aS", silu_done)
                        c.wait(e, "tkS", tok_done)
                        c.inc(e.scalar_tensor_tensor(s_act[:, s % 2, c0:c0 + w], pu[:, 0:w],
                                                     s_cec[:, s:s + 1], s_sg[:, s % 2, 0:w],
                                                     op0=OP.mult, op1=OP.mult), "vS")
                    bank_free[2 * si] = bank_free[2 * si + 1] = c.val("vS")
                act_done = c.val("vS")

                for grp, gn in ((0, 4), (1, 4), (2, 3)):
                    prev_v = c.val("vS")

                    @sec("tensor")
                    def _(e, s=s, grp=grp, gn=gn, w=max(prev_v, act_done)):
                        c.wait(e, "vS", w)
                        for j in range(gn):
                            ic = grp * 4 + j
                            sl = ic % 8
                            c.inc(e.transpose(ptb[:, sl * 128:(sl + 1) * 128],
                                              s_act[:, s % 2, ic * 128:(ic + 1) * 128],
                                              s_idbf[:]), "tS")
                    tdone = c.val("tS")

                    @sec("vector")
                    def _(e, s=s, grp=grp, gn=gn, tdone=tdone):
                        c.wait(e, "tS", tdone)
                        for j in range(gn):
                            ic = grp * 4 + j
                            sl = ic % 8
                            c.inc(e.tensor_copy(s_act[:, s % 2, ic * 128:(ic + 1) * 128],
                                                ptb[:, sl * 128:(sl + 1) * 128]), "vS")
                actT_done = c.val("vS")

                for fbk in range(4):
                    buf = fbk % 2

                    @sec("gpsimd")
                    def _(e, s=s, fbk=fbk, buf=buf, wfree=wd_free[buf]):
                        c.wait(e, "vS", widx_done)
                        c.wait(e, "tS", wfree)
                        c.inc(e.indirect_dma_start(
                            wdv[:, buf, :], None, d_dwf[:],
                            bass.IndirectOffsetOnAxis(ap=s_widxd[:, s:s + 1], axis=0),
                            element_offset=fbk * 128 * (11 * 512)), f"idS{buf}", 16)
                        wd_load[fbk] = (f"idS{buf}", c.val(f"idS{buf}"))

                    bi = dn_i % 6
                    dn_i += 1
                    bank = pb[bi]
                    bfree = bank_free[bi]

                    @sec("tensor")
                    def _(e, s=s, fbk=fbk, buf=buf, bank=bank, bfree=bfree,
                          actT_done=actT_done):
                        c.wait(e, "vS", max(bfree, actT_done))
                        c.wait(e, *wd_load[fbk])
                        for ic in range(11):
                            c.inc(e.matmul(bank[:], s_act[:, s % 2, ic * 128:(ic + 1) * 128],
                                           s_wd[:, buf, ic, :],
                                           start=(ic == 0), stop=(ic == 10)), "tS")
                        wd_free[buf] = c.val("tS")
                    dmm_done = c.val("tS")

                    yb = dn_i % 2
                    ysn = f"ysS{yb}"
                    ysp = c.val(ysn)

                    @sec("vector")
                    def _(e, bank=bank, dmm_done=dmm_done, ysn=ysn, ysp=ysp, yb=yb):
                        c.wait(e, "tS", dmm_done)
                        c.wait(e, ysn, ysp)
                        c.inc(e.tensor_copy(s_yg[:, yb, :], bank[:]), "vS")
                    cpv = c.val("vS")
                    bank_free[bi] = cpv

                    @sec("sync")
                    def _(e, s=s, fbk=fbk, yb=yb, cpv=cpv, ysn=ysn):
                        c.wait(e, "vS", cpv)
                        c.inc(e.dma_start(d_yasg[s * 128:(s + 1) * 128,
                                                 fbk * 512:(fbk + 1) * 512],
                                          s_yg[:, yb, :]), ysn, 16)

            ys_final = (c.val("ysS0"), c.val("ysS1"))
            ga_free = [c.val("vS")] * 2
            gcnt = [0]
            for tt in range(TT):
                for k in range(2):
                    for qc in range(4):
                        gb = gcnt[0] % 2
                        gcnt[0] += 1

                        @sec("gpsimd")
                        def _(e, tt=tt, k=k, qc=qc, gb=gb, gf=ga_free[gb],
                              first=(gcnt[0] == 1)):
                            if first:
                                c.wait(e, "ysS0", ys_final[0])
                                c.wait(e, "ysS1", ys_final[1])
                            c.wait(e, "vS", max(gf, widx_done))
                            c.inc(e.indirect_dma_start(
                                s_yg[:, gb, :], None, d_yasg[:],
                                bass.IndirectOffsetOnAxis(
                                    ap=s_posi[:, tt * 2 + k:tt * 2 + k + 1], axis=0),
                                element_offset=qc * 512), f"ygS{gb}", 16)
                        gld = (f"ygS{gb}", c.val(f"ygS{gb}"))

                        @sec("vector")
                        def _(e, tt=tt, qc=qc, gb=gb, gld=gld):
                            c.wait(e, *gld)
                            c.inc(e.tensor_tensor(s_x[:, tt, qc * 512:(qc + 1) * 512],
                                                  s_yg[:, gb, :],
                                                  s_x[:, tt, qc * 512:(qc + 1) * 512],
                                                  op=OP.add), "vS")
                        ga_free[gb] = c.val("vS")

            bias_mm_last = [0]
            sdb_mm_last = [0]
            for idx in range(2):
                for ibp in range(3):
                    w = IBW[ibp]
                    c0 = IBOFF[ibp]
                    my_h = [hh0[0] + i for i in range(4)]
                    hh0[0] += 4
                    my_u = [uu0[0] + i for i in range(4)]
                    uu0[0] += 4

                    @sec("gpsimd")
                    def _(e, idx=idx, c0=c0, w=w, pbm=bias_mm_last[0]):
                        c.wait(e, "tS", pbm)
                        c.inc(e.dma_start(s_sgbb[0:1, 0:w],
                                          d_sgb[0:1, idx * IFFN + c0:idx * IFFN + c0 + w]), "bbS", 16)
                        c.inc(e.dma_start(s_subb[0:1, 0:w],
                                          d_sub[0:1, idx * IFFN + c0:idx * IFFN + c0 + w]), "bbS", 16)
                    bias_done = c.val("bbS")

                    @sec("sync")
                    def _(e, idx=idx, ibp=ibp, my_h=my_h,
                          gfree=[RGfree[hh % 5] for hh in my_h]):
                        c.wait(e, "wpJ8", 16 * 56)
                        w4 = 4 * IBW[ibp]
                        for i, hh in enumerate(my_h):
                            buf = hh % 5
                            c.wait(e, "tS", gfree[i])
                            c.inc(e.dma_start(wgv[:, buf, 0:w4],
                                              d_sgwt[idx, ibp * 4 + i, :, 0:w4]), f"wgS{buf}", 16)
                            RGload[("m", hh)] = (f"wgS{buf}", c.val(f"wgS{buf}"))

                    @sec("scalar")
                    def _(e, idx=idx, ibp=ibp, my_u=my_u,
                          ufree=[ru_free[uu % 4] for uu in my_u]):
                        c.wait(e, "wpJ8", 16 * 56)
                        w4 = 4 * IBW[ibp]
                        for i, uu in enumerate(my_u):
                            buf = uu % 4
                            c.wait(e, "tS", ufree[i])
                            c.inc(e.dma_start(wuv[:, buf, 0:w4],
                                              d_suwt[idx, ibp * 4 + i, :, 0:w4]), f"wuS{buf}", 16)
                            ru_load[uu] = (f"wuS{buf}", c.val(f"wuS{buf}"))

                    for tt in range(TT):
                        si = sweep_i % 3
                        sweep_i += 1
                        pg, pu = psets[si]
                        pfree = max(bank_free[2 * si], bank_free[2 * si + 1])

                        @sec("tensor")
                        def _(e, tt=tt, w=w, pg=pg, pu=pu, pfree=pfree, my_h=my_h,
                              my_u=my_u, bias_done=bias_done, xnT_done=xnT_done):
                            c.wait(e, "vS", max(pfree, xnT_done))
                            for kf in range(KF):
                                hh = my_h[kf // 4]
                                uu = my_u[kf // 4]
                                gbuf = hh % 5
                                ubuf = uu % 4
                                if kf % 4 == 0:
                                    c.wait(e, *RGload[("m", hh)])
                                    c.wait(e, *ru_load[uu])
                                o0 = (kf % 4) * w
                                c.inc(e.matmul(pg[:, 0:w], xnT[:, kf, tt * 128:(tt + 1) * 128],
                                               wgv[:, gbuf, o0:o0 + w],
                                               start=(kf == 0), stop=False), "tS")
                                c.inc(e.matmul(pu[:, 0:w], xnT[:, kf, tt * 128:(tt + 1) * 128],
                                               wuv[:, ubuf, o0:o0 + w],
                                               start=(kf == 0), stop=False), "tS")
                                if kf % 4 == 3:
                                    RGfree[gbuf] = c.val("tS") - 1
                                    ru_free[ubuf] = c.val("tS")
                            c.wait(e, "bbS", bias_done)
                            c.inc(e.matmul(pg[:, 0:w], s_onesbf[0:1, :], s_sgbb[0:1, 0:w],
                                           start=False, stop=True), "tS")
                            c.inc(e.matmul(pu[:, 0:w], s_onesbf[0:1, :], s_subb[0:1, 0:w],
                                           start=False, stop=True), "tS")
                        mm_done = c.val("tS")
                        bias_mm_last[0] = mm_done

                        prev_stt = c.val("vS")

                        @sec("scalar")
                        def _(e, tt=tt, w=w, pg=pg, mm_done=mm_done, prev_stt=prev_stt):
                            c.wait(e, "tS", mm_done)
                            c.wait(e, "vS", prev_stt)
                            c.inc(e.activation(s_sg[:, tt % 2, 0:w], pg[:, 0:w], AF.Silu), "aS")
                        silu_done = c.val("aS")

                        @sec("vector")
                        def _(e, tt=tt, w=w, c0=c0, pu=pu, mm_done=mm_done,
                              silu_done=silu_done):
                            c.wait(e, "tS", mm_done)
                            c.wait(e, "aS", silu_done)
                            c.inc(e.scalar_tensor_tensor(s_act[:, tt, c0:c0 + w], pu[:, 0:w],
                                                         0.5, s_sg[:, tt % 2, 0:w],
                                                         op0=OP.mult, op1=OP.mult), "vS")
                        bank_free[2 * si] = bank_free[2 * si + 1] = c.val("vS")
                act_done = c.val("vS")

                for tt in range(TT):
                    for grp, gn in ((0, 4), (1, 4), (2, 3)):
                        prev_v = c.val("vS")

                        @sec("tensor")
                        def _(e, tt=tt, grp=grp, gn=gn, w=max(prev_v, act_done)):
                            c.wait(e, "vS", w)
                            for j in range(gn):
                                ic = grp * 4 + j
                                sl = ic % 8
                                c.inc(e.transpose(ptb[:, sl * 128:(sl + 1) * 128],
                                                  s_act[:, tt, ic * 128:(ic + 1) * 128],
                                                  s_idbf[:]), "tS")
                        tdone = c.val("tS")

                        @sec("vector")
                        def _(e, tt=tt, grp=grp, gn=gn, tdone=tdone):
                            c.wait(e, "tS", tdone)
                            for j in range(gn):
                                ic = grp * 4 + j
                                sl = ic % 8
                                c.inc(e.tensor_copy(s_act[:, tt, ic * 128:(ic + 1) * 128],
                                                    ptb[:, sl * 128:(sl + 1) * 128]), "vS")
                actT_done = c.val("vS")

                for fbk in range(4):
                    buf = fbk % 2
                    add_bias = (idx == 1)
                    if add_bias:
                        @sec("gpsimd")
                        def _(e, fbk=fbk, pbm=sdb_mm_last[0]):
                            c.wait(e, "tS", pbm)
                            c.inc(e.dma_start(s_sdbb[0:1, 0:512],
                                              d_sdb[0:1, fbk * 512:(fbk + 1) * 512]), "bbS", 16)
                        sdb_done = c.val("bbS")

                    @sec("scalar")
                    def _(e, idx=idx, fbk=fbk, buf=buf, wfree=wd_free[buf]):
                        c.wait(e, "wpJ8", 16 * 56)
                        c.wait(e, "tS", wfree)
                        c.inc(e.dma_start(s_wd[:, buf, :, :],
                                          d_sdwt[idx, fbk].rearrange("p (ic i) -> p ic i", ic=11)),
                              f"wdS{buf}", 16)
                        wd_load[fbk] = (f"wdS{buf}", c.val(f"wdS{buf}"))

                    for tt in range(TT):
                        bi = dn_i % 6
                        dn_i += 1
                        bank = pb[bi]
                        bfree = bank_free[bi]

                        @sec("tensor")
                        def _(e, idx=idx, tt=tt, fbk=fbk, buf=buf, bank=bank, bfree=bfree,
                              actT_done=actT_done, add_bias=add_bias,
                              sdb_done=(c.val("bbS") if add_bias else 0)):
                            c.wait(e, "vS", max(bfree, actT_done))
                            c.wait(e, *wd_load[fbk])
                            for ic in range(11):
                                last = (ic == 10) and not add_bias
                                c.inc(e.matmul(bank[:], s_act[:, tt, ic * 128:(ic + 1) * 128],
                                               s_wd[:, buf, ic, :],
                                               start=(ic == 0), stop=last), "tS")
                            if add_bias:
                                c.wait(e, "bbS", sdb_done)
                                c.inc(e.matmul(bank[:], s_halves[0:1, :], s_sdbb[0:1, 0:512],
                                               start=False, stop=True), "tS")
                            if tt == TT - 1:
                                wd_free[buf] = c.val("tS")
                        dmm_done = c.val("tS")
                        if add_bias:
                            sdb_mm_last[0] = dmm_done

                        @sec("vector")
                        def _(e, tt=tt, fbk=fbk, bank=bank, dmm_done=dmm_done):
                            c.wait(e, "tS", dmm_done)
                            c.inc(e.tensor_tensor(s_x[:, tt, fbk * 512:(fbk + 1) * 512],
                                                  bank[:], s_x[:, tt, fbk * 512:(fbk + 1) * 512],
                                                  op=OP.add), "vS")
                        bank_free[bi] = c.val("vS")
            return c.val("vS")

        wincnt = [0]

        def open_ffn_window(st, sparse=False):
            wincnt[0] += 1
            sfx = wincnt[0]
            fb_ = {}
            fb_["sparse"] = sparse
            fb_["xnT"] = sb(st, f"s_xnT_{sfx}", [128, KF, TPC], BF16)
            fb_["wu"] = sb(st, f"s_wu_{sfx}", [128, 4 if sparse else 5, 4, 512], BF16)
            fb_["wd"] = sb(st, f"s_wd_{sfx}", [128, 2, 11, 512], BF16)
            fb_["act"] = sb(st, f"s_act_{sfx}", [128, TT, IFFN], BF16)
            if sparse:
                fb_["sgb"] = sb(st, f"s_sgbb_{sfx}", [1, 512], BF16)
                fb_["sub"] = sb(st, f"s_subb_{sfx}", [1, 512], BF16)
                fb_["sdb"] = sb(st, f"s_sdbb_{sfx}", [1, 512], BF16)
            else:
                fb_["sgb"] = sb(st, f"s_sgbb_{sfx}", [1, IFFN], BF16)
                fb_["sub"] = sb(st, f"s_subb_{sfx}", [1, IFFN], BF16)
                fb_["sdb"] = sb(st, f"s_sdbb_{sfx}", [1, H], BF16)
            fb_["stage"] = sb(st, f"s_stage_{sfx}", [128, 4, 128], F32)
            if sparse:
                fb_["ut"] = sb(st, f"s_ut_{sfx}", [128, 128], BF16)
                fb_["bind"] = sb(st, f"s_bind_{sfx}", [128, TT, NEXP], BF16)
                fb_["xg"] = sb(st, f"s_xg_{sfx}", [128, 2, H], BF16)
                fb_["yg"] = sb(st, f"s_yg_{sfx}", [128, 2, 512], F32)
                fb_["zeroi"] = sb(st, f"s_zeroi_{sfx}", [128, NSLOT], I32)
                fb_["posi"] = sb(st, f"s_posi_{sfx}", [128, 16], I32)
                fb_["sidx"] = sb(st, f"s_sidx_{sfx}", [128, 1], I32)
                fb_["sval"] = sb(st, f"s_sval_{sfx}", [128, 1], I32)
                fb_["svalf"] = sb(st, f"s_svalf_{sfx}", [128, 1], F32)
                fb_["iotap"] = sb(st, f"s_iotap_{sfx}", [128, 1], F32)
                fb_["tokc"] = sb(st, f"s_tokc_{sfx}", [128, NSLOT], I32)
                fb_["cec"] = sb(st, f"s_cec_{sfx}", [128, NSLOT], F32)
                fb_["widxg"] = sb(st, f"s_widxg_{sfx}", [128, NSLOT], I32)
                fb_["widxd"] = sb(st, f"s_widxd_{sfx}", [128, NSLOT], I32)

                @sec("sync")
                def _(e, fb_=fb_):
                    c.inc(e.dma_start(fb_["ut"][:], d_utri[:]), "ldS", 16)
                fb_["ut_done"] = c.val("ldS")
            else:
                @sec("gpsimd")
                def _(e, fb_=fb_):
                    c.inc(e.dma_start(fb_["sgb"][:], d_sgb[0:1, 0:IFFN]), "bbS", 16)
                    c.inc(e.dma_start(fb_["sub"][:], d_sub[0:1, 0:IFFN]), "bbS", 16)
                    c.inc(e.dma_start(fb_["sdb"][:], d_sdb[:]), "bbS", 16)
            fb_["bias_done"] = c.val("bbS")
            fb_["sq_junk"] = fb_["act"].rearrange("p a b -> p (a b)")[:, 0:H]
            return fb_

        w2 = ExitStack()
        fb1 = open_ffn_window(w2)
        xnT1_done = layernorm_to_xnT(fb1["xnT"], fb1["sq_junk"],
                                     xwait=("vS", rwT_done), with_logits=True,
                                     s_stage=fb1["stage"])
        ce_done = top2()
        x1_done = moe(fb1, ce_done, xnT1_done)
        if debug:
            @sec("sync")
            def _(e):
                c.wait(e, "vS", x1_done)
                c.inc(e.dma_start(dbg["x1"].rearrange("(tt p) f -> p tt f", p=128), s_x[:]), "outS", 16)

        fb1_xnT = fb1["xnT"]
        w2.close()
        soft_barrier()

        w3 = ExitStack()
        fm1 = sb(w3, "s_fm1", [128, KF, TPC], BF16)
        fm2 = sb(w3, "s_fm2", [128, KF, TPC], BF16)
        s_acc = sb(w3, "s_acc", [128, 64, 8], F32)
        s_accr = sb(w3, "s_accr", [128, 64], F32)
        s_hw64 = sb(w3, "s_hw64", [128, 64], F32)
        s_relc = sb(w3, "s_relc", [128, 8], F32)
        s_relrow = sb(w3, "s_relrow", [8, 128], F32)
        w1 = ExitStack()
        xnT2 = sb(w1, "s_xnT2", [128, KF, TPC], BF16)

        sqj2 = fm1.rearrange("p a b -> p (a b)")[:, 0:H]
        xnT2_done = layernorm_to_xnT(xnT2, sqj2, xwait=("vS", x1_done), with_logits=False)

        phidx = [120]

        def tm_sweep(name, wsrc, lhsT_buf, lhsT_done, evac_fn, nco=4, dst_wait=()):
            psets = [pb[0], pb[1], pb[2]]
            pfree = [c.val("vS")] * 3
            swp = [0]
            for coln in range(nco):
                my_h = [phidx[0] + i for i in range(4)]
                phidx[0] += 4

                @sec("sync")
                def _(e, coln=coln, my_h=my_h, gfree=[RGfree[hh % 5] for hh in my_h]):
                    c.wait(e, "wpA", wpA_done)
                    for i, hh in enumerate(my_h):
                        buf = hh % 5
                        c.wait(e, "tS", gfree[i])
                        c.inc(e.dma_start(
                            s_wg[:, buf, :, :],
                            wsrc[coln * 4 + i].rearrange("p (kf i) -> p kf i", kf=4)), f"wgS{buf}", 16)
                        RGload[("m", hh)] = (f"wgS{buf}", c.val(f"wgS{buf}"))

                for tt in range(TT):
                    si = swp[0] % 3
                    swp[0] += 1
                    pg = psets[si]
                    pf = pfree[si]

                    @sec("tensor")
                    def _(e, tt=tt, coln=coln, pg=pg, pf=pf, my_h=my_h, lhsT_done=lhsT_done):
                        c.wait(e, "vS", max(pf, lhsT_done))
                        for kf in range(KF):
                            hh = my_h[kf // 4]
                            buf = hh % 5
                            if kf % 4 == 0:
                                c.wait(e, *RGload[("m", hh)])
                            c.inc(e.matmul(pg[:], lhsT_buf[:, kf, tt * 128:(tt + 1) * 128],
                                           s_wg[:, buf, kf % 4, :],
                                           start=(kf == 0), stop=(kf == KF - 1)), "tS")
                            if kf % 4 == 3:
                                RGfree[buf] = c.val("tS")
                    mm_done = c.val("tS")

                    @sec("vector")
                    def _(e, tt=tt, coln=coln, pg=pg, mm_done=mm_done, dst_wait=dst_wait):
                        c.wait(e, "tS", mm_done)
                        for dw_ in dst_wait:
                            c.wait(e, dw_[0], dw_[1])
                        evac_fn(e, tt, coln, pg)
                    pfree[si] = c.val("vS")
            return c.val("vS")

        def tm_store(dst):
            dv = dst.rearrange("p a b -> p (a b)").rearrange("p (tt f) -> p tt f", tt=TT)

            def f(e, tt, coln, psum):
                c.inc(e.tensor_copy(dv[:, tt, coln * 512:(coln + 1) * 512], psum[:]), "vS")
            return f

        def transpose_tm_to_fm(src, dst, src_done, dst_wait=()):
            sv = src.rearrange("p a b -> p (a b)").rearrange("p (tt f) -> p tt f", tt=TT)
            for tt in range(TT):
                for grp in range(4):
                    prev_v = c.val("vS")

                    @sec("tensor")
                    def _(e, tt=tt, grp=grp, w=max(prev_v, src_done)):
                        c.wait(e, "vS", w)
                        for j in range(4):
                            kf = grp * 4 + j
                            sl = (grp % 2) * 4 + j
                            c.inc(e.transpose(ptb[:, sl * 128:(sl + 1) * 128],
                                              sv[:, tt, kf * 128:(kf + 1) * 128],
                                              s_idbf[:]), "tS")
                    tdone = c.val("tS")

                    @sec("vector")
                    def _(e, tt=tt, grp=grp, tdone=tdone, dst_wait=dst_wait):
                        c.wait(e, "tS", tdone)
                        for dw_ in dst_wait:
                            c.wait(e, dw_[0], dw_[1])
                        for j in range(4):
                            kf = grp * 4 + j
                            sl = (grp % 2) * 4 + j
                            c.inc(e.tensor_copy(dst[:, kf, tt * 128:(tt + 1) * 128],
                                                ptb[:, sl * 128:(sl + 1) * 128]), "vS")
            return c.val("vS")

        vtm_done = tm_sweep("v", d_Wvt, xnT2, xnT2_done, tm_store(fm1))
        vfm_done = transpose_tm_to_fm(fm1, fm2, vtm_done)

        @sec("sync")
        def _(e):
            c.wait(e, "vS", vfm_done)
            c.inc(e.dma_start(d_cc1i[4096:6144, :].rearrange("(kf p) t -> p kf t", p=128),
                              fm2[:]), "cbS", 16)
        v_sent = c.val("cbS")

        ktm_done = tm_sweep("k", d_Wkt, xnT2, xnT2_done, tm_store(fm1))
        kfm_done = transpose_tm_to_fm(fm1, fm2, ktm_done,
                                      dst_wait=(("cbS", v_sent),))

        @sec("sync")
        def _(e):
            c.wait(e, "vS", kfm_done)
            c.inc(e.dma_start(d_cc1i[0:2048, :].rearrange("(kf p) t -> p kf t", p=128),
                              fm2[:]), "cbS", 16)
        k_sent = c.val("cbS")

        kptm_done = tm_sweep("kp", d_Wkit, fm2, kfm_done, tm_store(fm1))
        kpfm_done = transpose_tm_to_fm(fm1, fm2, kptm_done,
                                       dst_wait=(("cbS", k_sent),))

        @sec("sync")
        def _(e):
            c.wait(e, "vS", kpfm_done)
            c.inc(e.dma_start(d_cc1i[2048:4096, :].rearrange("(kf p) t -> p kf t", p=128),
                              fm2[:]), "cbS", 16)
        kp_sent = c.val("cbS")

        qtm_done = tm_sweep("q", d_Wqt, xnT2, xnT2_done, tm_store(fm1))
        qfm_done = transpose_tm_to_fm(fm1, fm2, qtm_done,
                                      dst_wait=(("cbS", kp_sent),))

        @sec("sync")
        def _(e):
            c.wait(e, "vS", qfm_done)
            c.inc(e.dma_start(d_qsp.rearrange("(kf p) t -> p kf t", p=128), fm2[:]), "qsS", 16)
        q_sent = c.val("qsS")

        qptm_done = tm_sweep("qp", d_Wqit, fm2, qfm_done, tm_store(fm1))
        @sec("vector")
        def _(e):
            c.wait(e, "qsS", q_sent)
            c.inc(e.tensor_copy(s_stats[0:1, 0, 0:1], s_stats[0:1, 0, 0:1]), "vS")
        qp_fence = c.val("vS")
        qpfm_done = transpose_tm_to_fm(fm1, fm2, max(qptm_done, qp_fence))

        w1.close()

        @sec("gpsimd")
        def _(e):
            c.wait(e, "cbS", max(v_sent, k_sent, kp_sent))
            c.inc(e.collective_compute(
                "AllGather", OP.bypass,
                ins=[d_cc1i.ap().opt()], outs=[d_cc1o.ap().opt()],
                replica_groups=[[0, 1, 2, 3], [4, 5, 6, 7]]), "ccS")
        cc1_done = c.val("ccS")

        @sec("tensor")
        def _(e):
            c.wait(e, "ldS", lds_done)
            c.wait(e, "vS", c.val("vS"))
            c.inc(e.matmul(pb[6][:, 0:8], s_onesf[:], s_hw[:], start=True, stop=True), "tS")
        hwb_mm = c.val("tS")

        @sec("vector")
        def _(e, hwb_mm=hwb_mm):
            c.wait(e, "tS", hwb_mm)
            for qt in range(TT):
                c.inc(e.tensor_copy(s_hw64[:, qt * 8:(qt + 1) * 8], pb[6][:, 0:8]), "vS")
        hw64_done = c.val("vS")

        ind_junk = fm1.rearrange("p a b -> p (a b)")
        pfree_ind = [("vS", 0)] * 4
        ev_i = [0]
        for ih in range(NIH):
            for kb in range(8):
                my_h = [phidx[0], phidx[0] + 1]
                phidx[0] += 2
                rr, tl0 = kb // 2, (kb % 2) * 512

                @sec("sync")
                def _(e, ih=ih, kb=kb, rr=rr, tl0=tl0, my_h=my_h,
                      gfree=[RGfree[hh % 5] for hh in my_h], cc1_done=cc1_done):
                    c.wait(e, "ccS", cc1_done)
                    for i, hh in enumerate(my_h):
                        buf = hh % 5
                        c.wait(e, "tS", gfree[i])
                        c.inc(e.dma_start(
                            s_wg[:, buf, 0, :],
                            d_cc1o[rr * CC1R + 2048 + ih * 256 + i * 128:
                                   rr * CC1R + 2048 + ih * 256 + i * 128 + 128,
                                   tl0:tl0 + 512]), f"wgS{buf}", 16)
                        RGload[("m", hh)] = (f"wgS{buf}", c.val(f"wgS{buf}"))

                for qt in range(TT):
                    bi = ev_i[0] % 4
                    bank = pb[bi]
                    pf = pfree_ind[bi]

                    @sec("tensor")
                    def _(e, ih=ih, kb=kb, qt=qt, bank=bank, pf=pf, my_h=my_h,
                          qpfm_done=qpfm_done):
                        if qt == 0:
                            c.wait(e, *RGload[("m", my_h[0])])
                            c.wait(e, *RGload[("m", my_h[1])])
                            c.wait(e, "vS", qpfm_done)
                        c.wait(e, pf[0], pf[1])
                        for i, hh in enumerate(my_h):
                            buf = hh % 5
                            c.inc(e.matmul(bank[:], fm2[:, ih * 2 + i, qt * 128:(qt + 1) * 128],
                                           s_wg[:, buf, 0, :],
                                           start=(i == 0), stop=(i == 1)), "tS")
                        if qt == TT - 1:
                            for hh in my_h:
                                RGfree[hh % 5] = c.val("tS")
                    imm_done = c.val("tS")

                    acc_ap = s_acc[:, qt * 8 + ih, kb:kb + 1]
                    if ev_i[0] % 2 == 0:
                        @sec("scalar")
                        def _(e, bank=bank, acc_ap=acc_ap, imm_done=imm_done):
                            c.wait(e, "tS", imm_done)
                            c.wait(e, "aS", c.val("aS"))
                            c.inc(e.activation(ind_junk[:, 0:512], bank[:], AF.Relu,
                                               accum_out=acc_ap), "aS")
                        pfree_ind[bi] = ("aS", c.val("aS"))
                    else:
                        @sec("vector")
                        def _(e, bank=bank, acc_ap=acc_ap, imm_done=imm_done):
                            c.wait(e, "tS", imm_done)
                            c.wait(e, "vS", c.val("vS"))
                            c.inc(e.tensor_scalar(ind_junk[:, 512:1024], bank[:], 0.0, 0.0,
                                                  op0=OP.max, op1=OP.add, accum_out=acc_ap), "vS")
                        pfree_ind[bi] = ("vS", c.val("vS"))
                    ev_i[0] += 1

        ind_mm_all = c.val("tS")
        ind_ev_a = c.val("aS")
        ind_ev_v = c.val("vS")

        @sec("vector")
        def _(e):
            c.wait(e, "aS", ind_ev_a)
            c.wait(e, "vS", ind_ev_v)
            c.wait(e, "vS", hw64_done)
            c.inc(e.tensor_reduce(s_accr[:], s_acc[:], axis=AX.X, op=OP.add), "vS")
            c.wait(e, "vS")
            c.inc(e.tensor_tensor(s_accr[:], s_accr[:], s_hw64[:], op=OP.mult), "vS")
            c.wait(e, "vS")
            c.inc(e.tensor_reduce(s_relc[:], s_accr.rearrange("p (qt ih) -> p qt ih", qt=TT),
                                  axis=AX.X, op=OP.add), "vS")
        relc_raw = c.val("vS")

        @sec("scalar")
        def _(e):
            c.wait(e, "ldS", lds_done)
            c.inc(e.activation(s_temp[:], s_temp[:], AF.Exp, scale=-1.0), "aS")
        et_done = c.val("aS")

        @sec("tensor")
        def _(e, et_done=et_done):
            c.wait(e, "aS", et_done)
            c.wait(e, "vS", c.val("vS"))
            c.inc(e.matmul(pb[5][:, 0:1], s_onesf[:], s_temp[:], start=True, stop=True), "tS")
        etb_mm = c.val("tS")

        @sec("vector")
        def _(e, etb_mm=etb_mm):
            c.wait(e, "tS", etb_mm)
            c.inc(e.tensor_copy(s_smallc[:, 0:1], pb[5][:, 0:1]), "vS")
            c.wait(e, "vS")
            c.inc(e.tensor_scalar(s_relc[:], s_relc[:], s_smallc[:, 0:1], None, op0=OP.mult), "vS")
        relc_done = c.val("vS")

        @sec("tensor")
        def _(e, relc_done=relc_done):
            c.wait(e, "vS", relc_done)
            c.inc(e.transpose(pb[6][0:8, 0:128], s_relc[:], s_idf[:]), "tS")
        relT_mm = c.val("tS")

        @sec("vector")
        def _(e, relT_mm=relT_mm):
            c.wait(e, "tS", relT_mm)
            c.inc(e.tensor_copy(s_relrow[:], pb[6][0:8, 0:128]), "vS")
        relrow_done = c.val("vS")

        @sec("sync")
        def _(e, relrow_done=relrow_done):
            c.wait(e, "vS", relrow_done)
            c.inc(e.dma_start(d_cc2i[0:1, :].rearrange("one (qt q) -> qt (one q)", qt=TT),
                              s_relrow[:]), "rlS", 16)
        rel_sent = c.val("rlS")

        @sec("gpsimd")
        def _(e, rel_sent=rel_sent):
            c.wait(e, "rlS", rel_sent)
            c.inc(e.collective_compute(
                "AllGather", OP.bypass,
                ins=[d_cc2i.ap().opt()], outs=[d_cc2o.ap().opt()],
                replica_groups=[[0, 1, 2, 3], [4, 5, 6, 7]]), "ccS")
        cc2_done = c.val("ccS")

        if debug:
            @sec("sync")
            def _(e, cc2_done=cc2_done):
                c.wait(e, "ccS", cc2_done)
                c.inc(e.dma_start(dbg["rel"][:], d_cc2o[:]), "outS", 16)

        w3.close()
        soft_barrier()

        w4 = ExitStack()
        s_rel4 = [sb(w4, f"s_rel4_{r}", [1, TPC], F32) for r in range(4)]
        s_relb = sb(w4, "s_relb", [128, 4096], F32)
        s_bjunk = sb(w4, "s_bjunk", [128, 4096], BF16)
        s_iot1 = sb(w4, "s_iot1", [1, 128], F32)
        s_trow = sb(w4, "s_trow", [1, 128], F32)
        s_crow = sb(w4, "s_crow", [1, 128], F32)
        s_brw = sb(w4, "s_brw", [1, 128], mybir.dt.uint8)
        s_brw2 = sb(w4, "s_brw2", [1, 128], mybir.dt.uint8)
        s_neg = sb(w4, "s_neg", [1, 128], F32)
        s_pos = sb(w4, "s_pos", [1, 128], F32)

        @sec("vector")
        def _(e):
            c.wait(e, "vS", c.val("vS"))
            c.inc(e.memset(s_neg[:], -3.0e38), "vS")
            c.inc(e.memset(s_pos[:], 3.0e38), "vS")

        @sec("sync")
        def _(e, cc2_done=cc2_done):
            c.wait(e, "ccS", cc2_done)
            for r in range(4):
                c.inc(e.dma_start(s_rel4[r][:], d_cc2o[r:r + 1, :]), "r4S", 16)
        rel4_done = c.val("r4S")

        @sec("gpsimd")
        def _(e):
            c.inc(e.iota(s_iot1[:], [[1, 128]], base=0, channel_multiplier=0,
                         allow_small_or_imprecise_dtypes=True), "ioS")
        iot1_done = c.val("ioS")

        for rseg in range(8):
            r, i0 = rseg // 2, (rseg % 2) * 512
            bank = pb[rseg % 2]
            prev_v = c.val("vS")

            @sec("tensor")
            def _(e, r=r, i0=i0, bank=bank, prev_v=prev_v, rel4_done=rel4_done):
                c.wait(e, "r4S", rel4_done)
                c.wait(e, "vS", prev_v)
                c.inc(e.matmul(bank[:], s_onesf[:], s_rel4[r][0:1, i0:i0 + 512],
                               start=True, stop=True), "tS")
            mmd = c.val("tS")

            @sec("vector")
            def _(e, rseg=rseg, bank=bank, mmd=mmd):
                c.wait(e, "tS", mmd)
                c.inc(e.tensor_copy(s_relb[:, rseg * 512:(rseg + 1) * 512], bank[:]), "vS")
        relb_done = c.val("vS")

        for g in range(NKT):
            bank = pb[2 + (g % 2)]
            prev_v = c.val("vS")

            @sec("tensor")
            def _(e, g=g, bank=bank, prev_v=prev_v, relb_done=relb_done):
                c.wait(e, "vS", max(prev_v, relb_done))
                c.inc(e.transpose(bank[:, 0:128], s_relb[:, g * 128:(g + 1) * 128], s_idf[:]), "tS")
            mmd = c.val("tS")

            @sec("vector")
            def _(e, g=g, bank=bank, mmd=mmd):
                c.wait(e, "tS", mmd)
                c.inc(e.tensor_copy(s_relpc[:, g:g + 1], bank[:, 0:1]), "vS")
        relpc_done = c.val("vS")

        @sec("vector")
        def _(e, relb_done=relb_done):
            c.wait(e, "vS", relb_done)
            c.inc(e.tensor_reduce(s_bis[0:1, 0:1], s_relb[0:1, :], axis=AX.X, op=OP.min), "vS")
            c.inc(e.tensor_reduce(s_bis[0:1, 1:2], s_relb[0:1, :], axis=AX.X, op=OP.max), "vS")
        bis_init = c.val("vS")

        for rnd in range(NROUND):
            @sec("vector")
            def _(e, rnd=rnd):
                c.wait(e, "vS", c.val("vS"))
                if rnd == 0:
                    c.wait(e, "ioS", iot1_done)
                c.inc(e.tensor_tensor(s_bis[0:1, 2:3], s_bis[0:1, 1:2], s_bis[0:1, 0:1],
                                      op=OP.subtract), "vS")
                c.wait(e, "vS")
                c.inc(e.tensor_scalar(s_bis[0:1, 2:3], s_bis[0:1, 2:3], 1.0 / 127.0, None,
                                      op0=OP.mult), "vS")
                c.wait(e, "vS")
                c.inc(e.tensor_scalar(s_trow[:], s_iot1[:], s_bis[0:1, 2:3], s_bis[0:1, 0:1],
                                      op0=OP.mult, op1=OP.add), "vS")
            trow_done = c.val("vS")

            bank = pb[4]
            @sec("tensor")
            def _(e, trow_done=trow_done, bank=bank):
                c.wait(e, "vS", trow_done)
                c.inc(e.transpose(bank[:, 0:1], s_trow[:], s_idf[0:1, 0:1]), "tS")
            tcol_mm = c.val("tS")

            @sec("vector")
            def _(e, tcol_mm=tcol_mm, bank=bank):
                c.wait(e, "tS", tcol_mm)
                c.inc(e.tensor_copy(s_smallc[:, 1:2], bank[:, 0:1]), "vS")
                c.wait(e, "vS")
                c.inc(e.tensor_scalar(s_bjunk[:], s_relb[:], s_smallc[:, 1:2], 0.0,
                                      op0=OP.is_ge, op1=OP.add,
                                      accum_out=s_smallc[:, 2:3]), "vS")
            cnt_done = c.val("vS")

            bank2 = pb[5]
            @sec("tensor")
            def _(e, cnt_done=cnt_done, bank2=bank2):
                c.wait(e, "vS", cnt_done)
                c.inc(e.transpose(bank2[0:1, 0:128], s_smallc[:, 2:3], s_idf[:]), "tS")
            crow_mm = c.val("tS")

            @sec("vector")
            def _(e, crow_mm=crow_mm, bank2=bank2):
                c.wait(e, "tS", crow_mm)
                c.inc(e.tensor_copy(s_crow[:], bank2[0:1, 0:128]), "vS")
                c.wait(e, "vS")
                c.inc(e.tensor_scalar(s_brw[:], s_crow[:], 1024.0, None, op0=OP.is_ge), "vS")
                c.inc(e.tensor_scalar(s_brw2[:], s_crow[:], 1024.0, None, op0=OP.is_lt), "vS")
                c.wait(e, "vS")
                c.inc(e.select(s_crow[:], s_brw[:], s_trow[:], s_neg[:], add_drain=True), "vS")
                c.wait(e, "vS")
                c.inc(e.tensor_reduce(s_bis[0:1, 0:1], s_crow[:], axis=AX.X, op=OP.max), "vS")
                c.wait(e, "vS")
                c.inc(e.select(s_crow[:], s_brw2[:], s_trow[:], s_pos[:], add_drain=True), "vS")
                c.wait(e, "vS")
                c.inc(e.tensor_reduce(s_bis[0:1, 1:2], s_crow[:], axis=AX.X, op=OP.min), "vS")

        thr_done = c.val("vS")
        bankt = pb[6]

        @sec("tensor")
        def _(e, thr_done=thr_done, bankt=bankt):
            c.wait(e, "vS", thr_done)
            c.inc(e.matmul(bankt[:, 0:1], s_onesf[:], s_bis[0:1, 0:1], start=True, stop=True), "tS")
        lob_mm = c.val("tS")

        @sec("vector")
        def _(e, lob_mm=lob_mm, bankt=bankt):
            c.wait(e, "tS", lob_mm)
            c.inc(e.tensor_copy(s_smallc[:, 3:4], bankt[:, 0:1]), "vS")
            c.wait(e, "vS")
            c.wait(e, "vS", relpc_done)
            c.inc(e.tensor_scalar(s_sel[:], s_relpc[:], s_smallc[:, 3:4], None, op0=OP.is_ge), "vS")
        sel_done = c.val("vS")

        if debug:
            @sec("sync")
            def _(e, sel_done=sel_done):
                c.wait(e, "vS", sel_done)
                c.inc(e.dma_start(dbg["sel"][:], s_sel[:]), "outS", 16)

        w4.close()
        soft_barrier()

        w5 = ExitStack()
        s_mask = sb(w5, "s_mask", [128, NKT, 512], BF16)
        s_av = sb(w5, "s_av", [128, KF, TPC], BF16)
        s_kh = sb(w5, "s_kh", [128, 2, NKT, 128], BF16)
        s_vh = sb(w5, "s_vh", [128, 2, NKT, 128], BF16)
        s_qh = sb(w5, "s_qh", [128, 2, 512], BF16)
        s_P = sb(w5, "s_P", [128, 4, 512], BF16)
        s_qpos = sb(w5, "s_qpos", [128, 512], F32)
        s_kpos = sb(w5, "s_kpos", [128, NKT], F32)
        s_kp512 = sb(w5, "s_kp512", [128, NKT], F32)
        s_sumr = sb(w5, "s_sumr", [1, 512], F32)
        s_rb = sb(w5, "s_rb", [128, 512], F32)
        s_c32 = sb(w5, "s_c32", [128, 1], F32)

        @sec("vector")
        def _(e):
            c.inc(e.memset(s_c32[:], ESHIFT), "vS")
        c32_done = c.val("vS")

        @sec("gpsimd")
        def _(e):
            c.inc(e.iota(s_kpos[:], [[128, NKT]], base=0, channel_multiplier=1,
                         allow_small_or_imprecise_dtypes=True), "ioS")
        kpos_done = c.val("ioS")

        @sec("vector")
        def _(e, kpos_done=kpos_done):
            c.wait(e, "ioS", kpos_done)
            c.inc(e.tensor_scalar(s_kp512[:], s_kpos[:], 512.5, None, op0=OP.add), "vS")
        kp512_done = c.val("vS")

        @sec("tensor")
        def _(e):
            c.wait(e, "ldS", lds_done)
            c.wait(e, "vS", c.val("vS"))
            c.inc(e.matmul(pb[6][:, 0:1], s_onesf[:], s_meta[0:1, 0:1], start=True, stop=True), "tS")
        q0b_mm = c.val("tS")

        @sec("vector")
        def _(e, q0b_mm=q0b_mm):
            c.wait(e, "tS", q0b_mm)
            c.inc(e.tensor_copy(s_smallc[:, 4:5], pb[6][:, 0:1]), "vS")
        q0b_done = c.val("vS")

        kh_free = [0, 0]
        vh_free = [0, 0]
        qh_free = [0, 0]
        kh_load = {}
        vh_load = {}
        qh_load = {}
        exp_done = [0, 0]
        pslot_free = [0, 0]
        pav_rb_free = [0, 0]

        for qb in range(2):
            @sec("gpsimd")
            def _(e, qb=qb):
                c.wait(e, "vS", c.val("vS"))
                c.inc(e.iota(s_qpos[:], [[1, 512]], base=qb * 512, channel_multiplier=0,
                             allow_small_or_imprecise_dtypes=True), "ioS")
            qio_done = c.val("ioS")

            @sec("vector")
            def _(e, qio_done=qio_done, q0b_done=q0b_done):
                c.wait(e, "ioS", qio_done)
                c.wait(e, "vS", q0b_done)
                c.inc(e.tensor_scalar(s_qpos[:], s_qpos[:], s_smallc[:, 4:5], None, op0=OP.add), "vS")
            qpos_done = c.val("vS")

            prev_t_mask = c.val("tS")

            @sec("vector")
            def _(e, qpos_done=qpos_done, sel_done=sel_done, kp512_done=kp512_done,
                  prev_t_mask=prev_t_mask):
                c.wait(e, "vS", max(qpos_done, sel_done, kp512_done))
                c.wait(e, "tS", prev_t_mask)
                for g in range(NKT):
                    c.inc(e.tensor_scalar(s_P[:, 0, :], s_qpos[:], s_kp512[:, g:g + 1],
                                          s_sel[:, g:g + 1], op0=OP.is_lt, op1=OP.max), "vS")
                    c.wait(e, "vS")
                    c.inc(e.scalar_tensor_tensor(s_mask[:, g, :], s_qpos[:], s_kpos[:, g:g + 1],
                                                 s_P[:, 0, :], op0=OP.is_ge, op1=OP.mult), "vS")
                    c.wait(e, "vS")
            mask_done = c.val("vS")

            for h in range(NH):
                hb = h % 2
                @sec("sync")
                def _(e, qb=qb, h=h, hb=hb, khf=kh_free[hb], cc1_done=cc1_done):
                    c.wait(e, "ccS", cc1_done)
                    c.wait(e, "tS", khf)
                    for r in range(4):
                        c.inc(e.dma_start(
                            s_kh[:, hb, r * 8:(r + 1) * 8, :].rearrange("p g f -> p (g f)"),
                            d_cc1o[r * CC1R + h * 128:r * CC1R + h * 128 + 128, :]), f"stS{hb}", 16)
                    kh_load[(qb, h)] = (f"stS{hb}", c.val(f"stS{hb}"))

                @sec("scalar")
                def _(e, qb=qb, h=h, hb=hb, vhf=vh_free[hb], qhf=qh_free[hb], q_sent=q_sent):
                    c.wait(e, "qsS", q_sent)
                    c.wait(e, "tS", max(vhf, qhf))
                    for r in range(4):
                        c.inc(e.dma_start(
                            s_vh[:, hb, r * 8:(r + 1) * 8, :],
                            d_cc1o[r * CC1R + 4096 + h * 128:
                                   r * CC1R + 4096 + h * 128 + 128, :],
                            transpose=True), f"vqS{hb}", 16)
                    c.inc(e.dma_start(s_qh[:, hb, :],
                                      d_qsp[h * 128:(h + 1) * 128, qb * 512:(qb + 1) * 512]), f"vqS{hb}", 16)
                    vh_load[(qb, h)] = (f"vqS{hb}", c.val(f"vqS{hb}"))

                pav = pb[4 + (h % 2)]
                psums_ap = pb[6][0:1, :] if h % 2 == 0 else pb[6][32:33, :]
                for gq in range(NKT // 2):
                    sb0 = pb[(gq % 2) * 2]
                    sb1 = pb[(gq % 2) * 2 + 1]

                    @sec("tensor")
                    def _(e, qb=qb, h=h, hb=hb, gq=gq, sb0=sb0, sb1=sb1,
                          ef=exp_done[gq % 2], pvf=pav_rb_free[h % 2],
                          vq=vh_load.get((qb, h))):
                        if gq == 0:
                            c.wait(e, *kh_load[(qb, h)])
                            c.wait(e, *vq)
                            c.wait(e, "vS", max(mask_done, pvf))
                        c.wait(e, "aS", ef)
                        for j, sbk in ((0, sb0), (1, sb1)):
                            g = gq * 2 + j
                            c.inc(e.matmul(sbk[:], s_kh[:, hb, g, :], s_qh[:, hb, :],
                                           start=True, stop=True), "tS")
                    smm = c.val("tS")

                    @sec("scalar")
                    def _(e, gq=gq, sb0=sb0, sb1=sb1, smm=smm,
                          pf=pslot_free[gq % 2], c32_done=c32_done):
                        c.wait(e, "tS", max(smm, pf))
                        if gq == 0:
                            c.wait(e, "vS", max(c32_done, mask_done))
                        for j, sbk in ((0, sb0), (1, sb1)):
                            c.inc(e.activation(s_P[:, (gq % 2) * 2 + j, :], sbk[:], AF.Exp,
                                               bias=s_c32[:], scale=SCALE), "aS")
                    exp_done[gq % 2] = c.val("aS")
                    edone = c.val("aS")

                    @sec("vector")
                    def _(e, gq=gq, edone=edone):
                        c.wait(e, "aS", edone)
                        for j in range(2):
                            g = gq * 2 + j
                            sl = (gq % 2) * 2 + j
                            c.inc(e.tensor_tensor(s_P[:, sl, :], s_P[:, sl, :],
                                                  s_mask[:, g, :], op=OP.mult), "vS")
                    multdone = c.val("vS")

                    @sec("tensor")
                    def _(e, qb=qb, h=h, hb=hb, gq=gq, pav=pav, psums_ap=psums_ap,
                          multdone=multdone, vq=vh_load.get((qb, h))):
                        c.wait(e, "vS", multdone)
                        if gq == 0:
                            c.wait(e, *vq)
                        for j in range(2):
                            g = gq * 2 + j
                            sl = (gq % 2) * 2 + j
                            c.inc(e.matmul(pav[:], s_vh[:, hb, g, :], s_P[:, sl, :],
                                           start=(g == 0), stop=(g == NKT - 1)), "tS")
                            c.inc(e.matmul(psums_ap, s_onesbf[:, 0:1], s_P[:, sl, :],
                                           start=(g == 0), stop=(g == NKT - 1)), "tS")
                        if gq == NKT // 2 - 1:
                            vh_free[hb] = c.val("tS")
                            kh_free[hb] = c.val("tS")
                            qh_free[hb] = c.val("tS")
                    pslot_free[gq % 2] = c.val("tS")
                av_mm_done = c.val("tS")

                @sec("vector")
                def _(e, h=h, psums_ap=psums_ap, av_mm_done=av_mm_done):
                    c.wait(e, "tS", av_mm_done)
                    c.inc(e.tensor_copy(s_sumr[:], psums_ap), "vS")
                    c.wait(e, "vS")
                    c.inc(e.reciprocal(s_sumr[:], s_sumr[:]), "vS")
                sum_done = c.val("vS")

                rbb = pb[4 + ((h + 1) % 2)]

                @sec("tensor")
                def _(e, sum_done=sum_done, rbb=rbb, pvf=pav_rb_free[(h + 1) % 2]):
                    c.wait(e, "vS", max(sum_done, pvf))
                    c.inc(e.matmul(rbb[:], s_onesf[:], s_sumr[:], start=True, stop=True), "tS")
                rb_mm = c.val("tS")

                @sec("vector")
                def _(e, qb=qb, h=h, pav=pav, rbb=rbb, rb_mm=rb_mm):
                    c.wait(e, "tS", rb_mm)
                    c.inc(e.tensor_copy(s_rb[:], rbb[:]), "vS")
                    c.wait(e, "vS")
                    c.inc(e.tensor_tensor(s_av[:, h, qb * 512:(qb + 1) * 512], pav[:],
                                          s_rb[:], op=OP.mult), "vS")
                pav_rb_free[(h + 1) % 2] = c.val("vS")
                pav_rb_free[h % 2] = c.val("vS")
        av_done = c.val("vS")

        def wo_evac(e, tt, coln, psum):
            c.inc(e.tensor_tensor(s_x[:, tt, coln * 512:(coln + 1) * 512], psum[:],
                                  s_x[:, tt, coln * 512:(coln + 1) * 512], op=OP.add), "vS")

        x2_done = tm_sweep("wo", d_Wot, s_av, av_done, wo_evac)

        if debug:
            @sec("sync")
            def _(e, x2_done=x2_done):
                c.wait(e, "vS", x2_done)
                c.inc(e.dma_start(dbg["x2"].rearrange("(tt p) f -> p tt f", p=128), s_x[:]), "outS", 16)

        w5.close()
        soft_barrier()

        w6 = ExitStack()
        fb2 = open_ffn_window(w6, sparse=True)
        xnT3_done = layernorm_to_xnT(fb2["xnT"], fb2["sq_junk"],
                                     xwait=("vS", x2_done), with_logits=True,
                                     s_stage=fb2["stage"], spill=fb2["xg"])
        ce2_done = top2()
        x3_done = moe_sparse(fb2, ce2_done, xnT3_done)
        w6.close()

        @sec("sync")
        def _(e):
            c.wait(e, "vS", c.val("vS"))
            c.wait(e, "aS", c.val("aS"))
            c.wait(e, "tS", c.val("tS"))
            c.inc(e.dma_start(d_out.rearrange("(tt p) f -> p tt f", p=128), s_x[:]), "outS", 16)
            c.wait(e, "outS", c.val("outS"))

    return nc, dbg



def make_in_maps(inputs):
    x_full = np.ascontiguousarray(inputs["L0_attn_hidden_states"], dtype=np.float32)
    ident = np.eye(128, dtype=np.float32)
    consts = dict(
        id_bf=ident.astype(ml_dtypes.bfloat16),
        id_f32=ident,
        ones_bf=np.ones((128, 128), dtype=np.float32).astype(ml_dtypes.bfloat16),
        ones_f32=np.ones((1, 128), dtype=np.float32),
        halves_bf=np.full((1, 128), 0.5, dtype=np.float32).astype(ml_dtypes.bfloat16),
        utri=np.triu(np.ones((128, 128), dtype=np.float32)).astype(ml_dtypes.bfloat16),
        Wq=inputs["L0_attn_Wq"], Wk=inputs["L0_attn_Wk"], Wv=inputs["L0_attn_Wv"],
        Wo=inputs["L0_attn_Wo"], Wq_ind=inputs["L0_attn_Wq_ind"], Wk_ind=inputs["L0_attn_Wk_ind"],
        head_w=np.asarray(inputs["L0_attn_head_weights"], dtype=np.float32).reshape(1, 8),
        temp=np.asarray(inputs["L0_attn_temperature_param"], dtype=np.float32).reshape(1, 1),
        router_w=inputs["L0_ffn_router_w"],
        gate_w=inputs["L0_ffn_gate_w"], up_w=inputs["L0_ffn_up_w"], down_w=inputs["L0_ffn_down_w"],
        s_gate_w=inputs["L0_ffn_s_gate_w"], s_up_w=inputs["L0_ffn_s_up_w"],
        s_down_w=inputs["L0_ffn_s_down_w"],
        s_gate_b=np.asarray(inputs["L0_ffn_s_gate_b"], dtype=np.float32).reshape(1, SI),
        s_up_b=np.asarray(inputs["L0_ffn_s_up_b"], dtype=np.float32).reshape(1, SI),
        s_down_b=np.asarray(inputs["L0_ffn_s_down_b"], dtype=np.float32).reshape(1, H),
    )
    out = {}
    for k, v in consts.items():
        v = np.ascontiguousarray(v)
        if v.dtype != ml_dtypes.bfloat16:
            v = v.astype(np.float32, copy=False)
        out[k] = v
    in_maps = []
    for core in range(NCORES):
        b, sl = core // 4, core % 4
        meta = np.zeros((1, 8), dtype=np.float32)
        meta[0, 0] = sl * TPC
        meta[0, 1] = float(b)
        m = dict(out)
        m["x_in"] = np.ascontiguousarray(x_full[b, sl * TPC:(sl + 1) * TPC, :])
        m["meta"] = meta
        in_maps.append(m)
    return in_maps


def kernel(**inputs):
    nc, _ = build_program(debug=False)
    in_maps = make_in_maps(inputs)
    res = run_bass_kernel_spmd(nc, in_maps, core_ids=list(range(NCORES)))
    out = np.empty((B, S, H), dtype=np.float32)
    for core in range(NCORES):
        b, sl = core // 4, core % 4
        out[b, sl * TPC:(sl + 1) * TPC, :] = res.results[core]["x_out"]
    return out


# version 10
# speedup vs baseline: 1.0040x; 1.0040x over previous
import numpy as np
import ml_dtypes
from contextlib import ExitStack

import concourse.bass as bass
import concourse.mybir as mybir
from concourse.alu_op_type import AluOpType as OP
from concourse.bass_utils import run_bass_kernel_spmd

F32 = mybir.dt.float32
BF16 = mybir.dt.bfloat16
U8 = mybir.dt.uint8
I32 = mybir.dt.int32
AF = mybir.ActivationFunctionType
AX = mybir.AxisListType

B, S, H = 2, 4096, 2048
NCORES = 8
TPC = 1024
TT = 8
KF = 16
NH, HD = 16, 128
NIH = 8
NEXP, IFFN = 8, 1408
SI = 2816
SCALE = float(np.float32(1.0) / np.sqrt(np.float32(HD)))
ESHIFT = -32.0
BIG = 1.0e9
NROUND = 8
EPS = 1e-5
NSLOT = 23
IBW = [512, 512, 384]
IBOFF = [0, 512, 1024]
NKT = 32

DEBUG = False


class Ctx:
    def __init__(self, nc, stack):
        self.nc = nc
        self.stack = stack
        self.sem = {}
        self.cnt = {}

    def mksem(self, name):
        self.sem[name] = self.stack.enter_context(self.nc.semaphore(name))
        self.cnt[name] = 0

    def inc(self, inst, name, n=1):
        inst.then_inc(self.sem[name], n)
        self.cnt[name] += n
        return self.cnt[name]

    def val(self, name):
        return self.cnt[name]

    def wait(self, e, name, v=None):
        if v is None:
            v = self.cnt[name]
        if v > 0:
            e.wait_ge(self.sem[name], v)


def build_program(debug=DEBUG):
    nc = bass.Bass("TRN2", target_bir_lowering=False)

    d_x = nc.dram_tensor("x_in", [TPC, H], F32, kind="ExternalInput")
    d_meta = nc.dram_tensor("meta", [1, 8], F32, kind="ExternalInput")
    d_idbf = nc.dram_tensor("id_bf", [128, 128], BF16, kind="ExternalInput")
    d_idf = nc.dram_tensor("id_f32", [128, 128], F32, kind="ExternalInput")
    d_onesbf = nc.dram_tensor("ones_bf", [128, 128], BF16, kind="ExternalInput")
    d_onesf = nc.dram_tensor("ones_f32", [1, 128], F32, kind="ExternalInput")
    d_halves = nc.dram_tensor("halves_bf", [1, 128], BF16, kind="ExternalInput")
    d_Wq = nc.dram_tensor("Wq", [H, H], F32, kind="ExternalInput")
    d_Wk = nc.dram_tensor("Wk", [H, H], F32, kind="ExternalInput")
    d_Wv = nc.dram_tensor("Wv", [H, H], F32, kind="ExternalInput")
    d_Wo = nc.dram_tensor("Wo", [H, H], F32, kind="ExternalInput")
    d_Wqi = nc.dram_tensor("Wq_ind", [H, H], F32, kind="ExternalInput")
    d_Wki = nc.dram_tensor("Wk_ind", [H, H], F32, kind="ExternalInput")
    d_hw = nc.dram_tensor("head_w", [1, 8], F32, kind="ExternalInput")
    d_temp = nc.dram_tensor("temp", [1, 1], F32, kind="ExternalInput")
    d_rw = nc.dram_tensor("router_w", [8, H], F32, kind="ExternalInput")
    d_gw = nc.dram_tensor("gate_w", [NEXP, H, IFFN], F32, kind="ExternalInput")
    d_uw = nc.dram_tensor("up_w", [NEXP, H, IFFN], F32, kind="ExternalInput")
    d_dw = nc.dram_tensor("down_w", [NEXP, IFFN, H], F32, kind="ExternalInput")
    d_sgw = nc.dram_tensor("s_gate_w", [H, SI], F32, kind="ExternalInput")
    d_suw = nc.dram_tensor("s_up_w", [H, SI], F32, kind="ExternalInput")
    d_sdw = nc.dram_tensor("s_down_w", [SI, H], F32, kind="ExternalInput")
    d_sgb = nc.dram_tensor("s_gate_b", [1, SI], F32, kind="ExternalInput")
    d_sub = nc.dram_tensor("s_up_b", [1, SI], F32, kind="ExternalInput")
    d_sdb = nc.dram_tensor("s_down_b", [1, H], F32, kind="ExternalInput")
    d_out = nc.dram_tensor("x_out", [TPC, H], F32, kind="ExternalOutput")

    d_gwt = nc.dram_tensor("gwt", [NEXP, 12, 128, 2048], BF16)
    d_uwt = nc.dram_tensor("uwt", [NEXP, 12, 128, 2048], BF16)
    d_dwt = nc.dram_tensor("dwt", [NEXP, 4, 128, 11 * 512], BF16)
    d_sgwt = nc.dram_tensor("sgwt", [2, 12, 128, 2048], BF16)
    d_suwt = nc.dram_tensor("suwt", [2, 12, 128, 2048], BF16)
    d_sdwt = nc.dram_tensor("sdwt", [2, 4, 128, 11 * 512], BF16)
    d_Wqt = nc.dram_tensor("Wqt", [16, 128, 2048], BF16)
    d_Wkt = nc.dram_tensor("Wkt", [16, 128, 2048], BF16)
    d_Wvt = nc.dram_tensor("Wvt", [16, 128, 2048], BF16)
    d_Wot = nc.dram_tensor("Wot", [16, 128, 2048], BF16)
    d_Wqit = nc.dram_tensor("Wqit", [16, 128, 2048], BF16)
    d_Wkit = nc.dram_tensor("Wkit", [16, 128, 2048], BF16)

    CC1R = 6144
    d_cc1i = nc.dram_tensor("cc1i", [CC1R, TPC], BF16)
    d_cc1o = nc.dram_tensor("cc1o", [4 * CC1R, TPC], BF16)
    d_cc2i = nc.dram_tensor("cc2i", [1, TPC], F32)
    d_cc2o = nc.dram_tensor("cc2o", [4, TPC], F32)
    d_qsp = nc.dram_tensor("qspill", [H, TPC], BF16)
    d_utri = nc.dram_tensor("utri", [128, 128], BF16, kind="ExternalInput")
    d_tok = nc.dram_tensor("tokidx", [NSLOT * 128, 1], I32)
    d_cev = nc.dram_tensor("cevals", [NSLOT * 128, 1], F32)
    d_xng = nc.dram_tensor("xng", [TPC, H], BF16)
    d_yasg = nc.dram_tensor("yasg", [NSLOT * 128, H], F32)

    dbg = {}
    if debug:
        dbg["x1"] = nc.dram_tensor("dbg_x1", [TPC, H], F32, kind="ExternalOutput")
        dbg["rel"] = nc.dram_tensor("dbg_rel", [4, TPC], F32, kind="ExternalOutput")
        dbg["sel"] = nc.dram_tensor("dbg_sel", [128, NKT], F32, kind="ExternalOutput")
        dbg["x2"] = nc.dram_tensor("dbg_x2", [TPC, H], F32, kind="ExternalOutput")

    main = ExitStack()
    with main:
        c = Ctx(nc, main)
        sb = lambda st, *a: st.enter_context(nc.sbuf_tensor(*a))
        psn = lambda *a: main.enter_context(nc.psum_tensor(*a))
        semnames = (["tS", "vS", "aS", "gS", "ioS", "ccS", "ldS", "bbS", "cbS",
                     "qsS", "rlS", "r4S", "outS", "wpA"]
                    + [f"wpJ{j}" for j in range(9)]
                    + [f"wgS{i}" for i in range(5)] + [f"wuS{i}" for i in range(5)]
                    + [f"wdS{i}" for i in range(2)] + [f"stS{i}" for i in range(2)]
                    + [f"vqS{i}" for i in range(2)]
                    + ["bzS", "skS", "tkS"]
                    + [f"xsS{i}" for i in range(2)] + [f"gaS{i}" for i in range(2)]
                    + [f"igS{i}" for i in range(5)] + [f"iuS{i}" for i in range(4)]
                    + [f"idS{i}" for i in range(2)] + [f"ysS{i}" for i in range(2)]
                    + [f"ygS{i}" for i in range(2)])
        for sname in semnames:
            c.mksem(sname)

        s_x = sb(main, "s_x", [128, TT, H], F32)
        s_idbf = sb(main, "s_idbf", [128, 128], BF16)
        s_idf = sb(main, "s_idf", [128, 128], F32)
        s_onesbf = sb(main, "s_onesbf", [128, 128], BF16)
        s_onesf = sb(main, "s_onesf", [1, 128], F32)
        s_halves = sb(main, "s_halves", [1, 128], BF16)
        s_meta = sb(main, "s_meta", [1, 8], F32)
        s_hw = sb(main, "s_hw", [1, 8], F32)
        s_temp = sb(main, "s_temp", [1, 1], F32)
        s_stats = sb(main, "s_stats", [128, 6, TT], F32)
        s_xn_tm = sb(main, "s_xn_tm", [128, 1, H], F32)
        s_rwTf = sb(main, "s_rwTf", [128, KF, 8], F32)
        s_lgtm = sb(main, "s_lgtm", [128, TT, 8], F32)
        s_r2 = sb(main, "s_r2", [128, 8, TT], F32)
        s_ce = sb(main, "s_ce", [128, TT, 8], F32)
        s_wg = sb(main, "s_wg", [128, 5, 4, 512], BF16)
        s_sg = sb(main, "s_sg", [128, 2, 512], BF16)
        s_relpc = sb(main, "s_relpc", [128, NKT], F32)
        s_sel = sb(main, "s_sel", [128, NKT], F32)
        s_bis = sb(main, "s_bis", [1, 16], F32)
        s_smallc = sb(main, "s_smallc", [128, 8], F32)

        pb = [psn(f"pb{i}", [128, 512], F32) for i in range(7)]
        ptb = psn("ptb", [128, 1024], BF16)

        block = main.enter_context(nc.Block())
        sec = lambda name: getattr(block, name)

        def snap():
            return {k: c.val(k) for k in c.cnt}

        def soft_barrier():
            bar = snap()
            for en in ("tensor", "vector", "scalar", "gpsimd", "sync"):
                @sec(en)
                def _(e, bar=bar):
                    for k, v in bar.items():
                        c.wait(e, k, v)

        RGfree = {i: 0 for i in range(5)}
        RGload = {}

        @sec("sync")
        def _(e):
            c.inc(e.dma_start(s_x[:], d_x.rearrange("(tt p) f -> p tt f", p=128)), "ldS", 16)
            c.inc(e.dma_start(s_idbf[:], d_idbf[:]), "ldS", 16)
            c.inc(e.dma_start(s_idf[:], d_idf[:]), "ldS", 16)
            c.inc(e.dma_start(s_onesbf[:], d_onesbf[:]), "ldS", 16)
            c.inc(e.dma_start(s_onesf[:], d_onesf[:]), "ldS", 16)
            c.inc(e.dma_start(s_meta[:], d_meta[:]), "ldS", 16)
            c.inc(e.dma_start(s_hw[:], d_hw[:]), "ldS", 16)
            c.inc(e.dma_start(s_temp[:], d_temp[:]), "ldS", 16)
            c.inc(e.dma_start(s_xn_tm[0:8, 0, :], d_rw[:]), "ldS", 16)
            c.inc(e.dma_start(s_halves[:], d_halves[:]), "ldS", 16)
        lds_done = c.val("ldS")

        wp_done = {}

        def prep_gu(e, dstt, srcw, semn):
            n = 0
            for ibp in range(3):
                w = 512 if ibp < 2 else 384
                for q in range(4):
                    c.inc(e.dma_start(
                        dstt[ibp * 4 + q, :, 0:4 * w].rearrange("p (kf i) -> p kf i", kf=4),
                        srcw[512 * q:512 * (q + 1), 512 * ibp:512 * ibp + w]
                        .rearrange("(kf p) i -> p kf i", p=128)), semn, 16)
                    n += 1
            return n

        def prep_dw(e, dstt, srcw, semn):
            for fb in range(4):
                c.inc(e.dma_start(
                    dstt[fb].rearrange("p (ic i) -> p ic i", ic=11),
                    srcw[:, 512 * fb:512 * (fb + 1)]
                    .rearrange("(ic p) i -> p ic i", p=128)), semn, 16)
            return 4

        def prep_att(e, dstt, srcw, semn):
            dv = dstt.rearrange("b p (kf i) -> b p kf i", kf=4)
            for coln in range(4):
                for q in range(4):
                    c.inc(e.dma_start(
                        dv[coln * 4 + q],
                        srcw[512 * q:512 * (q + 1), 512 * coln:512 * (coln + 1)]
                        .rearrange("(kf p) i -> p kf i", p=128)), semn, 16)
            return 16

        @sec("gpsimd")
        def _(e):
            for j in range(NEXP):
                prep_gu(e, d_gwt[j], d_gw[j], f"wpJ{j}")
                prep_gu(e, d_uwt[j], d_uw[j], f"wpJ{j}")
                prep_dw(e, d_dwt[j], d_dw[j], f"wpJ{j}")
            for hh in range(2):
                prep_gu(e, d_sgwt[hh], d_sgw[:, hh * IFFN:(hh + 1) * IFFN], "wpJ8")
                prep_gu(e, d_suwt[hh], d_suw[:, hh * IFFN:(hh + 1) * IFFN], "wpJ8")
                prep_dw(e, d_sdwt[hh], d_sdw[hh * IFFN:(hh + 1) * IFFN, :], "wpJ8")
            for dst, srcw in ((d_Wvt, d_Wv), (d_Wkt, d_Wk), (d_Wkit, d_Wki),
                              (d_Wqt, d_Wq), (d_Wqit, d_Wqi), (d_Wot, d_Wo)):
                prep_att(e, dst, srcw, "wpA")
        for j in range(NEXP):
            wp_done[("exp", j)] = (f"wpJ{j}", 16 * 28)
        wp_done[("sh", 0)] = wp_done[("sh", 1)] = ("wpJ8", 16 * 56)
        wpA_done = c.val("wpA")

        for kf in range(KF):
            bank = pb[4 + (kf % 2)]
            prev_v = c.val("vS")

            @sec("tensor")
            def _(e, kf=kf, bank=bank, prev_v=prev_v):
                if kf == 0:
                    c.wait(e, "ldS", lds_done)
                c.wait(e, "vS", prev_v)
                c.inc(e.transpose(bank[:, 0:8], s_xn_tm[0:8, 0, kf * 128:(kf + 1) * 128],
                                  s_idf[0:8, 0:8]), "tS")
            tdone = c.val("tS")

            @sec("vector")
            def _(e, kf=kf, bank=bank, tdone=tdone):
                c.wait(e, "tS", tdone)
                c.inc(e.tensor_copy(s_rwTf[:, kf, :], bank[:, 0:8]), "vS")
        rwT_done = c.val("vS")

        def layernorm_to_xnT(xnT, sq_junk, xwait=None, with_logits=False, s_stage=None,
                             spill=None):
            @sec("scalar")
            def _(e):
                c.wait(e, "ldS", lds_done)
                if xwait:
                    c.wait(e, xwait[0], xwait[1])
                c.wait(e, "vS", c.val("vS"))
                for tt in range(TT):
                    c.wait(e, "aS", c.val("aS"))
                    c.inc(e.activation(sq_junk, s_x[:, tt, :], AF.Square,
                                       accum_out=s_stats[:, 1, tt:tt + 1]), "aS")
            sq_done = c.val("aS")

            @sec("vector")
            def _(e):
                c.wait(e, "ldS", lds_done)
                if xwait:
                    c.wait(e, xwait[0], xwait[1])
                for tt in range(TT):
                    c.inc(e.reduce_sum(s_stats[:, 0, tt:tt + 1], s_x[:, tt, :], axis=AX.X), "vS")
                c.wait(e, "vS")
                c.wait(e, "aS", sq_done)
                c.inc(e.tensor_scalar(s_stats[:, 2, :], s_stats[:, 0, :], 1.0 / H, None, op0=OP.mult), "vS")
                c.wait(e, "vS")
                c.inc(e.tensor_tensor(s_stats[:, 3, :], s_stats[:, 2, :], s_stats[:, 2, :], op=OP.mult), "vS")
                c.wait(e, "vS")
                c.inc(e.scalar_tensor_tensor(s_stats[:, 3, :], s_stats[:, 1, :], 1.0 / H,
                                             s_stats[:, 3, :], op0=OP.mult, op1=OP.subtract), "vS")
                c.wait(e, "vS")
                c.inc(e.tensor_scalar(s_stats[:, 3, :], s_stats[:, 3, :], EPS, None, op0=OP.add), "vS")
            varpe_done = c.val("vS")

            @sec("scalar")
            def _(e):
                c.wait(e, "vS", varpe_done)
                c.inc(e.activation(s_stats[:, 4, :], s_stats[:, 3, :], AF.Ln), "aS")
                c.wait(e, "aS")
                c.inc(e.activation(s_stats[:, 5, :], s_stats[:, 4, :], AF.Exp, scale=-0.5), "aS")
            rstd_done = c.val("aS")

            for tt in range(TT):
                prev_t = c.val("tS")

                @sec("vector")
                def _(e, tt=tt, prev_t=prev_t):
                    c.wait(e, "aS", rstd_done)
                    c.wait(e, "tS", prev_t)
                    c.inc(e.tensor_scalar(s_xn_tm[:, 0, :], s_x[:, tt, :],
                                          s_stats[:, 2, tt:tt + 1], s_stats[:, 5, tt:tt + 1],
                                          op0=OP.subtract, op1=OP.mult), "vS")
                xn_ready = c.val("vS")

                if spill is not None:
                    s_xgs = spill
                    sn = f"xsS{tt % 2}"
                    prevs = c.val(sn)

                    @sec("vector")
                    def _(e, tt=tt, sn=sn, prevs=prevs, s_xgs=s_xgs):
                        c.wait(e, sn, prevs)
                        c.inc(e.tensor_copy(s_xgs[:, tt % 2, :], s_xn_tm[:, 0, :]), "vS")
                    castv = c.val("vS")

                    @sec("sync")
                    def _(e, tt=tt, sn=sn, castv=castv, s_xgs=s_xgs):
                        c.wait(e, "vS", castv)
                        c.inc(e.dma_start(d_xng[tt * 128:(tt + 1) * 128, :],
                                          s_xgs[:, tt % 2, :]), sn, 16)

                for half in range(2):
                    bank = pb[4 + half]
                    for grp in range(2):
                        prev_v = c.val("vS")

                        @sec("tensor")
                        def _(e, half=half, grp=grp, bank=bank, w=max(xn_ready, prev_v)):
                            c.wait(e, "vS", w)
                            for j in range(4):
                                kf = half * 8 + grp * 4 + j
                                c.inc(e.transpose(bank[:, j * 128:(j + 1) * 128],
                                                  s_xn_tm[:, 0, kf * 128:(kf + 1) * 128],
                                                  s_idf[:]), "tS")
                        tdone = c.val("tS")

                        @sec("vector")
                        def _(e, tt=tt, half=half, grp=grp, bank=bank, tdone=tdone,
                              with_logits=with_logits):
                            c.wait(e, "tS", tdone)
                            for j in range(4):
                                kf = half * 8 + grp * 4 + j
                                c.inc(e.tensor_copy(xnT[:, kf, tt * 128:(tt + 1) * 128],
                                                    bank[:, j * 128:(j + 1) * 128]), "vS")
                            if with_logits:
                                for j in range(4):
                                    c.inc(e.tensor_copy(s_stage[:, j, :],
                                                        bank[:, j * 128:(j + 1) * 128]), "vS")
                        vdone = c.val("vS")

                        if with_logits:
                            @sec("tensor")
                            def _(e, half=half, grp=grp, vdone=vdone):
                                c.wait(e, "vS", vdone)
                                for j in range(4):
                                    kf = half * 8 + grp * 4 + j
                                    c.inc(e.matmul(pb[6][:, 0:8], s_stage[:, j, :],
                                                   s_rwTf[:, kf, :],
                                                   start=(kf == 0), stop=(kf == KF - 1)), "tS")
                if with_logits:
                    lg_mm_done = c.val("tS")

                    @sec("vector")
                    def _(e, tt=tt, lg_mm_done=lg_mm_done):
                        c.wait(e, "tS", lg_mm_done)
                        c.inc(e.tensor_copy(s_lgtm[:, tt, :], pb[6][:, 0:8]), "vS")
            return c.val("vS")

        def top2():
            @sec("vector")
            def _(e):
                c.wait(e, "vS", c.val("vS"))
                for tt in range(TT):
                    c.inc(e.reduce_max(s_r2[:, 0, tt:tt + 1], s_lgtm[:, tt, :], axis=AX.X), "vS")
                c.wait(e, "vS")
                for tt in range(TT):
                    c.inc(e.tensor_scalar(s_ce[:, tt, :], s_lgtm[:, tt, :],
                                          s_r2[:, 0, tt:tt + 1], -BIG, op0=OP.is_ge, op1=OP.mult), "vS")
                c.wait(e, "vS")
                for tt in range(TT):
                    c.inc(e.tensor_tensor(s_ce[:, tt, :], s_lgtm[:, tt, :], s_ce[:, tt, :], op=OP.add), "vS")
                c.wait(e, "vS")
                for tt in range(TT):
                    c.inc(e.reduce_max(s_r2[:, 1, tt:tt + 1], s_ce[:, tt, :], axis=AX.X), "vS")
                c.wait(e, "vS")
                c.inc(e.tensor_tensor(s_r2[:, 2, :], s_r2[:, 1, :], s_r2[:, 0, :], op=OP.subtract), "vS")
            d_done = c.val("vS")

            @sec("scalar")
            def _(e):
                c.wait(e, "vS", d_done)
                c.inc(e.activation(s_r2[:, 3, :], s_r2[:, 2, :], AF.Exp), "aS")
            ed_done = c.val("aS")

            @sec("vector")
            def _(e):
                c.wait(e, "aS", ed_done)
                c.inc(e.tensor_scalar(s_r2[:, 4, :], s_r2[:, 3, :], 1.0, None, op0=OP.add), "vS")
                c.wait(e, "vS")
                c.inc(e.reciprocal(s_r2[:, 4, :], s_r2[:, 4, :]), "vS")
                c.wait(e, "vS")
                c.inc(e.tensor_scalar(s_r2[:, 4, :], s_r2[:, 4, :], 0.5, None, op0=OP.mult), "vS")
                c.wait(e, "vS")
                c.inc(e.tensor_tensor(s_r2[:, 5, :], s_r2[:, 3, :], s_r2[:, 4, :], op=OP.mult), "vS")
                c.wait(e, "vS")
                for tt in range(TT):
                    c.inc(e.tensor_scalar(s_ce[:, tt, :], s_lgtm[:, tt, :],
                                          s_r2[:, 0, tt:tt + 1], s_r2[:, 4, tt:tt + 1],
                                          op0=OP.is_equal, op1=OP.mult), "vS")
                c.wait(e, "vS")
                for tt in range(TT):
                    c.inc(e.tensor_scalar(s_lgtm[:, tt, :], s_lgtm[:, tt, :],
                                          s_r2[:, 1, tt:tt + 1], s_r2[:, 5, tt:tt + 1],
                                          op0=OP.is_equal, op1=OP.mult), "vS")
                c.wait(e, "vS")
                for tt in range(TT):
                    c.inc(e.tensor_tensor(s_ce[:, tt, :], s_ce[:, tt, :], s_lgtm[:, tt, :], op=OP.add), "vS")
            return c.val("vS")

        def moe(fb_, ce_done, xnT_done):
            xnT = fb_["xnT"]
            s_wu, s_wd, s_act = fb_["wu"], fb_["wd"], fb_["act"]
            s_sgbb, s_subb, s_sdbb = fb_["sgb"], fb_["sub"], fb_["sdb"]
            bias_done = fb_["bias_done"]
            jobs = [("exp", i) for i in range(NEXP)] + [("sh", 0), ("sh", 1)]
            ru_free = {i: 0 for i in range(5)}
            ru_load = {}
            wd_free = [0, 0]
            wd_load = {}
            psets = [(pb[0], pb[1]), (pb[2], pb[3]), (pb[4], pb[5])]
            bank_free = [c.val("vS")] * 6
            sweep_i = 0
            hidx = 0

            def gu_src(kind, idx, mat, ibp, q):
                w = IBW[ibp]
                if kind == "exp":
                    t = d_gwt if mat == "g" else d_uwt
                    return t[idx, ibp * 4 + q, :, 0:4 * w]
                t = d_sgwt if mat == "g" else d_suwt
                return t[idx, ibp * 4 + q, :, 0:4 * w]

            def dw_src(kind, idx, fbk):
                t = d_dwt if kind == "exp" else d_sdwt
                return t[idx, fbk].rearrange("p (ic i) -> p ic i", ic=11)

            for kind, idx in jobs:
                if kind == "sh" and idx == 1:
                    prev_bias_mm = c.val("tS")

                    @sec("gpsimd")
                    def _(e, prev_bias_mm=prev_bias_mm, fb_=fb_):
                        c.wait(e, "tS", prev_bias_mm)
                        c.inc(e.dma_start(fb_["sgb"][:], d_sgb[0:1, IFFN:SI]), "bbS", 16)
                        c.inc(e.dma_start(fb_["sub"][:], d_sub[0:1, IFFN:SI]), "bbS", 16)
                    bias_done = c.val("bbS")
                for ibp in range(3):
                    w = IBW[ibp]
                    c0 = IBOFF[ibp]
                    my_h = [hidx + i for i in range(4)]
                    hidx += 4

                    @sec("sync")
                    def _(e, kind=kind, idx=idx, ibp=ibp, my_h=my_h,
                          gfree=[RGfree[hh % 5] for hh in my_h]):
                        c.wait(e, *wp_done[(kind, idx)])
                        for i, hh in enumerate(my_h):
                            buf = hh % 5
                            c.wait(e, "tS", gfree[i])
                            w4 = 4 * IBW[ibp]
                            c.inc(e.dma_start(
                                s_wg.rearrange("p b kf i -> p b (kf i)")[:, buf, 0:w4],
                                gu_src(kind, idx, "g", ibp, i)), f"wgS{buf}", 16)
                            RGload[("m", hh)] = (f"wgS{buf}", c.val(f"wgS{buf}"))

                    @sec("scalar")
                    def _(e, kind=kind, idx=idx, ibp=ibp, my_h=my_h,
                          ufree=[ru_free[hh % 5] for hh in my_h]):
                        c.wait(e, *wp_done[(kind, idx)])
                        for i, hh in enumerate(my_h):
                            buf = hh % 5
                            c.wait(e, "tS", ufree[i])
                            w4 = 4 * IBW[ibp]
                            c.inc(e.dma_start(
                                s_wu.rearrange("p b kf i -> p b (kf i)")[:, buf, 0:w4],
                                gu_src(kind, idx, "u", ibp, i)), f"wuS{buf}", 16)
                            ru_load[hh] = (f"wuS{buf}", c.val(f"wuS{buf}"))

                    for tt in range(TT):
                        si = sweep_i % 3
                        sweep_i += 1
                        pg, pu = psets[si]
                        pfree = max(bank_free[2 * si], bank_free[2 * si + 1])

                        @sec("tensor")
                        def _(e, kind=kind, idx=idx, tt=tt, w=w, c0=c0, pg=pg, pu=pu,
                              pfree=pfree, my_h=my_h):
                            c.wait(e, "vS", max(pfree, ce_done, xnT_done))
                            for kf in range(KF):
                                hh = my_h[kf // 4]
                                buf = hh % 5
                                if kf % 4 == 0:
                                    c.wait(e, *RGload[("m", hh)])
                                    c.wait(e, *ru_load[hh])
                                last = (kf == KF - 1) and (kind == "exp")
                                wgv = s_wg.rearrange("p b kf i -> p b (kf i)")
                                wuv = s_wu.rearrange("p b kf i -> p b (kf i)")
                                o0 = (kf % 4) * w
                                c.inc(e.matmul(pg[:, 0:w], xnT[:, kf, tt * 128:(tt + 1) * 128],
                                               wgv[:, buf, o0:o0 + w],
                                               start=(kf == 0), stop=last), "tS")
                                c.inc(e.matmul(pu[:, 0:w], xnT[:, kf, tt * 128:(tt + 1) * 128],
                                               wuv[:, buf, o0:o0 + w],
                                               start=(kf == 0), stop=last), "tS")
                                if kf % 4 == 3:
                                    RGfree[buf] = c.val("tS") - 1
                                    ru_free[buf] = c.val("tS")
                            if kind == "sh":
                                c.wait(e, "bbS", bias_done)
                                c.inc(e.matmul(pg[:, 0:w], s_onesbf[0:1, :],
                                               s_sgbb[0:1, c0:c0 + w],
                                               start=False, stop=True), "tS")
                                c.inc(e.matmul(pu[:, 0:w], s_onesbf[0:1, :],
                                               s_subb[0:1, c0:c0 + w],
                                               start=False, stop=True), "tS")
                        mm_done = c.val("tS")

                        prev_stt = c.val("vS")

                        @sec("scalar")
                        def _(e, tt=tt, w=w, pg=pg, mm_done=mm_done, prev_stt=prev_stt):
                            c.wait(e, "tS", mm_done)
                            c.wait(e, "vS", prev_stt)
                            c.inc(e.activation(s_sg[:, tt % 2, 0:w], pg[:, 0:w], AF.Silu), "aS")
                        silu_done = c.val("aS")

                        @sec("vector")
                        def _(e, kind=kind, idx=idx, tt=tt, w=w, c0=c0, pu=pu,
                              mm_done=mm_done, silu_done=silu_done):
                            c.wait(e, "tS", mm_done)
                            c.wait(e, "aS", silu_done)
                            scalar = s_ce[:, tt, idx:idx + 1] if kind == "exp" else 0.5
                            c.inc(e.scalar_tensor_tensor(s_act[:, tt, c0:c0 + w], pu[:, 0:w],
                                                         scalar, s_sg[:, tt % 2, 0:w],
                                                         op0=OP.mult, op1=OP.mult), "vS")
                        bank_free[2 * si] = bank_free[2 * si + 1] = c.val("vS")
                act_done = c.val("vS")

                for tt in range(TT):
                    for grp, gn in ((0, 4), (1, 4), (2, 3)):
                        prev_v = c.val("vS")

                        @sec("tensor")
                        def _(e, tt=tt, grp=grp, gn=gn, w=max(prev_v, act_done)):
                            c.wait(e, "vS", w)
                            for j in range(gn):
                                ic = grp * 4 + j
                                sl = ic % 8
                                c.inc(e.transpose(ptb[:, sl * 128:(sl + 1) * 128],
                                                  s_act[:, tt, ic * 128:(ic + 1) * 128],
                                                  s_idbf[:]), "tS")
                        tdone = c.val("tS")

                        @sec("vector")
                        def _(e, tt=tt, grp=grp, gn=gn, tdone=tdone):
                            c.wait(e, "tS", tdone)
                            for j in range(gn):
                                ic = grp * 4 + j
                                sl = ic % 8
                                c.inc(e.tensor_copy(s_act[:, tt, ic * 128:(ic + 1) * 128],
                                                    ptb[:, sl * 128:(sl + 1) * 128]), "vS")
                actT_done = c.val("vS")

                for fbk in range(4):
                    buf = fbk % 2

                    @sec("scalar")
                    def _(e, kind=kind, idx=idx, fbk=fbk, buf=buf, wfree=wd_free[buf]):
                        c.wait(e, *wp_done[(kind, idx)])
                        c.wait(e, "tS", wfree)
                        c.inc(e.dma_start(s_wd[:, buf, :, :], dw_src(kind, idx, fbk)), f"wdS{buf}", 16)
                        wd_load[fbk] = (f"wdS{buf}", c.val(f"wdS{buf}"))

                    for tt in range(TT):
                        bi = (fbk * TT + tt) % 6
                        bank = pb[bi]
                        bfree = bank_free[bi]

                        @sec("tensor")
                        def _(e, kind=kind, idx=idx, tt=tt, fbk=fbk, buf=buf, bank=bank,
                              bfree=bfree, actT_done=actT_done):
                            c.wait(e, "vS", max(bfree, actT_done))
                            c.wait(e, *wd_load[fbk])
                            add_bias = (kind == "sh" and idx == 1)
                            for ic in range(11):
                                last = (ic == 10) and not add_bias
                                c.inc(e.matmul(bank[:], s_act[:, tt, ic * 128:(ic + 1) * 128],
                                               s_wd[:, buf, ic, :],
                                               start=(ic == 0), stop=last), "tS")
                            if add_bias:
                                c.wait(e, "bbS", bias_done)
                                c.inc(e.matmul(bank[:], s_halves[0:1, :],
                                               s_sdbb[0:1, fbk * 512:(fbk + 1) * 512],
                                               start=False, stop=True), "tS")
                            if tt == TT - 1:
                                wd_free[buf] = c.val("tS")
                        dmm_done = c.val("tS")

                        @sec("vector")
                        def _(e, tt=tt, fbk=fbk, bank=bank, dmm_done=dmm_done):
                            c.wait(e, "tS", dmm_done)
                            c.inc(e.tensor_tensor(s_x[:, tt, fbk * 512:(fbk + 1) * 512],
                                                  bank[:], s_x[:, tt, fbk * 512:(fbk + 1) * 512],
                                                  op=OP.add), "vS")
                        bank_free[bi] = c.val("vS")
            return c.val("vS")

        def moe_sparse(fb_, ce_done, xnT_done):
            xnT = fb_["xnT"]
            s_wu, s_wd, s_act = fb_["wu"], fb_["wd"], fb_["act"]
            s_sgbb, s_subb, s_sdbb = fb_["sgb"], fb_["sub"], fb_["sdb"]
            s_ut, s_bind = fb_["ut"], fb_["bind"]
            s_xg, s_yg = fb_["xg"], fb_["yg"]
            s_zeroi, s_posi = fb_["zeroi"], fb_["posi"]
            s_sidx, s_sval = fb_["sidx"], fb_["sval"]
            s_svalf, s_iotap = fb_["svalf"], fb_["iotap"]
            s_tokc, s_cec = fb_["tokc"], fb_["cec"]
            s_widxg, s_widxd = fb_["widxg"], fb_["widxd"]
            stF = fb_["stage"].rearrange("p a b -> p (a b)")
            r2F = s_r2.rearrange("p a b -> p (a b)")
            lgF = s_lgtm.rearrange("p a b -> p (a b)")
            PREF = lambda a, b: stF[:, a:b]
            TB = lambda a, b: stF[:, 64 + a:64 + b]
            ZERO = stF[:, 128:128 + NSLOT]
            PAB = lambda j: stF[:, 152 + j:153 + j]
            PABf = stF[:, 152:168]
            CAB = lambda j: stF[:, 168 + j:169 + j]
            SCR = stF[:, 184:192]
            TOT = lambda a, b: stF[0:1, 192 + a:192 + b]
            TR2 = lambda a, b: stF[0:1, 272 + a:272 + b]
            SEROW = stF[0:1, 352:384]
            SIOTA = stF[0:1, 384:416]
            SCR32 = stF[0:1, 416:448]
            wgv = s_wg.rearrange("p b kf i -> p b (kf i)")
            wuv = s_wu.rearrange("p b kf i -> p b (kf i)")
            wdv = s_wd.rearrange("p b ic i -> p b (ic i)")
            d_gwf = d_gwt.rearrange("e b p r -> (e b p) r")
            d_uwf = d_uwt.rearrange("e b p r -> (e b p) r")
            d_dwf = d_dwt.rearrange("e f p r -> (e f p) r")
            bank_free = [c.val("vS")] * 6

            @sec("gpsimd")
            def _(e):
                c.wait(e, "vS", ce_done)
                c.inc(e.iota(s_iotap[:], [[1, 1]], base=0, channel_multiplier=1,
                             allow_small_or_imprecise_dtypes=True), "ioS")
                c.inc(e.iota(SIOTA, [[1, 32]], base=0, channel_multiplier=0,
                             allow_small_or_imprecise_dtypes=True), "ioS")
            iot_done = c.val("ioS")

            @sec("vector")
            def _(e):
                c.wait(e, "vS", ce_done)
                c.inc(e.memset(ZERO, 0.0), "vS")
                c.inc(e.memset(s_zeroi[:], 0), "vS")
                c64 = s_ce.rearrange("p a b -> p (a b)")
                i64 = s_bind.rearrange("p a b -> p (a b)")
                c.inc(e.tensor_scalar(i64, c64, 0.0, None, op0=OP.is_gt), "vS")
            ind_done = c.val("vS")

            @sec("gpsimd")
            def _(e):
                c.wait(e, "vS", ind_done)
                c.inc(e.dma_start(d_cev.rearrange("(p s) one -> p (s one)", p=128), ZERO), "bzS", 16)
                c.inc(e.dma_start(d_tok.rearrange("(p s) one -> p (s one)", p=128), s_zeroi[:]), "bzS", 16)
            zf_done = c.val("bzS")

            for tt in range(TT):
                bf = max(bank_free[tt % 2], bank_free[2 + tt % 2])

                @sec("tensor")
                def _(e, tt=tt, bf=bf, ut_done=fb_["ut_done"]):
                    c.wait(e, "ldS", ut_done)
                    c.wait(e, "vS", max(bf, ind_done))
                    c.inc(e.matmul(pb[tt % 2][:, 0:NEXP], s_ut[:], s_bind[:, tt, :],
                                   start=True, stop=True), "tS")
                    c.inc(e.matmul(pb[2 + tt % 2][0:1, 0:NEXP], s_ut[:, 127:128],
                                   s_bind[:, tt, :], start=True, stop=True), "tS")
                td = c.val("tS")

                @sec("vector")
                def _(e, tt=tt, td=td):
                    c.wait(e, "tS", td)
                    c.inc(e.tensor_copy(PREF(tt * 8, tt * 8 + 8), pb[tt % 2][:, 0:NEXP]), "vS")
                    c.inc(e.tensor_copy(TOT(tt * 8, tt * 8 + 8), pb[2 + tt % 2][0:1, 0:NEXP]), "vS")
                bank_free[tt % 2] = bank_free[2 + tt % 2] = c.val("vS")
            pref_done = c.val("vS")

            @sec("vector")
            def _(e):
                c.wait(e, "vS", pref_done)
                c.inc(e.tensor_copy(TR2(0, 64), TOT(0, 64)), "vS")
                c.wait(e, "vS")
                c.inc(e.tensor_tensor(TR2(8, 64), TR2(8, 64), TOT(0, 56), op=OP.add), "vS")
                c.wait(e, "vS")
                c.inc(e.tensor_copy(TOT(0, 64), TR2(0, 64)), "vS")
                c.wait(e, "vS")
                c.inc(e.tensor_tensor(TR2(16, 64), TR2(16, 64), TOT(0, 48), op=OP.add), "vS")
                c.wait(e, "vS")
                c.inc(e.tensor_copy(TOT(0, 64), TR2(0, 64)), "vS")
                c.wait(e, "vS")
                c.inc(e.tensor_tensor(TR2(32, 64), TR2(32, 64), TOT(0, 32), op=OP.add), "vS")
                c.wait(e, "vS")
                c.inc(e.memset(TOT(0, 8), 0.0), "vS")
                c.wait(e, "vS")
                c.inc(e.tensor_copy(TOT(8, 64), TR2(0, 56)), "vS")
                c.wait(e, "vS")
                c.inc(e.tensor_copy(TOT(64, 72), TR2(56, 64)), "vS")
                c.wait(e, "vS")
                c.inc(e.memset(TR2(72, 80), 0.0), "vS")
                for k in range(1, 9):
                    c.wait(e, "vS")
                    c.inc(e.tensor_scalar(TOT(72, 80), TOT(64, 72), k * 128 - 0.5, None,
                                          op0=OP.is_ge), "vS")
                    c.wait(e, "vS")
                    c.inc(e.tensor_tensor(TR2(72, 80), TR2(72, 80), TOT(72, 80), op=OP.add), "vS")
                c.wait(e, "vS")
                c.inc(e.tensor_scalar(TOT(72, 80), TR2(72, 80), 128.0, None, op0=OP.mult), "vS")
                c.wait(e, "vS")
                c.inc(e.tensor_tensor(TOT(72, 80), TOT(64, 72), TOT(72, 80), op=OP.is_gt), "vS")
                c.wait(e, "vS")
                c.inc(e.tensor_tensor(TR2(72, 80), TR2(72, 80), TOT(72, 80), op=OP.add), "vS")
                c.wait(e, "vS")
                c.inc(e.tensor_copy(TOT(72, 80), TR2(72, 80)), "vS")
                c.wait(e, "vS")
                c.inc(e.tensor_tensor(TOT(73, 80), TOT(73, 80), TR2(72, 79), op=OP.add), "vS")
                c.wait(e, "vS")
                c.inc(e.tensor_copy(TR2(72, 80), TOT(72, 80)), "vS")
                c.wait(e, "vS")
                c.inc(e.tensor_tensor(TOT(74, 80), TOT(74, 80), TR2(72, 78), op=OP.add), "vS")
                c.wait(e, "vS")
                c.inc(e.tensor_copy(TR2(72, 80), TOT(72, 80)), "vS")
                c.wait(e, "vS")
                c.inc(e.tensor_tensor(TOT(76, 80), TOT(76, 80), TR2(72, 76), op=OP.add), "vS")
                c.wait(e, "vS")
                c.inc(e.memset(TR2(72, 73), 0.0), "vS")
                c.wait(e, "vS")
                c.inc(e.tensor_copy(TR2(73, 80), TOT(72, 79)), "vS")
                c.wait(e, "vS")
                c.inc(e.tensor_scalar(TR2(72, 80), TR2(72, 80), 128.0, None, op0=OP.mult), "vS")
            offd_done = c.val("vS")

            @sec("vector")
            def _(e):
                c.wait(e, "vS", offd_done)
                c.wait(e, "ioS", iot_done)
                c.inc(e.memset(SEROW, 0.0), "vS")
                c.wait(e, "vS")
                c.inc(e.tensor_scalar(TR2(0, 8), TR2(72, 80), 1.0 / 128, None, op0=OP.mult), "vS")
                c.wait(e, "vS")
                for k in range(1, 8):
                    c.inc(e.tensor_scalar(SCR32, SIOTA, TR2(k, k + 1), None, op0=OP.is_ge), "vS")
                    c.wait(e, "vS")
                    c.inc(e.tensor_tensor(SEROW, SEROW, SCR32, op=OP.add), "vS")
                    c.wait(e, "vS")
            se_done = c.val("vS")

            bf01 = max(bank_free[0], bank_free[1])

            @sec("tensor")
            def _(e, bf01=bf01):
                c.wait(e, "vS", max(se_done, bf01))
                c.inc(e.matmul(pb[0][:, 0:8], s_onesf[:], TR2(72, 80), start=True, stop=True), "tS")
                c.inc(e.matmul(pb[1][:, 0:64], s_onesf[:], TOT(0, 64), start=True, stop=True), "tS")
            bc_done = c.val("tS")

            MBIG = 1.0e6

            @sec("vector")
            def _(e, bc_done=bc_done):
                c.wait(e, "tS", bc_done)
                c.inc(e.tensor_copy(SCR, pb[0][:, 0:8]), "vS")
                c.inc(e.tensor_copy(TB(0, 64), pb[1][:, 0:64]), "vS")
                c.wait(e, "vS")
                c.inc(e.tensor_tensor(r2F[:, 0:64], PREF(0, 64), TB(0, 64), op=OP.add), "vS")
                c.wait(e, "vS")
                for tt in range(TT):
                    c.inc(e.tensor_tensor(r2F[:, tt * 8:tt * 8 + 8],
                                          r2F[:, tt * 8:tt * 8 + 8], SCR, op=OP.add), "vS")
                c.wait(e, "vS")
                c.inc(e.tensor_scalar(r2F[:, 0:64], r2F[:, 0:64], 1.0, None, op0=OP.subtract), "vS")
                c.wait(e, "vS")
                for tt in range(TT):
                    iv = s_bind[:, tt, :]
                    pv = r2F[:, tt * 8:tt * 8 + 8]
                    c.inc(e.scalar_tensor_tensor(SCR, iv, MBIG, pv, op0=OP.mult, op1=OP.subtract), "vS")
                    c.wait(e, "vS")
                    c.inc(e.tensor_scalar(SCR, SCR, -1.0, MBIG, op0=OP.mult, op1=OP.add), "vS")
                    c.wait(e, "vS")
                    c.inc(e.tensor_reduce(PAB(tt * 2), SCR, axis=AX.X, op=OP.min), "vS")
                    c.wait(e, "vS")
                    c.inc(e.scalar_tensor_tensor(SCR, iv, MBIG, pv, op0=OP.mult, op1=OP.add), "vS")
                    c.wait(e, "vS")
                    c.inc(e.tensor_scalar(SCR, SCR, 1.0, -MBIG, op0=OP.mult, op1=OP.add), "vS")
                    c.wait(e, "vS")
                    c.inc(e.tensor_reduce(PAB(tt * 2 + 1), SCR, axis=AX.X, op=OP.max), "vS")
                    c.wait(e, "vS")
                    for k in range(2):
                        c.inc(e.tensor_scalar(SCR, pv, PAB(tt * 2 + k), None, op0=OP.is_equal), "vS")
                        c.wait(e, "vS")
                        c.inc(e.tensor_tensor(SCR, SCR, s_ce[:, tt, :], op=OP.mult), "vS")
                        c.wait(e, "vS")
                        c.inc(e.tensor_reduce(CAB(tt * 2 + k), SCR, axis=AX.X, op=OP.add), "vS")
                        c.wait(e, "vS")
            pos_done = c.val("vS")
            bank_free[0] = bank_free[1] = pos_done

            bf2 = bank_free[2]

            @sec("tensor")
            def _(e, bf2=bf2):
                c.wait(e, "vS", max(bf2, se_done))
                c.inc(e.matmul(pb[2][:, 0:32], s_onesf[:], SEROW, start=True, stop=True), "tS")
            seb_mm = c.val("tS")

            @sec("vector")
            def _(e, seb_mm=seb_mm):
                c.wait(e, "tS", seb_mm)
                c.wait(e, "ioS", iot_done)
                c.wait(e, "vS", pos_done)
                c.inc(e.tensor_copy(s_posi[:], PABf), "vS")
                c.wait(e, "vS")
                c.inc(e.tensor_copy(lgF[:, 0:32], pb[2][:, 0:32]), "vS")
                c.wait(e, "vS")
                c.inc(e.tensor_scalar(lgF[:, 32:32 + NSLOT], lgF[:, 0:NSLOT], 1536.0,
                                      s_iotap[:], op0=OP.mult, op1=OP.add), "vS")
                c.wait(e, "vS")
                c.inc(e.tensor_copy(s_widxg[:], lgF[:, 32:32 + NSLOT]), "vS")
                c.wait(e, "vS")
                c.inc(e.tensor_scalar(lgF[:, 32:32 + NSLOT], lgF[:, 0:NSLOT], 512.0,
                                      s_iotap[:], op0=OP.mult, op1=OP.add), "vS")
                c.wait(e, "vS")
                c.inc(e.tensor_copy(s_widxd[:], lgF[:, 32:32 + NSLOT]), "vS")
            widx_done = c.val("vS")
            bank_free[2] = widx_done

            for tt in range(TT):
                for k in range(2):
                    pvv = c.val("vS")
                    kprev = c.val("skS")

                    @sec("vector")
                    def _(e, tt=tt, k=k, pvv=pvv, kprev=kprev):
                        c.wait(e, "vS", max(pvv, pos_done))
                        c.wait(e, "skS", kprev)
                        c.wait(e, "ioS", iot_done)
                        c.inc(e.tensor_scalar(s_svalf[:], s_iotap[:], 1.0, float(tt * 128),
                                              op0=OP.mult, op1=OP.add), "vS")
                        c.wait(e, "vS")
                        c.inc(e.tensor_copy(s_sval[:], s_svalf[:]), "vS")
                        c.wait(e, "vS")
                        c.inc(e.tensor_copy(s_sidx[:], PAB(tt * 2 + k)), "vS")
                    spv = c.val("vS")

                    @sec("gpsimd")
                    def _(e, spv=spv):
                        c.wait(e, "vS", spv)
                        c.wait(e, "bzS", zf_done)
                        c.inc(e.indirect_dma_start(
                            d_tok[:], bass.IndirectOffsetOnAxis(ap=s_sidx[:], axis=0),
                            s_sval[:], None), "skS", 16)
                    sc1 = c.val("skS")

                    @sec("vector")
                    def _(e, sc1=sc1, tt=tt, k=k):
                        c.wait(e, "skS", sc1)
                        c.inc(e.tensor_copy(s_svalf[:], CAB(tt * 2 + k)), "vS")
                    sv2 = c.val("vS")

                    @sec("gpsimd")
                    def _(e, sv2=sv2):
                        c.wait(e, "vS", sv2)
                        c.inc(e.indirect_dma_start(
                            d_cev[:], bass.IndirectOffsetOnAxis(ap=s_sidx[:], axis=0),
                            s_svalf[:], None), "skS", 16)
            scat_done = c.val("skS")

            @sec("sync")
            def _(e):
                c.wait(e, "skS", scat_done)
                c.inc(e.dma_start(s_tokc[:], d_tok.rearrange("(s p) one -> p (s one)", p=128),
                                  allow_slow_non_contiguous=True), "tkS", 16)
                c.inc(e.dma_start(s_cec[:], d_cev.rearrange("(s p) one -> p (s one)", p=128),
                                  allow_slow_non_contiguous=True), "tkS", 16)
            tok_done = c.val("tkS")

            psets = [(pb[0], pb[1]), (pb[2], pb[3]), (pb[4], pb[5])]
            sweep_i = 0
            dn_i = 0
            ru_free = {i: 0 for i in range(4)}
            ru_load = {}
            wd_free = [0, 0]
            wd_load = {}
            xg_free = [0, 0]
            hh0 = [1000]
            uu0 = [0]
            wp_all = [(f"wpJ{j}", 16 * 28) for j in range(NEXP)]
            xs_final = (c.val("xsS0"), c.val("xsS1"))

            for s in range(NSLOT):
                xb = s % 2

                @sec("gpsimd")
                def _(e, s=s, xb=xb, xf=xg_free[xb]):
                    if s == 0:
                        for sn, v in wp_all:
                            c.wait(e, sn, v)
                        c.wait(e, "xsS0", xs_final[0])
                        c.wait(e, "xsS1", xs_final[1])
                    c.wait(e, "tkS", tok_done)
                    c.wait(e, "tS", xf)
                    c.inc(e.indirect_dma_start(
                        s_xg[:, xb, :], None, d_xng[:],
                        bass.IndirectOffsetOnAxis(ap=s_tokc[:, s:s + 1], axis=0)), f"gaS{xb}", 16)
                ga_ld = (f"gaS{xb}", c.val(f"gaS{xb}"))

                for grp in range(4):
                    prev_v = c.val("vS")

                    @sec("tensor")
                    def _(e, s=s, xb=xb, grp=grp, w=prev_v, ga_ld=ga_ld):
                        if grp == 0:
                            c.wait(e, *ga_ld)
                        c.wait(e, "vS", w)
                        for j in range(4):
                            kf = grp * 4 + j
                            sl = (grp % 2) * 4 + j
                            c.inc(e.transpose(ptb[:, sl * 128:(sl + 1) * 128],
                                              s_xg[:, xb, kf * 128:(kf + 1) * 128],
                                              s_idbf[:]), "tS")
                    tdone = c.val("tS")

                    @sec("vector")
                    def _(e, s=s, xb=xb, grp=grp, tdone=tdone):
                        c.wait(e, "tS", tdone)
                        for j in range(4):
                            kf = grp * 4 + j
                            sl = (grp % 2) * 4 + j
                            c.inc(e.tensor_copy(s_xg[:, xb, kf * 128:(kf + 1) * 128],
                                                ptb[:, sl * 128:(sl + 1) * 128]), "vS")
                xgT_done = c.val("vS")

                for ibp in range(3):
                    w = IBW[ibp]
                    c0 = IBOFF[ibp]
                    my_h = [hh0[0] + i for i in range(4)]
                    hh0[0] += 4
                    my_u = [uu0[0] + i for i in range(4)]
                    uu0[0] += 4

                    @sec("gpsimd")
                    def _(e, s=s, ibp=ibp, my_h=my_h, my_u=my_u,
                          gfree=[RGfree[hh % 5] for hh in my_h],
                          ufree=[ru_free[uu % 4] for uu in my_u]):
                        c.wait(e, "vS", widx_done)
                        w4 = 4 * IBW[ibp]
                        for i, hh in enumerate(my_h):
                            buf = hh % 5
                            c.wait(e, "tS", gfree[i])
                            c.inc(e.indirect_dma_start(
                                wgv[:, buf, 0:w4], None, d_gwf[:],
                                bass.IndirectOffsetOnAxis(ap=s_widxg[:, s:s + 1], axis=0),
                                element_offset=(ibp * 4 + i) * 128 * 2048), f"igS{buf}", 16)
                            RGload[("m", hh)] = (f"igS{buf}", c.val(f"igS{buf}"))
                        for i, uu in enumerate(my_u):
                            buf = uu % 4
                            c.wait(e, "tS", ufree[i])
                            c.inc(e.indirect_dma_start(
                                wuv[:, buf, 0:w4], None, d_uwf[:],
                                bass.IndirectOffsetOnAxis(ap=s_widxg[:, s:s + 1], axis=0),
                                element_offset=(ibp * 4 + i) * 128 * 2048), f"iuS{buf}", 16)
                            ru_load[uu] = (f"iuS{buf}", c.val(f"iuS{buf}"))

                    si = sweep_i % 3
                    sweep_i += 1
                    pg, pu = psets[si]
                    pfree = max(bank_free[2 * si], bank_free[2 * si + 1])

                    @sec("tensor")
                    def _(e, s=s, w=w, pg=pg, pu=pu, pfree=pfree, my_h=my_h, my_u=my_u,
                          xgT_done=xgT_done):
                        c.wait(e, "vS", max(pfree, xgT_done))
                        for kf in range(KF):
                            hh = my_h[kf // 4]
                            uu = my_u[kf // 4]
                            gbuf = hh % 5
                            ubuf = uu % 4
                            if kf % 4 == 0:
                                c.wait(e, *RGload[("m", hh)])
                                c.wait(e, *ru_load[uu])
                            o0 = (kf % 4) * w
                            c.inc(e.matmul(pg[:, 0:w], s_xg[:, s % 2, kf * 128:(kf + 1) * 128],
                                           wgv[:, gbuf, o0:o0 + w],
                                           start=(kf == 0), stop=(kf == KF - 1)), "tS")
                            c.inc(e.matmul(pu[:, 0:w], s_xg[:, s % 2, kf * 128:(kf + 1) * 128],
                                           wuv[:, ubuf, o0:o0 + w],
                                           start=(kf == 0), stop=(kf == KF - 1)), "tS")
                            if kf % 4 == 3:
                                RGfree[gbuf] = c.val("tS") - 1
                                ru_free[ubuf] = c.val("tS")
                    mm_done = c.val("tS")
                    if ibp == 2:
                        xg_free[s % 2] = mm_done

                    prev_stt = c.val("vS")

                    @sec("scalar")
                    def _(e, s=s, w=w, pg=pg, mm_done=mm_done, prev_stt=prev_stt):
                        c.wait(e, "tS", mm_done)
                        c.wait(e, "vS", prev_stt)
                        c.inc(e.activation(s_sg[:, s % 2, 0:w], pg[:, 0:w], AF.Silu), "aS")
                    silu_done = c.val("aS")

                    @sec("vector")
                    def _(e, s=s, w=w, c0=c0, pu=pu, mm_done=mm_done, silu_done=silu_done):
                        c.wait(e, "tS", mm_done)
                        c.wait(e, "aS", silu_done)
                        c.wait(e, "tkS", tok_done)
                        c.inc(e.scalar_tensor_tensor(s_act[:, s % 2, c0:c0 + w], pu[:, 0:w],
                                                     s_cec[:, s:s + 1], s_sg[:, s % 2, 0:w],
                                                     op0=OP.mult, op1=OP.mult), "vS")
                    bank_free[2 * si] = bank_free[2 * si + 1] = c.val("vS")
                act_done = c.val("vS")

                for grp, gn in ((0, 4), (1, 4), (2, 3)):
                    prev_v = c.val("vS")

                    @sec("tensor")
                    def _(e, s=s, grp=grp, gn=gn, w=max(prev_v, act_done)):
                        c.wait(e, "vS", w)
                        for j in range(gn):
                            ic = grp * 4 + j
                            sl = ic % 8
                            c.inc(e.transpose(ptb[:, sl * 128:(sl + 1) * 128],
                                              s_act[:, s % 2, ic * 128:(ic + 1) * 128],
                                              s_idbf[:]), "tS")
                    tdone = c.val("tS")

                    @sec("vector")
                    def _(e, s=s, grp=grp, gn=gn, tdone=tdone):
                        c.wait(e, "tS", tdone)
                        for j in range(gn):
                            ic = grp * 4 + j
                            sl = ic % 8
                            c.inc(e.tensor_copy(s_act[:, s % 2, ic * 128:(ic + 1) * 128],
                                                ptb[:, sl * 128:(sl + 1) * 128]), "vS")
                actT_done = c.val("vS")

                for fbk in range(4):
                    buf = fbk % 2

                    @sec("gpsimd")
                    def _(e, s=s, fbk=fbk, buf=buf, wfree=wd_free[buf]):
                        c.wait(e, "vS", widx_done)
                        c.wait(e, "tS", wfree)
                        c.inc(e.indirect_dma_start(
                            wdv[:, buf, :], None, d_dwf[:],
                            bass.IndirectOffsetOnAxis(ap=s_widxd[:, s:s + 1], axis=0),
                            element_offset=fbk * 128 * (11 * 512)), f"idS{buf}", 16)
                        wd_load[fbk] = (f"idS{buf}", c.val(f"idS{buf}"))

                    bi = dn_i % 6
                    dn_i += 1
                    bank = pb[bi]
                    bfree = bank_free[bi]

                    @sec("tensor")
                    def _(e, s=s, fbk=fbk, buf=buf, bank=bank, bfree=bfree,
                          actT_done=actT_done):
                        c.wait(e, "vS", max(bfree, actT_done))
                        c.wait(e, *wd_load[fbk])
                        for ic in range(11):
                            c.inc(e.matmul(bank[:], s_act[:, s % 2, ic * 128:(ic + 1) * 128],
                                           s_wd[:, buf, ic, :],
                                           start=(ic == 0), stop=(ic == 10)), "tS")
                        wd_free[buf] = c.val("tS")
                    dmm_done = c.val("tS")

                    yb = dn_i % 2
                    ysn = f"ysS{yb}"
                    ysp = c.val(ysn)

                    @sec("vector")
                    def _(e, bank=bank, dmm_done=dmm_done, ysn=ysn, ysp=ysp, yb=yb):
                        c.wait(e, "tS", dmm_done)
                        c.wait(e, ysn, ysp)
                        c.inc(e.tensor_copy(s_yg[:, yb, :], bank[:]), "vS")
                    cpv = c.val("vS")
                    bank_free[bi] = cpv

                    @sec("sync")
                    def _(e, s=s, fbk=fbk, yb=yb, cpv=cpv, ysn=ysn):
                        c.wait(e, "vS", cpv)
                        c.inc(e.dma_start(d_yasg[s * 128:(s + 1) * 128,
                                                 fbk * 512:(fbk + 1) * 512],
                                          s_yg[:, yb, :]), ysn, 16)

            ys_final = (c.val("ysS0"), c.val("ysS1"))
            ga_free = [c.val("vS")] * 2
            gcnt = [0]
            for tt in range(TT):
                for k in range(2):
                    for qc in range(4):
                        gb = gcnt[0] % 2
                        gcnt[0] += 1

                        @sec("gpsimd")
                        def _(e, tt=tt, k=k, qc=qc, gb=gb, gf=ga_free[gb],
                              first=(gcnt[0] == 1)):
                            if first:
                                c.wait(e, "ysS0", ys_final[0])
                                c.wait(e, "ysS1", ys_final[1])
                            c.wait(e, "vS", max(gf, widx_done))
                            c.inc(e.indirect_dma_start(
                                s_yg[:, gb, :], None, d_yasg[:],
                                bass.IndirectOffsetOnAxis(
                                    ap=s_posi[:, tt * 2 + k:tt * 2 + k + 1], axis=0),
                                element_offset=qc * 512), f"ygS{gb}", 16)
                        gld = (f"ygS{gb}", c.val(f"ygS{gb}"))

                        @sec("vector")
                        def _(e, tt=tt, qc=qc, gb=gb, gld=gld):
                            c.wait(e, *gld)
                            c.inc(e.tensor_tensor(s_x[:, tt, qc * 512:(qc + 1) * 512],
                                                  s_yg[:, gb, :],
                                                  s_x[:, tt, qc * 512:(qc + 1) * 512],
                                                  op=OP.add), "vS")
                        ga_free[gb] = c.val("vS")

            bias_mm_last = [0]
            sdb_mm_last = [0]
            for idx in range(2):
                for ibp in range(3):
                    w = IBW[ibp]
                    c0 = IBOFF[ibp]
                    my_h = [hh0[0] + i for i in range(4)]
                    hh0[0] += 4
                    my_u = [uu0[0] + i for i in range(4)]
                    uu0[0] += 4

                    @sec("gpsimd")
                    def _(e, idx=idx, c0=c0, w=w, pbm=bias_mm_last[0]):
                        c.wait(e, "tS", pbm)
                        c.inc(e.dma_start(s_sgbb[0:1, 0:w],
                                          d_sgb[0:1, idx * IFFN + c0:idx * IFFN + c0 + w]), "bbS", 16)
                        c.inc(e.dma_start(s_subb[0:1, 0:w],
                                          d_sub[0:1, idx * IFFN + c0:idx * IFFN + c0 + w]), "bbS", 16)
                    bias_done = c.val("bbS")

                    @sec("sync")
                    def _(e, idx=idx, ibp=ibp, my_h=my_h,
                          gfree=[RGfree[hh % 5] for hh in my_h]):
                        c.wait(e, "wpJ8", 16 * 56)
                        w4 = 4 * IBW[ibp]
                        for i, hh in enumerate(my_h):
                            buf = hh % 5
                            c.wait(e, "tS", gfree[i])
                            c.inc(e.dma_start(wgv[:, buf, 0:w4],
                                              d_sgwt[idx, ibp * 4 + i, :, 0:w4]), f"wgS{buf}", 16)
                            RGload[("m", hh)] = (f"wgS{buf}", c.val(f"wgS{buf}"))

                    @sec("scalar")
                    def _(e, idx=idx, ibp=ibp, my_u=my_u,
                          ufree=[ru_free[uu % 4] for uu in my_u]):
                        c.wait(e, "wpJ8", 16 * 56)
                        w4 = 4 * IBW[ibp]
                        for i, uu in enumerate(my_u):
                            buf = uu % 4
                            c.wait(e, "tS", ufree[i])
                            c.inc(e.dma_start(wuv[:, buf, 0:w4],
                                              d_suwt[idx, ibp * 4 + i, :, 0:w4]), f"wuS{buf}", 16)
                            ru_load[uu] = (f"wuS{buf}", c.val(f"wuS{buf}"))

                    for tt in range(TT):
                        si = sweep_i % 3
                        sweep_i += 1
                        pg, pu = psets[si]
                        pfree = max(bank_free[2 * si], bank_free[2 * si + 1])

                        @sec("tensor")
                        def _(e, tt=tt, w=w, pg=pg, pu=pu, pfree=pfree, my_h=my_h,
                              my_u=my_u, bias_done=bias_done, xnT_done=xnT_done):
                            c.wait(e, "vS", max(pfree, xnT_done))
                            for kf in range(KF):
                                hh = my_h[kf // 4]
                                uu = my_u[kf // 4]
                                gbuf = hh % 5
                                ubuf = uu % 4
                                if kf % 4 == 0:
                                    c.wait(e, *RGload[("m", hh)])
                                    c.wait(e, *ru_load[uu])
                                o0 = (kf % 4) * w
                                c.inc(e.matmul(pg[:, 0:w], xnT[:, kf, tt * 128:(tt + 1) * 128],
                                               wgv[:, gbuf, o0:o0 + w],
                                               start=(kf == 0), stop=False), "tS")
                                c.inc(e.matmul(pu[:, 0:w], xnT[:, kf, tt * 128:(tt + 1) * 128],
                                               wuv[:, ubuf, o0:o0 + w],
                                               start=(kf == 0), stop=False), "tS")
                                if kf % 4 == 3:
                                    RGfree[gbuf] = c.val("tS") - 1
                                    ru_free[ubuf] = c.val("tS")
                            c.wait(e, "bbS", bias_done)
                            c.inc(e.matmul(pg[:, 0:w], s_onesbf[0:1, :], s_sgbb[0:1, 0:w],
                                           start=False, stop=True), "tS")
                            c.inc(e.matmul(pu[:, 0:w], s_onesbf[0:1, :], s_subb[0:1, 0:w],
                                           start=False, stop=True), "tS")
                        mm_done = c.val("tS")
                        bias_mm_last[0] = mm_done

                        prev_stt = c.val("vS")

                        @sec("scalar")
                        def _(e, tt=tt, w=w, pg=pg, mm_done=mm_done, prev_stt=prev_stt):
                            c.wait(e, "tS", mm_done)
                            c.wait(e, "vS", prev_stt)
                            c.inc(e.activation(s_sg[:, tt % 2, 0:w], pg[:, 0:w], AF.Silu), "aS")
                        silu_done = c.val("aS")

                        @sec("vector")
                        def _(e, tt=tt, w=w, c0=c0, pu=pu, mm_done=mm_done,
                              silu_done=silu_done):
                            c.wait(e, "tS", mm_done)
                            c.wait(e, "aS", silu_done)
                            c.inc(e.scalar_tensor_tensor(s_act[:, tt, c0:c0 + w], pu[:, 0:w],
                                                         0.5, s_sg[:, tt % 2, 0:w],
                                                         op0=OP.mult, op1=OP.mult), "vS")
                        bank_free[2 * si] = bank_free[2 * si + 1] = c.val("vS")
                act_done = c.val("vS")

                for tt in range(TT):
                    for grp, gn in ((0, 4), (1, 4), (2, 3)):
                        prev_v = c.val("vS")

                        @sec("tensor")
                        def _(e, tt=tt, grp=grp, gn=gn, w=max(prev_v, act_done)):
                            c.wait(e, "vS", w)
                            for j in range(gn):
                                ic = grp * 4 + j
                                sl = ic % 8
                                c.inc(e.transpose(ptb[:, sl * 128:(sl + 1) * 128],
                                                  s_act[:, tt, ic * 128:(ic + 1) * 128],
                                                  s_idbf[:]), "tS")
                        tdone = c.val("tS")

                        @sec("vector")
                        def _(e, tt=tt, grp=grp, gn=gn, tdone=tdone):
                            c.wait(e, "tS", tdone)
                            for j in range(gn):
                                ic = grp * 4 + j
                                sl = ic % 8
                                c.inc(e.tensor_copy(s_act[:, tt, ic * 128:(ic + 1) * 128],
                                                    ptb[:, sl * 128:(sl + 1) * 128]), "vS")
                actT_done = c.val("vS")

                for fbk in range(4):
                    buf = fbk % 2
                    add_bias = (idx == 1)
                    if add_bias:
                        @sec("gpsimd")
                        def _(e, fbk=fbk, pbm=sdb_mm_last[0]):
                            c.wait(e, "tS", pbm)
                            c.inc(e.dma_start(s_sdbb[0:1, 0:512],
                                              d_sdb[0:1, fbk * 512:(fbk + 1) * 512]), "bbS", 16)
                        sdb_done = c.val("bbS")

                    @sec("scalar")
                    def _(e, idx=idx, fbk=fbk, buf=buf, wfree=wd_free[buf]):
                        c.wait(e, "wpJ8", 16 * 56)
                        c.wait(e, "tS", wfree)
                        c.inc(e.dma_start(s_wd[:, buf, :, :],
                                          d_sdwt[idx, fbk].rearrange("p (ic i) -> p ic i", ic=11)),
                              f"wdS{buf}", 16)
                        wd_load[fbk] = (f"wdS{buf}", c.val(f"wdS{buf}"))

                    for tt in range(TT):
                        bi = dn_i % 6
                        dn_i += 1
                        bank = pb[bi]
                        bfree = bank_free[bi]

                        @sec("tensor")
                        def _(e, idx=idx, tt=tt, fbk=fbk, buf=buf, bank=bank, bfree=bfree,
                              actT_done=actT_done, add_bias=add_bias,
                              sdb_done=(c.val("bbS") if add_bias else 0)):
                            c.wait(e, "vS", max(bfree, actT_done))
                            c.wait(e, *wd_load[fbk])
                            for ic in range(11):
                                last = (ic == 10) and not add_bias
                                c.inc(e.matmul(bank[:], s_act[:, tt, ic * 128:(ic + 1) * 128],
                                               s_wd[:, buf, ic, :],
                                               start=(ic == 0), stop=last), "tS")
                            if add_bias:
                                c.wait(e, "bbS", sdb_done)
                                c.inc(e.matmul(bank[:], s_halves[0:1, :], s_sdbb[0:1, 0:512],
                                               start=False, stop=True), "tS")
                            if tt == TT - 1:
                                wd_free[buf] = c.val("tS")
                        dmm_done = c.val("tS")
                        if add_bias:
                            sdb_mm_last[0] = dmm_done

                        @sec("vector")
                        def _(e, tt=tt, fbk=fbk, bank=bank, dmm_done=dmm_done):
                            c.wait(e, "tS", dmm_done)
                            c.inc(e.tensor_tensor(s_x[:, tt, fbk * 512:(fbk + 1) * 512],
                                                  bank[:], s_x[:, tt, fbk * 512:(fbk + 1) * 512],
                                                  op=OP.add), "vS")
                        bank_free[bi] = c.val("vS")
            return c.val("vS")

        wincnt = [0]

        def open_ffn_window(st, sparse=False):
            wincnt[0] += 1
            sfx = wincnt[0]
            fb_ = {}
            fb_["sparse"] = sparse
            fb_["xnT"] = sb(st, f"s_xnT_{sfx}", [128, KF, TPC], BF16)
            fb_["wu"] = sb(st, f"s_wu_{sfx}", [128, 4 if sparse else 5, 4, 512], BF16)
            fb_["wd"] = sb(st, f"s_wd_{sfx}", [128, 2, 11, 512], BF16)
            fb_["act"] = sb(st, f"s_act_{sfx}", [128, TT, IFFN], BF16)
            if sparse:
                fb_["sgb"] = sb(st, f"s_sgbb_{sfx}", [1, 512], BF16)
                fb_["sub"] = sb(st, f"s_subb_{sfx}", [1, 512], BF16)
                fb_["sdb"] = sb(st, f"s_sdbb_{sfx}", [1, 512], BF16)
            else:
                fb_["sgb"] = sb(st, f"s_sgbb_{sfx}", [1, IFFN], BF16)
                fb_["sub"] = sb(st, f"s_subb_{sfx}", [1, IFFN], BF16)
                fb_["sdb"] = sb(st, f"s_sdbb_{sfx}", [1, H], BF16)
            fb_["stage"] = sb(st, f"s_stage_{sfx}", [128, 4, 128], F32)
            if sparse:
                fb_["ut"] = sb(st, f"s_ut_{sfx}", [128, 128], BF16)
                fb_["bind"] = sb(st, f"s_bind_{sfx}", [128, TT, NEXP], BF16)
                fb_["xg"] = sb(st, f"s_xg_{sfx}", [128, 2, H], BF16)
                fb_["yg"] = sb(st, f"s_yg_{sfx}", [128, 2, 512], F32)
                fb_["zeroi"] = sb(st, f"s_zeroi_{sfx}", [128, NSLOT], I32)
                fb_["posi"] = sb(st, f"s_posi_{sfx}", [128, 16], I32)
                fb_["sidx"] = sb(st, f"s_sidx_{sfx}", [128, 1], I32)
                fb_["sval"] = sb(st, f"s_sval_{sfx}", [128, 1], I32)
                fb_["svalf"] = sb(st, f"s_svalf_{sfx}", [128, 1], F32)
                fb_["iotap"] = sb(st, f"s_iotap_{sfx}", [128, 1], F32)
                fb_["tokc"] = sb(st, f"s_tokc_{sfx}", [128, NSLOT], I32)
                fb_["cec"] = sb(st, f"s_cec_{sfx}", [128, NSLOT], F32)
                fb_["widxg"] = sb(st, f"s_widxg_{sfx}", [128, NSLOT], I32)
                fb_["widxd"] = sb(st, f"s_widxd_{sfx}", [128, NSLOT], I32)

                @sec("sync")
                def _(e, fb_=fb_):
                    c.inc(e.dma_start(fb_["ut"][:], d_utri[:]), "ldS", 16)
                fb_["ut_done"] = c.val("ldS")
            else:
                @sec("gpsimd")
                def _(e, fb_=fb_):
                    c.inc(e.dma_start(fb_["sgb"][:], d_sgb[0:1, 0:IFFN]), "bbS", 16)
                    c.inc(e.dma_start(fb_["sub"][:], d_sub[0:1, 0:IFFN]), "bbS", 16)
                    c.inc(e.dma_start(fb_["sdb"][:], d_sdb[:]), "bbS", 16)
            fb_["bias_done"] = c.val("bbS")
            fb_["sq_junk"] = fb_["act"].rearrange("p a b -> p (a b)")[:, 0:H]
            return fb_

        w2 = ExitStack()
        fb1 = open_ffn_window(w2)
        xnT1_done = layernorm_to_xnT(fb1["xnT"], fb1["sq_junk"],
                                     xwait=("vS", rwT_done), with_logits=True,
                                     s_stage=fb1["stage"])
        ce_done = top2()
        x1_done = moe(fb1, ce_done, xnT1_done)
        if debug:
            @sec("sync")
            def _(e):
                c.wait(e, "vS", x1_done)
                c.inc(e.dma_start(dbg["x1"].rearrange("(tt p) f -> p tt f", p=128), s_x[:]), "outS", 16)

        fb1_xnT = fb1["xnT"]
        w2.close()
        soft_barrier()

        w3 = ExitStack()
        fm1 = sb(w3, "s_fm1", [128, KF, TPC], BF16)
        fm2 = sb(w3, "s_fm2", [128, KF, TPC], BF16)
        s_acc = sb(w3, "s_acc", [128, 64, 8], F32)
        s_accr = sb(w3, "s_accr", [128, 64], F32)
        s_hw64 = sb(w3, "s_hw64", [128, 64], F32)
        s_relc = sb(w3, "s_relc", [128, 8], F32)
        s_relrow = sb(w3, "s_relrow", [8, 128], F32)
        w1 = ExitStack()
        xnT2 = sb(w1, "s_xnT2", [128, KF, TPC], BF16)

        sqj2 = fm1.rearrange("p a b -> p (a b)")[:, 0:H]
        xnT2_done = layernorm_to_xnT(xnT2, sqj2, xwait=("vS", x1_done), with_logits=False)

        phidx = [120]

        def tm_sweep(name, wsrc, lhsT_buf, lhsT_done, evac_fn, nco=4, dst_wait=()):
            psets = [pb[0], pb[1], pb[2]]
            pfree = [c.val("vS")] * 3
            swp = [0]
            for coln in range(nco):
                my_h = [phidx[0] + i for i in range(4)]
                phidx[0] += 4

                @sec("sync")
                def _(e, coln=coln, my_h=my_h, gfree=[RGfree[hh % 5] for hh in my_h]):
                    c.wait(e, "wpA", wpA_done)
                    for i, hh in enumerate(my_h):
                        buf = hh % 5
                        c.wait(e, "tS", gfree[i])
                        c.inc(e.dma_start(
                            s_wg[:, buf, :, :],
                            wsrc[coln * 4 + i].rearrange("p (kf i) -> p kf i", kf=4)), f"wgS{buf}", 16)
                        RGload[("m", hh)] = (f"wgS{buf}", c.val(f"wgS{buf}"))

                for tt in range(TT):
                    si = swp[0] % 3
                    swp[0] += 1
                    pg = psets[si]
                    pf = pfree[si]

                    @sec("tensor")
                    def _(e, tt=tt, coln=coln, pg=pg, pf=pf, my_h=my_h, lhsT_done=lhsT_done):
                        c.wait(e, "vS", max(pf, lhsT_done))
                        for kf in range(KF):
                            hh = my_h[kf // 4]
                            buf = hh % 5
                            if kf % 4 == 0:
                                c.wait(e, *RGload[("m", hh)])
                            c.inc(e.matmul(pg[:], lhsT_buf[:, kf, tt * 128:(tt + 1) * 128],
                                           s_wg[:, buf, kf % 4, :],
                                           start=(kf == 0), stop=(kf == KF - 1)), "tS")
                            if kf % 4 == 3:
                                RGfree[buf] = c.val("tS")
                    mm_done = c.val("tS")

                    @sec("vector")
                    def _(e, tt=tt, coln=coln, pg=pg, mm_done=mm_done, dst_wait=dst_wait):
                        c.wait(e, "tS", mm_done)
                        for dw_ in dst_wait:
                            c.wait(e, dw_[0], dw_[1])
                        evac_fn(e, tt, coln, pg)
                    pfree[si] = c.val("vS")
            return c.val("vS")

        def tm_store(dst):
            dv = dst.rearrange("p a b -> p (a b)").rearrange("p (tt f) -> p tt f", tt=TT)

            def f(e, tt, coln, psum):
                c.inc(e.tensor_copy(dv[:, tt, coln * 512:(coln + 1) * 512], psum[:]), "vS")
            return f

        def transpose_tm_to_fm(src, dst, src_done, dst_wait=()):
            sv = src.rearrange("p a b -> p (a b)").rearrange("p (tt f) -> p tt f", tt=TT)
            for tt in range(TT):
                for grp in range(4):
                    prev_v = c.val("vS")

                    @sec("tensor")
                    def _(e, tt=tt, grp=grp, w=max(prev_v, src_done)):
                        c.wait(e, "vS", w)
                        for j in range(4):
                            kf = grp * 4 + j
                            sl = (grp % 2) * 4 + j
                            c.inc(e.transpose(ptb[:, sl * 128:(sl + 1) * 128],
                                              sv[:, tt, kf * 128:(kf + 1) * 128],
                                              s_idbf[:]), "tS")
                    tdone = c.val("tS")

                    @sec("vector")
                    def _(e, tt=tt, grp=grp, tdone=tdone, dst_wait=dst_wait):
                        c.wait(e, "tS", tdone)
                        for dw_ in dst_wait:
                            c.wait(e, dw_[0], dw_[1])
                        for j in range(4):
                            kf = grp * 4 + j
                            sl = (grp % 2) * 4 + j
                            c.inc(e.tensor_copy(dst[:, kf, tt * 128:(tt + 1) * 128],
                                                ptb[:, sl * 128:(sl + 1) * 128]), "vS")
            return c.val("vS")

        vtm_done = tm_sweep("v", d_Wvt, xnT2, xnT2_done, tm_store(fm1))
        vfm_done = transpose_tm_to_fm(fm1, fm2, vtm_done)

        @sec("sync")
        def _(e):
            c.wait(e, "vS", vfm_done)
            c.inc(e.dma_start(d_cc1i[4096:6144, :].rearrange("(kf p) t -> p kf t", p=128),
                              fm2[:]), "cbS", 16)
        v_sent = c.val("cbS")

        ktm_done = tm_sweep("k", d_Wkt, xnT2, xnT2_done, tm_store(fm1))
        kfm_done = transpose_tm_to_fm(fm1, fm2, ktm_done,
                                      dst_wait=(("cbS", v_sent),))

        @sec("sync")
        def _(e):
            c.wait(e, "vS", kfm_done)
            c.inc(e.dma_start(d_cc1i[0:2048, :].rearrange("(kf p) t -> p kf t", p=128),
                              fm2[:]), "cbS", 16)
        k_sent = c.val("cbS")

        kptm_done = tm_sweep("kp", d_Wkit, fm2, kfm_done, tm_store(fm1))
        kpfm_done = transpose_tm_to_fm(fm1, fm2, kptm_done,
                                       dst_wait=(("cbS", k_sent),))

        @sec("sync")
        def _(e):
            c.wait(e, "vS", kpfm_done)
            c.inc(e.dma_start(d_cc1i[2048:4096, :].rearrange("(kf p) t -> p kf t", p=128),
                              fm2[:]), "cbS", 16)
        kp_sent = c.val("cbS")

        qtm_done = tm_sweep("q", d_Wqt, xnT2, xnT2_done, tm_store(fm1))
        qfm_done = transpose_tm_to_fm(fm1, fm2, qtm_done,
                                      dst_wait=(("cbS", kp_sent),))

        @sec("sync")
        def _(e):
            c.wait(e, "vS", qfm_done)
            c.inc(e.dma_start(d_qsp.rearrange("(kf p) t -> p kf t", p=128), fm2[:]), "qsS", 16)
        q_sent = c.val("qsS")

        qptm_done = tm_sweep("qp", d_Wqit, fm2, qfm_done, tm_store(fm1))
        @sec("vector")
        def _(e):
            c.wait(e, "qsS", q_sent)
            c.inc(e.tensor_copy(s_stats[0:1, 0, 0:1], s_stats[0:1, 0, 0:1]), "vS")
        qp_fence = c.val("vS")
        qpfm_done = transpose_tm_to_fm(fm1, fm2, max(qptm_done, qp_fence))

        w1.close()

        @sec("gpsimd")
        def _(e):
            c.wait(e, "cbS", max(v_sent, k_sent, kp_sent))
            c.inc(e.collective_compute(
                "AllGather", OP.bypass,
                ins=[d_cc1i.ap().opt()], outs=[d_cc1o.ap().opt()],
                replica_groups=[[0, 1, 2, 3], [4, 5, 6, 7]]), "ccS")
        cc1_done = c.val("ccS")

        @sec("tensor")
        def _(e):
            c.wait(e, "ldS", lds_done)
            c.wait(e, "vS", c.val("vS"))
            c.inc(e.matmul(pb[6][:, 0:8], s_onesf[:], s_hw[:], start=True, stop=True), "tS")
        hwb_mm = c.val("tS")

        @sec("vector")
        def _(e, hwb_mm=hwb_mm):
            c.wait(e, "tS", hwb_mm)
            for qt in range(TT):
                c.inc(e.tensor_copy(s_hw64[:, qt * 8:(qt + 1) * 8], pb[6][:, 0:8]), "vS")
        hw64_done = c.val("vS")

        ind_junk = fm1.rearrange("p a b -> p (a b)")
        pfree_ind = [("vS", 0)] * 4
        ev_i = [0]
        for ih in range(NIH):
            for kb in range(8):
                my_h = [phidx[0], phidx[0] + 1]
                phidx[0] += 2
                rr, tl0 = kb // 2, (kb % 2) * 512

                @sec("sync")
                def _(e, ih=ih, kb=kb, rr=rr, tl0=tl0, my_h=my_h,
                      gfree=[RGfree[hh % 5] for hh in my_h], cc1_done=cc1_done):
                    c.wait(e, "ccS", cc1_done)
                    for i, hh in enumerate(my_h):
                        buf = hh % 5
                        c.wait(e, "tS", gfree[i])
                        c.inc(e.dma_start(
                            s_wg[:, buf, 0, :],
                            d_cc1o[rr * CC1R + 2048 + ih * 256 + i * 128:
                                   rr * CC1R + 2048 + ih * 256 + i * 128 + 128,
                                   tl0:tl0 + 512]), f"wgS{buf}", 16)
                        RGload[("m", hh)] = (f"wgS{buf}", c.val(f"wgS{buf}"))

                for qt in range(TT):
                    bi = ev_i[0] % 4
                    bank = pb[bi]
                    pf = pfree_ind[bi]

                    @sec("tensor")
                    def _(e, ih=ih, kb=kb, qt=qt, bank=bank, pf=pf, my_h=my_h,
                          qpfm_done=qpfm_done):
                        if qt == 0:
                            c.wait(e, *RGload[("m", my_h[0])])
                            c.wait(e, *RGload[("m", my_h[1])])
                            c.wait(e, "vS", qpfm_done)
                        c.wait(e, pf[0], pf[1])
                        for i, hh in enumerate(my_h):
                            buf = hh % 5
                            c.inc(e.matmul(bank[:], fm2[:, ih * 2 + i, qt * 128:(qt + 1) * 128],
                                           s_wg[:, buf, 0, :],
                                           start=(i == 0), stop=(i == 1)), "tS")
                        if qt == TT - 1:
                            for hh in my_h:
                                RGfree[hh % 5] = c.val("tS")
                    imm_done = c.val("tS")

                    acc_ap = s_acc[:, qt * 8 + ih, kb:kb + 1]
                    if ev_i[0] % 2 == 0:
                        @sec("scalar")
                        def _(e, bank=bank, acc_ap=acc_ap, imm_done=imm_done):
                            c.wait(e, "tS", imm_done)
                            c.wait(e, "aS", c.val("aS"))
                            c.inc(e.activation(ind_junk[:, 0:512], bank[:], AF.Relu,
                                               accum_out=acc_ap), "aS")
                        pfree_ind[bi] = ("aS", c.val("aS"))
                    else:
                        @sec("vector")
                        def _(e, bank=bank, acc_ap=acc_ap, imm_done=imm_done):
                            c.wait(e, "tS", imm_done)
                            c.wait(e, "vS", c.val("vS"))
                            c.inc(e.tensor_scalar(ind_junk[:, 512:1024], bank[:], 0.0, 0.0,
                                                  op0=OP.max, op1=OP.add, accum_out=acc_ap), "vS")
                        pfree_ind[bi] = ("vS", c.val("vS"))
                    ev_i[0] += 1

        ind_mm_all = c.val("tS")
        ind_ev_a = c.val("aS")
        ind_ev_v = c.val("vS")

        @sec("vector")
        def _(e):
            c.wait(e, "aS", ind_ev_a)
            c.wait(e, "vS", ind_ev_v)
            c.wait(e, "vS", hw64_done)
            c.inc(e.tensor_reduce(s_accr[:], s_acc[:], axis=AX.X, op=OP.add), "vS")
            c.wait(e, "vS")
            c.inc(e.tensor_tensor(s_accr[:], s_accr[:], s_hw64[:], op=OP.mult), "vS")
            c.wait(e, "vS")
            c.inc(e.tensor_reduce(s_relc[:], s_accr.rearrange("p (qt ih) -> p qt ih", qt=TT),
                                  axis=AX.X, op=OP.add), "vS")
        relc_raw = c.val("vS")

        @sec("scalar")
        def _(e):
            c.wait(e, "ldS", lds_done)
            c.inc(e.activation(s_temp[:], s_temp[:], AF.Exp, scale=-1.0), "aS")
        et_done = c.val("aS")

        @sec("tensor")
        def _(e, et_done=et_done):
            c.wait(e, "aS", et_done)
            c.wait(e, "vS", c.val("vS"))
            c.inc(e.matmul(pb[5][:, 0:1], s_onesf[:], s_temp[:], start=True, stop=True), "tS")
        etb_mm = c.val("tS")

        @sec("vector")
        def _(e, etb_mm=etb_mm):
            c.wait(e, "tS", etb_mm)
            c.inc(e.tensor_copy(s_smallc[:, 0:1], pb[5][:, 0:1]), "vS")
            c.wait(e, "vS")
            c.inc(e.tensor_scalar(s_relc[:], s_relc[:], s_smallc[:, 0:1], None, op0=OP.mult), "vS")
        relc_done = c.val("vS")

        @sec("tensor")
        def _(e, relc_done=relc_done):
            c.wait(e, "vS", relc_done)
            c.inc(e.transpose(pb[6][0:8, 0:128], s_relc[:], s_idf[:]), "tS")
        relT_mm = c.val("tS")

        @sec("vector")
        def _(e, relT_mm=relT_mm):
            c.wait(e, "tS", relT_mm)
            c.inc(e.tensor_copy(s_relrow[:], pb[6][0:8, 0:128]), "vS")
        relrow_done = c.val("vS")

        @sec("sync")
        def _(e, relrow_done=relrow_done):
            c.wait(e, "vS", relrow_done)
            c.inc(e.dma_start(d_cc2i[0:1, :].rearrange("one (qt q) -> qt (one q)", qt=TT),
                              s_relrow[:]), "rlS", 16)
        rel_sent = c.val("rlS")

        @sec("gpsimd")
        def _(e, rel_sent=rel_sent):
            c.wait(e, "rlS", rel_sent)
            c.inc(e.collective_compute(
                "AllGather", OP.bypass,
                ins=[d_cc2i.ap().opt()], outs=[d_cc2o.ap().opt()],
                replica_groups=[[0, 1, 2, 3], [4, 5, 6, 7]]), "ccS")
        cc2_done = c.val("ccS")

        if debug:
            @sec("sync")
            def _(e, cc2_done=cc2_done):
                c.wait(e, "ccS", cc2_done)
                c.inc(e.dma_start(dbg["rel"][:], d_cc2o[:]), "outS", 16)

        w3.close()
        soft_barrier()

        w4 = ExitStack()
        s_rel4 = [sb(w4, f"s_rel4_{r}", [1, TPC], F32) for r in range(4)]
        s_relb = sb(w4, "s_relb", [128, 4096], F32)
        s_bjunk = sb(w4, "s_bjunk", [128, 4096], BF16)
        s_iot1 = sb(w4, "s_iot1", [1, 128], F32)
        s_trow = sb(w4, "s_trow", [1, 128], F32)
        s_crow = sb(w4, "s_crow", [1, 128], F32)
        s_brw = sb(w4, "s_brw", [1, 128], mybir.dt.uint8)
        s_brw2 = sb(w4, "s_brw2", [1, 128], mybir.dt.uint8)
        s_neg = sb(w4, "s_neg", [1, 128], F32)
        s_pos = sb(w4, "s_pos", [1, 128], F32)

        @sec("vector")
        def _(e):
            c.wait(e, "vS", c.val("vS"))
            c.inc(e.memset(s_neg[:], -3.0e38), "vS")
            c.inc(e.memset(s_pos[:], 3.0e38), "vS")

        @sec("sync")
        def _(e, cc2_done=cc2_done):
            c.wait(e, "ccS", cc2_done)
            for r in range(4):
                c.inc(e.dma_start(s_rel4[r][:], d_cc2o[r:r + 1, :]), "r4S", 16)
        rel4_done = c.val("r4S")

        @sec("gpsimd")
        def _(e):
            c.inc(e.iota(s_iot1[:], [[1, 128]], base=0, channel_multiplier=0,
                         allow_small_or_imprecise_dtypes=True), "ioS")
        iot1_done = c.val("ioS")

        for rseg in range(8):
            r, i0 = rseg // 2, (rseg % 2) * 512
            bank = pb[rseg % 2]
            prev_v = c.val("vS")

            @sec("tensor")
            def _(e, r=r, i0=i0, bank=bank, prev_v=prev_v, rel4_done=rel4_done):
                c.wait(e, "r4S", rel4_done)
                c.wait(e, "vS", prev_v)
                c.inc(e.matmul(bank[:], s_onesf[:], s_rel4[r][0:1, i0:i0 + 512],
                               start=True, stop=True), "tS")
            mmd = c.val("tS")

            @sec("vector")
            def _(e, rseg=rseg, bank=bank, mmd=mmd):
                c.wait(e, "tS", mmd)
                c.inc(e.tensor_copy(s_relb[:, rseg * 512:(rseg + 1) * 512], bank[:]), "vS")
        relb_done = c.val("vS")

        for g in range(NKT):
            bank = pb[2 + (g % 2)]
            prev_v = c.val("vS")

            @sec("tensor")
            def _(e, g=g, bank=bank, prev_v=prev_v, relb_done=relb_done):
                c.wait(e, "vS", max(prev_v, relb_done))
                c.inc(e.transpose(bank[:, 0:128], s_relb[:, g * 128:(g + 1) * 128], s_idf[:]), "tS")
            mmd = c.val("tS")

            @sec("vector")
            def _(e, g=g, bank=bank, mmd=mmd):
                c.wait(e, "tS", mmd)
                c.inc(e.tensor_copy(s_relpc[:, g:g + 1], bank[:, 0:1]), "vS")
        relpc_done = c.val("vS")

        @sec("vector")
        def _(e, relb_done=relb_done):
            c.wait(e, "vS", relb_done)
            c.inc(e.tensor_reduce(s_bis[0:1, 0:1], s_relb[0:1, :], axis=AX.X, op=OP.min), "vS")
            c.inc(e.tensor_reduce(s_bis[0:1, 1:2], s_relb[0:1, :], axis=AX.X, op=OP.max), "vS")
        bis_init = c.val("vS")

        for rnd in range(NROUND):
            @sec("vector")
            def _(e, rnd=rnd):
                c.wait(e, "vS", c.val("vS"))
                if rnd == 0:
                    c.wait(e, "ioS", iot1_done)
                c.inc(e.tensor_tensor(s_bis[0:1, 2:3], s_bis[0:1, 1:2], s_bis[0:1, 0:1],
                                      op=OP.subtract), "vS")
                c.wait(e, "vS")
                c.inc(e.tensor_scalar(s_bis[0:1, 2:3], s_bis[0:1, 2:3], 1.0 / 127.0, None,
                                      op0=OP.mult), "vS")
                c.wait(e, "vS")
                c.inc(e.tensor_scalar(s_trow[:], s_iot1[:], s_bis[0:1, 2:3], s_bis[0:1, 0:1],
                                      op0=OP.mult, op1=OP.add), "vS")
            trow_done = c.val("vS")

            bank = pb[4]
            @sec("tensor")
            def _(e, trow_done=trow_done, bank=bank):
                c.wait(e, "vS", trow_done)
                c.inc(e.transpose(bank[:, 0:1], s_trow[:], s_idf[0:1, 0:1]), "tS")
            tcol_mm = c.val("tS")

            @sec("vector")
            def _(e, tcol_mm=tcol_mm, bank=bank):
                c.wait(e, "tS", tcol_mm)
                c.inc(e.tensor_copy(s_smallc[:, 1:2], bank[:, 0:1]), "vS")
                c.wait(e, "vS")
                c.inc(e.tensor_scalar(s_bjunk[:], s_relb[:], s_smallc[:, 1:2], 0.0,
                                      op0=OP.is_ge, op1=OP.add,
                                      accum_out=s_smallc[:, 2:3]), "vS")
            cnt_done = c.val("vS")

            bank2 = pb[5]
            @sec("tensor")
            def _(e, cnt_done=cnt_done, bank2=bank2):
                c.wait(e, "vS", cnt_done)
                c.inc(e.transpose(bank2[0:1, 0:128], s_smallc[:, 2:3], s_idf[:]), "tS")
            crow_mm = c.val("tS")

            @sec("vector")
            def _(e, crow_mm=crow_mm, bank2=bank2):
                c.wait(e, "tS", crow_mm)
                c.inc(e.tensor_copy(s_crow[:], bank2[0:1, 0:128]), "vS")
                c.wait(e, "vS")
                c.inc(e.tensor_scalar(s_brw[:], s_crow[:], 1024.0, None, op0=OP.is_ge), "vS")
                c.inc(e.tensor_scalar(s_brw2[:], s_crow[:], 1024.0, None, op0=OP.is_lt), "vS")
                c.wait(e, "vS")
                c.inc(e.select(s_crow[:], s_brw[:], s_trow[:], s_neg[:], add_drain=True), "vS")
                c.wait(e, "vS")
                c.inc(e.tensor_reduce(s_bis[0:1, 0:1], s_crow[:], axis=AX.X, op=OP.max), "vS")
                c.wait(e, "vS")
                c.inc(e.select(s_crow[:], s_brw2[:], s_trow[:], s_pos[:], add_drain=True), "vS")
                c.wait(e, "vS")
                c.inc(e.tensor_reduce(s_bis[0:1, 1:2], s_crow[:], axis=AX.X, op=OP.min), "vS")

        thr_done = c.val("vS")
        bankt = pb[6]

        @sec("tensor")
        def _(e, thr_done=thr_done, bankt=bankt):
            c.wait(e, "vS", thr_done)
            c.inc(e.matmul(bankt[:, 0:1], s_onesf[:], s_bis[0:1, 0:1], start=True, stop=True), "tS")
        lob_mm = c.val("tS")

        @sec("vector")
        def _(e, lob_mm=lob_mm, bankt=bankt):
            c.wait(e, "tS", lob_mm)
            c.inc(e.tensor_copy(s_smallc[:, 3:4], bankt[:, 0:1]), "vS")
            c.wait(e, "vS")
            c.wait(e, "vS", relpc_done)
            c.inc(e.tensor_scalar(s_sel[:], s_relpc[:], s_smallc[:, 3:4], None, op0=OP.is_ge), "vS")
        sel_done = c.val("vS")

        if debug:
            @sec("sync")
            def _(e, sel_done=sel_done):
                c.wait(e, "vS", sel_done)
                c.inc(e.dma_start(dbg["sel"][:], s_sel[:]), "outS", 16)

        w4.close()
        soft_barrier()

        w5 = ExitStack()
        s_mask = sb(w5, "s_mask", [128, NKT, 512], BF16)
        s_av = sb(w5, "s_av", [128, KF, TPC], BF16)
        s_kh = sb(w5, "s_kh", [128, 2, NKT, 128], BF16)
        s_vh = sb(w5, "s_vh", [128, 2, NKT, 128], BF16)
        s_qh = sb(w5, "s_qh", [128, 2, 512], BF16)
        s_P = sb(w5, "s_P", [128, 4, 512], BF16)
        s_qpos = sb(w5, "s_qpos", [128, 512], F32)
        s_kpos = sb(w5, "s_kpos", [128, NKT], F32)
        s_kp512 = sb(w5, "s_kp512", [128, NKT], F32)
        s_sumr = sb(w5, "s_sumr", [1, 512], F32)
        s_rb = sb(w5, "s_rb", [128, 512], F32)
        s_c32 = sb(w5, "s_c32", [128, 1], F32)

        @sec("vector")
        def _(e):
            c.inc(e.memset(s_c32[:], ESHIFT), "vS")
        c32_done = c.val("vS")

        @sec("gpsimd")
        def _(e):
            c.inc(e.iota(s_kpos[:], [[128, NKT]], base=0, channel_multiplier=1,
                         allow_small_or_imprecise_dtypes=True), "ioS")
        kpos_done = c.val("ioS")

        @sec("vector")
        def _(e, kpos_done=kpos_done):
            c.wait(e, "ioS", kpos_done)
            c.inc(e.tensor_scalar(s_kp512[:], s_kpos[:], 512.5, None, op0=OP.add), "vS")
        kp512_done = c.val("vS")

        @sec("tensor")
        def _(e):
            c.wait(e, "ldS", lds_done)
            c.wait(e, "vS", c.val("vS"))
            c.inc(e.matmul(pb[6][:, 0:1], s_onesf[:], s_meta[0:1, 0:1], start=True, stop=True), "tS")
        q0b_mm = c.val("tS")

        @sec("vector")
        def _(e, q0b_mm=q0b_mm):
            c.wait(e, "tS", q0b_mm)
            c.inc(e.tensor_copy(s_smallc[:, 4:5], pb[6][:, 0:1]), "vS")
        q0b_done = c.val("vS")

        kh_free = [0, 0]
        vh_free = [0, 0]
        qh_free = [0, 0]
        kh_load = {}
        vh_load = {}
        qh_load = {}
        exp_done = [0, 0]
        pslot_free = [0, 0]
        pav_rb_free = [0, 0]

        for qb in range(2):
            @sec("gpsimd")
            def _(e, qb=qb):
                c.wait(e, "vS", c.val("vS"))
                c.inc(e.iota(s_qpos[:], [[1, 512]], base=qb * 512, channel_multiplier=0,
                             allow_small_or_imprecise_dtypes=True), "ioS")
            qio_done = c.val("ioS")

            @sec("vector")
            def _(e, qio_done=qio_done, q0b_done=q0b_done):
                c.wait(e, "ioS", qio_done)
                c.wait(e, "vS", q0b_done)
                c.inc(e.tensor_scalar(s_qpos[:], s_qpos[:], s_smallc[:, 4:5], None, op0=OP.add), "vS")
            qpos_done = c.val("vS")

            @sec("vector")
            def _(e, qpos_done=qpos_done, sel_done=sel_done, kp512_done=kp512_done):
                c.wait(e, "vS", max(qpos_done, sel_done, kp512_done))
                for g in range(NKT):
                    c.inc(e.tensor_scalar(s_mask[:, g, :], s_qpos[:], s_kp512[:, g:g + 1],
                                          s_sel[:, g:g + 1], op0=OP.is_lt, op1=OP.max), "vS")
                c.wait(e, "vS")
                for g in range(NKT):
                    c.inc(e.scalar_tensor_tensor(s_mask[:, g, :], s_qpos[:], s_kpos[:, g:g + 1],
                                                 s_mask[:, g, :], op0=OP.is_ge, op1=OP.mult), "vS")
            mask_done = c.val("vS")

            for h in range(NH):
                hb = h % 2
                @sec("sync")
                def _(e, qb=qb, h=h, hb=hb, khf=kh_free[hb], cc1_done=cc1_done):
                    c.wait(e, "ccS", cc1_done)
                    c.wait(e, "tS", khf)
                    for r in range(4):
                        c.inc(e.dma_start(
                            s_kh[:, hb, r * 8:(r + 1) * 8, :].rearrange("p g f -> p (g f)"),
                            d_cc1o[r * CC1R + h * 128:r * CC1R + h * 128 + 128, :]), f"stS{hb}", 16)
                    kh_load[(qb, h)] = (f"stS{hb}", c.val(f"stS{hb}"))

                @sec("scalar")
                def _(e, qb=qb, h=h, hb=hb, vhf=vh_free[hb], qhf=qh_free[hb], q_sent=q_sent):
                    c.wait(e, "qsS", q_sent)
                    c.wait(e, "tS", max(vhf, qhf))
                    for r in range(4):
                        c.inc(e.dma_start(
                            s_vh[:, hb, r * 8:(r + 1) * 8, :],
                            d_cc1o[r * CC1R + 4096 + h * 128:
                                   r * CC1R + 4096 + h * 128 + 128, :],
                            transpose=True), f"vqS{hb}", 16)
                    c.inc(e.dma_start(s_qh[:, hb, :],
                                      d_qsp[h * 128:(h + 1) * 128, qb * 512:(qb + 1) * 512]), f"vqS{hb}", 16)
                    vh_load[(qb, h)] = (f"vqS{hb}", c.val(f"vqS{hb}"))

                pav = pb[4 + (h % 2)]
                psums_ap = pb[6][0:1, :] if h % 2 == 0 else pb[6][32:33, :]
                for gq in range(NKT // 2):
                    sb0 = pb[(gq % 2) * 2]
                    sb1 = pb[(gq % 2) * 2 + 1]

                    @sec("tensor")
                    def _(e, qb=qb, h=h, hb=hb, gq=gq, sb0=sb0, sb1=sb1,
                          ef=exp_done[gq % 2], pvf=pav_rb_free[h % 2],
                          vq=vh_load.get((qb, h))):
                        if gq == 0:
                            c.wait(e, *kh_load[(qb, h)])
                            c.wait(e, *vq)
                            c.wait(e, "vS", pvf)
                        c.wait(e, "aS", ef)
                        for j, sbk in ((0, sb0), (1, sb1)):
                            g = gq * 2 + j
                            c.inc(e.matmul(sbk[:], s_kh[:, hb, g, :], s_qh[:, hb, :],
                                           start=True, stop=True), "tS")
                    smm = c.val("tS")

                    @sec("scalar")
                    def _(e, gq=gq, sb0=sb0, sb1=sb1, smm=smm,
                          pf=pslot_free[gq % 2], c32_done=c32_done):
                        c.wait(e, "tS", max(smm, pf))
                        if gq == 0:
                            c.wait(e, "vS", max(c32_done, mask_done))
                        for j, sbk in ((0, sb0), (1, sb1)):
                            c.inc(e.activation(s_P[:, (gq % 2) * 2 + j, :], sbk[:], AF.Exp,
                                               bias=s_c32[:], scale=SCALE), "aS")
                    exp_done[gq % 2] = c.val("aS")
                    edone = c.val("aS")

                    @sec("vector")
                    def _(e, gq=gq, edone=edone):
                        c.wait(e, "aS", edone)
                        for j in range(2):
                            g = gq * 2 + j
                            sl = (gq % 2) * 2 + j
                            c.inc(e.tensor_tensor(s_P[:, sl, :], s_P[:, sl, :],
                                                  s_mask[:, g, :], op=OP.mult), "vS")
                    multdone = c.val("vS")

                    @sec("tensor")
                    def _(e, qb=qb, h=h, hb=hb, gq=gq, pav=pav, psums_ap=psums_ap,
                          multdone=multdone, vq=vh_load.get((qb, h))):
                        c.wait(e, "vS", multdone)
                        if gq == 0:
                            c.wait(e, *vq)
                        for j in range(2):
                            g = gq * 2 + j
                            sl = (gq % 2) * 2 + j
                            c.inc(e.matmul(pav[:], s_vh[:, hb, g, :], s_P[:, sl, :],
                                           start=(g == 0), stop=(g == NKT - 1)), "tS")
                            c.inc(e.matmul(psums_ap, s_onesbf[:, 0:1], s_P[:, sl, :],
                                           start=(g == 0), stop=(g == NKT - 1)), "tS")
                        if gq == NKT // 2 - 1:
                            vh_free[hb] = c.val("tS")
                            kh_free[hb] = c.val("tS")
                            qh_free[hb] = c.val("tS")
                    pslot_free[gq % 2] = c.val("tS")
                av_mm_done = c.val("tS")

                @sec("vector")
                def _(e, h=h, psums_ap=psums_ap, av_mm_done=av_mm_done):
                    c.wait(e, "tS", av_mm_done)
                    c.inc(e.tensor_copy(s_sumr[:], psums_ap), "vS")
                    c.wait(e, "vS")
                    c.inc(e.reciprocal(s_sumr[:], s_sumr[:]), "vS")
                sum_done = c.val("vS")

                rbb = pb[4 + ((h + 1) % 2)]

                @sec("tensor")
                def _(e, sum_done=sum_done, rbb=rbb, pvf=pav_rb_free[(h + 1) % 2]):
                    c.wait(e, "vS", max(sum_done, pvf))
                    c.inc(e.matmul(rbb[:], s_onesf[:], s_sumr[:], start=True, stop=True), "tS")
                rb_mm = c.val("tS")

                @sec("vector")
                def _(e, qb=qb, h=h, pav=pav, rbb=rbb, rb_mm=rb_mm):
                    c.wait(e, "tS", rb_mm)
                    c.inc(e.tensor_copy(s_rb[:], rbb[:]), "vS")
                    c.wait(e, "vS")
                    c.inc(e.tensor_tensor(s_av[:, h, qb * 512:(qb + 1) * 512], pav[:],
                                          s_rb[:], op=OP.mult), "vS")
                pav_rb_free[(h + 1) % 2] = c.val("vS")
                pav_rb_free[h % 2] = c.val("vS")
        av_done = c.val("vS")

        def wo_evac(e, tt, coln, psum):
            c.inc(e.tensor_tensor(s_x[:, tt, coln * 512:(coln + 1) * 512], psum[:],
                                  s_x[:, tt, coln * 512:(coln + 1) * 512], op=OP.add), "vS")

        x2_done = tm_sweep("wo", d_Wot, s_av, av_done, wo_evac)

        if debug:
            @sec("sync")
            def _(e, x2_done=x2_done):
                c.wait(e, "vS", x2_done)
                c.inc(e.dma_start(dbg["x2"].rearrange("(tt p) f -> p tt f", p=128), s_x[:]), "outS", 16)

        w5.close()
        soft_barrier()

        w6 = ExitStack()
        fb2 = open_ffn_window(w6, sparse=True)
        xnT3_done = layernorm_to_xnT(fb2["xnT"], fb2["sq_junk"],
                                     xwait=("vS", x2_done), with_logits=True,
                                     s_stage=fb2["stage"], spill=fb2["xg"])
        ce2_done = top2()
        x3_done = moe_sparse(fb2, ce2_done, xnT3_done)
        w6.close()

        @sec("sync")
        def _(e):
            c.wait(e, "vS", c.val("vS"))
            c.wait(e, "aS", c.val("aS"))
            c.wait(e, "tS", c.val("tS"))
            c.inc(e.dma_start(d_out.rearrange("(tt p) f -> p tt f", p=128), s_x[:]), "outS", 16)
            c.wait(e, "outS", c.val("outS"))

    return nc, dbg



def make_in_maps(inputs):
    x_full = np.ascontiguousarray(inputs["L0_attn_hidden_states"], dtype=np.float32)
    ident = np.eye(128, dtype=np.float32)
    consts = dict(
        id_bf=ident.astype(ml_dtypes.bfloat16),
        id_f32=ident,
        ones_bf=np.ones((128, 128), dtype=np.float32).astype(ml_dtypes.bfloat16),
        ones_f32=np.ones((1, 128), dtype=np.float32),
        halves_bf=np.full((1, 128), 0.5, dtype=np.float32).astype(ml_dtypes.bfloat16),
        utri=np.triu(np.ones((128, 128), dtype=np.float32)).astype(ml_dtypes.bfloat16),
        Wq=inputs["L0_attn_Wq"], Wk=inputs["L0_attn_Wk"], Wv=inputs["L0_attn_Wv"],
        Wo=inputs["L0_attn_Wo"], Wq_ind=inputs["L0_attn_Wq_ind"], Wk_ind=inputs["L0_attn_Wk_ind"],
        head_w=np.asarray(inputs["L0_attn_head_weights"], dtype=np.float32).reshape(1, 8),
        temp=np.asarray(inputs["L0_attn_temperature_param"], dtype=np.float32).reshape(1, 1),
        router_w=inputs["L0_ffn_router_w"],
        gate_w=inputs["L0_ffn_gate_w"], up_w=inputs["L0_ffn_up_w"], down_w=inputs["L0_ffn_down_w"],
        s_gate_w=inputs["L0_ffn_s_gate_w"], s_up_w=inputs["L0_ffn_s_up_w"],
        s_down_w=inputs["L0_ffn_s_down_w"],
        s_gate_b=np.asarray(inputs["L0_ffn_s_gate_b"], dtype=np.float32).reshape(1, SI),
        s_up_b=np.asarray(inputs["L0_ffn_s_up_b"], dtype=np.float32).reshape(1, SI),
        s_down_b=np.asarray(inputs["L0_ffn_s_down_b"], dtype=np.float32).reshape(1, H),
    )
    out = {}
    for k, v in consts.items():
        v = np.ascontiguousarray(v)
        if v.dtype != ml_dtypes.bfloat16:
            v = v.astype(np.float32, copy=False)
        out[k] = v
    in_maps = []
    for core in range(NCORES):
        b, sl = core // 4, core % 4
        meta = np.zeros((1, 8), dtype=np.float32)
        meta[0, 0] = sl * TPC
        meta[0, 1] = float(b)
        m = dict(out)
        m["x_in"] = np.ascontiguousarray(x_full[b, sl * TPC:(sl + 1) * TPC, :])
        m["meta"] = meta
        in_maps.append(m)
    return in_maps


def kernel(**inputs):
    nc, _ = build_program(debug=False)
    in_maps = make_in_maps(inputs)
    res = run_bass_kernel_spmd(nc, in_maps, core_ids=list(range(NCORES)))
    out = np.empty((B, S, H), dtype=np.float32)
    for core in range(NCORES):
        b, sl = core // 4, core % 4
        out[b, sl * TPC:(sl + 1) * TPC, :] = res.results[core]["x_out"]
    return out
